# version 07
# speedup vs baseline: 1.2108x; 1.0267x over previous
import numpy as np
import ml_dtypes
from contextlib import ExitStack

import concourse.bass as bass
import concourse.mybir as mybir
from concourse.bass_utils import run_bass_kernel_spmd

F32 = mybir.dt.float32
BF16 = mybir.dt.bfloat16
AF = mybir.ActivationFunctionType
ALU = mybir.AluOpType
AX = mybir.AxisListType

N = 2048
D = 1024
NT = N // 128
DJ = D // 128
NP = N + 3
H = 8
HD = 128
FT = 64
S = 4096
SC = S // 128
HID = 4096
HC = HID // 128
NBL = 8
NBW = 256
HBL = 16
HBW = 256

BF = ml_dtypes.bfloat16


class Prog:
    ENGS = ("sync", "gpsimd", "tensor", "vector", "scalar")
    QDEPTH = {"y": 10, "g": 10, "o": 4}
    QENG = {"y": "sync", "g": "gpsimd", "o": "sync"}

    def __init__(self, nc, stack):
        self.nc = nc
        self.ops = {e: [] for e in self.ENGS}
        self.sems = {}
        self.counts = {}
        self.waited = {e: {} for e in self.ENGS}
        for name in ("t", "v", "s"):
            self.sems[name] = stack.enter_context(nc.semaphore("sem_" + name))
            self.counts[name] = 0
        self.qidx = {}
        for q, depth in self.QDEPTH.items():
            for i in range(depth):
                key = q + str(i)
                self.sems[key] = stack.enter_context(nc.semaphore("sem_" + key))
                self.counts[key] = 0
            self.qidx[q] = 0

    def wait(self, eng, sem, val):
        if val is None or val <= 0:
            return
        w = self.waited[eng]
        if w.get(sem, -1) >= val:
            return
        w[sem] = val
        h = self.sems[sem]
        self.ops[eng].append(lambda e: e.wait_ge(h, val))

    def waitm(self, eng, mark):
        if mark is None:
            return
        self.wait(eng, mark[0], mark[1])

    def run(self, eng, fn, sem=None, k=1, chain=True):
        if chain and eng in ("vector", "scalar"):
            self.ops[eng].append(lambda e: e.drain())
        if sem is None:
            self.ops[eng].append(fn)
            return None
        self.counts[sem] += k
        mark = self.counts[sem]
        h = self.sems[sem]
        self.ops[eng].append(lambda e: fn(e).then_inc(h, k))
        return mark

    def runf(self, eng, fn, sem=None, k=1):
        return self.run(eng, fn, sem, k, chain=False)

    def dma(self, q, dst, src):
        eng = self.QENG[q]
        i = self.qidx[q]
        self.qidx[q] += 1
        key = q + str(i % self.QDEPTH[q])
        prior = self.counts[key]
        if prior > 0:
            self.wait(eng, key, prior)
        self.counts[key] += 16
        val = self.counts[key]
        h = self.sems[key]
        self.ops[eng].append(lambda e: e.dma_start(dst, src).then_inc(h, 16))
        return (key, val)

    def qdrain(self, eng, q):
        for i in range(self.QDEPTH[q]):
            key = q + str(i)
            self.wait(eng, key, self.counts[key])

    def phase_fence(self):
        for eng in self.ENGS:
            for q in self.QDEPTH:
                self.qdrain(eng, q)
        for eng in ("sync", "gpsimd"):
            for s in ("t", "v", "s"):
                self.wait(eng, s, self.counts[s])

    def emit(self, name):
        with self.nc.Block(name) as blk:
            for eng in self.ENGS:
                ops = self.ops[eng]
                if not ops:
                    continue
                def body(e, ops=tuple(ops)):
                    for f in ops:
                        f(e)
                getattr(blk, eng)(body)
        self.ops = {e: [] for e in self.ENGS}


def build(stop_after=None, debug=False):
    nc = bass.Bass("TRN2", target_bir_lowering=False)

    dr = {}
    def din(name, shape, dtype=F32):
        dr[name] = nc.dram_tensor(name, list(shape), dtype, kind="ExternalInput").ap()
    def dout(name, shape, dtype=F32):
        dr[name] = nc.dram_tensor(name, list(shape), dtype, kind="ExternalOutput").ap()
    def dscr(name, shape, dtype=F32):
        kind = "ExternalOutput" if debug else "Internal"
        dr[name] = nc.dram_tensor(name, list(shape), dtype, kind=kind).ap()

    din("x", (N, D))
    din("w_qkv", (D, 3 * D))
    din("proj", (HD, FT))
    din("w_out", (D, D))
    din("b_out", (1, D))
    din("gx_col", (128, DJ))
    din("gs_col", (128, DJ))
    din("slot_embeds", (S, D))
    din("w1", (D, HID))
    din("w2", (HID, D))
    din("c_fm", (128, DJ))
    din("wmod", (D, 6 * D))
    din("ident_bf", (128, 128), BF16)
    din("ident_f32", (128, 128), F32)
    din("ones_bf", (128, 128), BF16)
    din("ones_f32", (128, 128), F32)
    din("mask_a", (128, 512))
    din("mask_b", (6, 512))
    dout("out", (N, D))
    dscr("L_sp", (NBL, S, NBW))
    dscr("h_sp", (HBL, HID, HBW), BF16)
    dscr("x2_sp", (N, D))

    dbg = {}
    if debug:
        def ddbg(name, shape, dtype=F32):
            dbg[name] = nc.dram_tensor("dbg_" + name, list(shape), dtype,
                                       kind="ExternalOutput").ap()
        ddbg("mod_row", (1, 6 * D))
        ddbg("h1p", (128, DJ * NP), BF16)
        ddbg("vtm", (128, (NT + 1) * D), BF16)
        ddbg("attn_cat", (128, H * N), BF16)
        ddbg("xn_tm", (128, NT * D), BF16)
        ddbg("xn_fm", (128, DJ * N), BF16)
        ddbg("rscale", (128, SC))
        ddbg("slots_fm", (128, DJ * S), BF16)
        ddbg("proc_sm", (128, SC * D), BF16)

    g = ExitStack()
    with g:
        P = Prog(nc, g)

        def sb(name, shape, dtype, stack=None, side=None):
            return (stack or g).enter_context(
                nc.sbuf_tensor(name, list(shape), dtype, side=side))

        def ps(name, shape, dtype, stack=None):
            return (stack or g).enter_context(
                nc.psum_tensor(name, list(shape), dtype))

        ident_bf = sb("ident_bf_s", (128, 128), BF16)
        ident_f32 = sb("ident_f32_s", (128, 128), F32)
        ones_bf = sb("ones_bf_s", (128, 128), BF16)
        ones_f32 = sb("ones_f32_s", (128, 128), F32)
        gx_col = sb("gx_col_s", (128, DJ), F32)
        gs_col = sb("gs_col_s", (128, DJ), F32)
        sa_col = sb("sa_col", (128, DJ), F32)
        oneca_col = sb("oneca_col", (128, DJ), F32)
        ga_col = sb("ga_col", (128, DJ), F32)
        sb_col = sb("sb_col", (128, DJ), F32)
        onecb_col = sb("onecb_col", (128, DJ), F32)
        gb_col = sb("gb_col", (128, DJ), F32)
        gprod_col = sb("gprod_col", (128, DJ), F32)
        rscale_all = sb("rscale_all", (128, SC), F32)
        ssq_se = sb("ssq_se", (128, SC), F32)
        colA = sb("colA", (128, 16), F32)
        colB = sb("colB", (128, 16), F32)
        colC = sb("colC", (128, 16), F32)
        colD = sb("colD", (128, 16), F32)
        eps_col = sb("eps_col", (128, 1), F32)

        h1_cm = nc.sbuf_tensor("h1p", [128, DJ * NP], BF16, side="right")
        h1p = h1_cm.__enter__()

        with ExitStack() as es:
            mod_row = sb("mod_row", (1, 6 * D), F32, es)
            rows_t = sb("rows_t", (1, D), F32, es)
            rows_u = sb("rows_u", (1, D), F32, es)
            c_fm = sb("c_fm_s", (128, DJ), F32, es)
            silu_bf = sb("silu_bf", (128, DJ), BF16, es)
            wsl = [sb("wsl%d" % i, (128, 8 * 512), BF16, es) for i in range(2)]
            x_st = [sb("xst%d" % i, (128, D), F32, es) for i in range(2)]
            sq_scr = sb("sq_scr", (128, D), F32, es)
            ln_bf = [sb("lnbf%d" % i, (128, D), BF16, es) for i in range(2)]
            p_mod = [ps("p_mod%d" % i, (1, 512), F32, es) for i in range(2)]
            p_tpb = [ps("p_tpb%d" % i, (128, 1024), BF16, es) for i in range(2)]
            p_rep = [ps("p_rep%d" % i, (128, 512), F32, es) for i in range(2)]

            for dst, src in [(ident_bf, "ident_bf"), (ident_f32, "ident_f32"),
                             (ones_bf, "ones_bf"), (ones_f32, "ones_f32"),
                             (gx_col, "gx_col"), (gs_col, "gs_col"),
                             (c_fm, "c_fm")]:
                P.dma("y", dst[:], dr[src][:])
            m_eps = P.run("vector", lambda e: e.memset(eps_col[:], 1e-5), "v")
            for j in range(DJ):
                P.run("vector", lambda e, j=j: e.memset(
                    h1p[:, j * NP:j * NP + 3], 0.0))
            P.qdrain("vector", "y")
            m_h1pad = P.run("vector", lambda e: e.tensor_tensor(
                gprod_col[:], gx_col[:], gs_col[:], ALU.mult), "v")

            P.qdrain("scalar", "y")
            m_silu = P.run("scalar", lambda e: e.activation(
                silu_bf[:], c_fm[:], AF.Silu), "s")

            mod_mm = []
            mod_ev = []
            for sl in range(12):
                pm = p_mod[sl % 2]
                wmk = []
                for j in range(DJ):
                    if sl >= 2:
                        P.wait("gpsimd", "t", mod_mm[sl - 2])
                    wmk.append(P.dma("g", wsl[sl % 2][:, j * 512:(j + 1) * 512],
                                     dr["wmod"][j * 128:(j + 1) * 128,
                                                sl * 512:(sl + 1) * 512]))
                if sl >= 2:
                    P.wait("tensor", "v", mod_ev[sl - 2])
                mk = None
                for j in range(DJ):
                    P.waitm("tensor", wmk[j])
                    if sl == 0 and j == 0:
                        P.wait("tensor", "s", m_silu)
                    mk = P.run("tensor", lambda e, pm=pm, j=j, sl=sl: e.matmul(
                        pm[:, 0:512], silu_bf[:, j:j + 1],
                        wsl[sl % 2][:, j * 512:(j + 1) * 512],
                        start=(j == 0), stop=(j == DJ - 1)),
                        "t" if j == DJ - 1 else None)
                mod_mm.append(mk)
                P.wait("vector", "t", mk)
                mod_ev.append(P.runf("vector", lambda e, pm=pm, sl=sl: e.tensor_copy(
                    mod_row[:, sl * 512:(sl + 1) * 512], pm[:, 0:512]), "v"))

            P.wait("vector", "v", mod_ev[-1])
            P.run("vector", lambda e: e.tensor_scalar(
                rows_t[:], mod_row[:, 1024:2048], 1.0, None, ALU.add))
            m_oneca = P.run("vector", lambda e: e.tensor_copy(
                mod_row[:, 1024:2048], rows_t[:]), "v")
            P.run("vector", lambda e: e.tensor_scalar(
                rows_t[:], mod_row[:, 4096:5120], 1.0, None, ALU.add))
            m_onecb = P.run("vector", lambda e: e.tensor_copy(
                mod_row[:, 4096:5120], rows_t[:]), "v")
            m_gv = m_onecb
            for off in (2048, 5120):
                P.wait("scalar", "v", m_gv)
                m_sgn = P.run("scalar", lambda e, o=off: e.activation(
                    rows_t[:], mod_row[:, o:o + D], AF.Sign), "s")
                m_abs = P.run("scalar", lambda e, o=off: e.activation(
                    rows_u[:], mod_row[:, o:o + D], AF.Abs), "s")
                P.wait("vector", "s", m_abs)
                P.run("vector", lambda e, o=off: e.tensor_tensor(
                    mod_row[:, o:o + D], rows_t[:], rows_t[:], ALU.mult))
                P.run("vector", lambda e, o=off: e.scalar_tensor_tensor(
                    rows_t[:], rows_t[:], 1.0, mod_row[:, o:o + D],
                    ALU.add, ALU.subtract))
                P.run("vector", lambda e: e.tensor_scalar(
                    rows_u[:], rows_u[:], 0.25, None, ALU.max))
                P.run("vector", lambda e, o=off: e.tensor_tensor(
                    rows_t[:], rows_t[:], rows_u[:], ALU.mult))
                m_gv = P.run("vector", lambda e, o=off: e.tensor_copy(
                    mod_row[:, o:o + D], rows_t[:]), "v")
            m_rows = m_gv
            P.wait("tensor", "v", m_rows)
            mk = None
            for k6 in range(6):
                for j in range(DJ):
                    mk = P.run("tensor", lambda e, k6=k6, j=j: e.matmul(
                        p_rep[k6 % 2][:, 8 * (k6 // 2) + j:8 * (k6 // 2) + j + 1],
                        mod_row[0:1, k6 * D + j * 128:k6 * D + (j + 1) * 128],
                        ones_f32[0:1, 0:1], start=True, stop=True),
                        "t" if j == DJ - 1 else None)
                if k6 % 2 == 1 or k6 == 5:
                    P.wait("vector", "t", mk)
            P.wait("vector", "t", mk)
            dsts = [sa_col, oneca_col, ga_col, sb_col, onecb_col, gb_col]
            m_cols = None
            for k6 in range(6):
                m_cols = P.run("vector", lambda e, k6=k6: e.tensor_copy(
                    dsts[k6][:],
                    p_rep[k6 % 2][:, 8 * (k6 // 2):8 * (k6 // 2) + 8]), "v")

            ssq_m, var_m, sqrt_m, ev_m = [], [], [], []
            lnh_m = [None] * NT
            x_mk = [None] * NT
            for i in range(NT):
                c = i % 16
                xs = x_st[i % 2]
                if i >= 2:
                    P.wait("sync", "v", lnh_m[i - 2])
                    P.wait("sync", "s", ssq_m[i - 2])
                x_mk[i] = P.dma("y", xs[:], dr["x"][i * 128:(i + 1) * 128, :])
                P.waitm("scalar", x_mk[i])
                if i >= 2:
                    P.wait("scalar", "v", var_m[i - 2])
                ssq_m.append(P.runf("scalar", lambda e, xs=xs, c=c: e.activation(
                    sq_scr[:], xs[:], AF.Square, accum_out=colA[:, c:c + 1]), "s"))
                P.waitm("vector", x_mk[i])
                P.runf("vector", lambda e, xs=xs, c=c: e.reduce_sum(
                    colC[:, c:c + 1], xs[:], AX.X))
                P.run("vector", lambda e, c=c: e.tensor_scalar(
                    colC[:, c:c + 1], colC[:, c:c + 1], 1.0 / D, None, ALU.mult))
                P.run("vector", lambda e, c=c: e.tensor_tensor(
                    colB[:, c:c + 1], colC[:, c:c + 1], colC[:, c:c + 1], ALU.mult))
                P.wait("vector", "s", ssq_m[i])
                var_m.append(P.run("vector", lambda e, c=c: e.scalar_tensor_tensor(
                    colB[:, c:c + 1], colA[:, c:c + 1], 1.0 / D, colB[:, c:c + 1],
                    ALU.mult, ALU.subtract), "v"))
                P.wait("scalar", "v", var_m[i])
                if i == 0:
                    P.wait("scalar", "v", m_eps)
                sqrt_m.append(P.runf("scalar", lambda e, c=c: e.activation(
                    colB[:, c:c + 1], colB[:, c:c + 1], AF.Sqrt,
                    bias=eps_col[:, 0:1]), "s"))
                P.wait("vector", "s", sqrt_m[i])
                P.run("vector", lambda e, c=c: e.reciprocal(
                    colB[:, c:c + 1], colB[:, c:c + 1]))
                lnh_m[i] = P.run("vector", lambda e, xs=xs, c=c, i=i: e.tensor_scalar(
                    ln_bf[i % 2][:], xs[:], colC[:, c:c + 1], colB[:, c:c + 1],
                    ALU.subtract, ALU.mult), "v")
                P.wait("tensor", "v", lnh_m[i])
                if ev_m:
                    P.wait("tensor", "v", ev_m[-1])
                pt = p_tpb[i % 2]
                mk = None
                for j in range(DJ):
                    mk = P.run("tensor", lambda e, pt=pt, j=j, i=i: e.transpose(
                        pt[:, j * 128:(j + 1) * 128],
                        ln_bf[i % 2][:, j * 128:(j + 1) * 128], ident_bf[:]),
                        "t" if j == DJ - 1 else None)
                P.wait("vector", "v", m_cols)
                P.wait("vector", "v", m_h1pad)
                P.wait("vector", "t", mk)
                lastev = None
                for j in range(DJ):
                    lastev = P.runf("vector", lambda e, pt=pt, j=j, i=i: e.tensor_scalar(
                        h1p[:, j * NP + 3 + i * 128: j * NP + 3 + i * 128 + 128],
                        pt[:, j * 128:(j + 1) * 128],
                        oneca_col[:, j:j + 1], sa_col[:, j:j + 1],
                        ALU.mult, ALU.add), "v" if j == DJ - 1 else None)
                ev_m.append(lastev)

            P.emit("p0")
            if debug:
                P.dma("o", dbg["mod_row"][:], mod_row[:])
                P.dma("o", dbg["h1p"][:], h1p[:])
                P.qdrain("sync", "o")
                P.emit("p0dbg")
        if stop_after == "p0":
            return nc, dr

        with ExitStack() as es1:
            attn_cat = sb("attn_cat", (128, H * N), BF16, es1)
            pp = [ps("pp%d" % i, (128, 512), F32, es1) for i in range(6)]
            pb = [ps("pb%d" % i, (128, 1024), BF16, es1) for i in range(2)]

            with ExitStack() as esh:
                mask_a = sb("mask_a_s", (128, 512), F32, esh)
                mask_b = sb("mask_b_s", (6, 512), F32, esh)
                wq_bf = [sb("wq%d" % i, (128, D), BF16, esh) for i in range(2)]
                wk_bf = [sb("wk%d" % i, (128, D), BF16, esh) for i in range(2)]
                proj_bf = sb("proj_bf", (128, FT), BF16, esh)
                v_tm = sb("v_tm", (128, (NT + 1) * D), BF16, esh)
                q_bf = sb("q_bf", (128, N), BF16, esh)
                k_bf = sb("k_bf", (128, N), BF16, esh)
                q_phi = [sb("q_phi%d" % i, (64, N), BF16, esh) for i in range(2)]
                k_pad = [sb("k_pad%d" % i, (64, N + 6), BF16, esh)
                         for i in range(2)]
                t_min_q = sb("t_min_q", (64, 512), F32, esh)
                t_min_k = sb("t_min_k", (64, 512), F32, esh)
                s_a = [sb("s_a%d" % i, (128, 512), BF16, esh) for i in range(2)]
                s_b = [sb("s_b%d" % i, (6, 512), BF16, esh) for i in range(2)]
                attn_tm = [sb("attn_tm%d" % i, (128, 512), BF16, esh)
                           for i in range(2)]
                se_st = sb("se_st", (128, D), F32, esh)
                sq2 = sb("sq2", (128, 2 * D), F32, esh)

                mka_mk = P.dma("y", mask_a[:], dr["mask_a"][:])
                mkb_mk = P.dma("y", mask_b[:], dr["mask_b"][:])
                m_proj = P.dma("g", proj_bf[:], dr["proj"][:])

                def qkv_block(dst, col0):
                    src = dr["w_qkv"][:, col0:col0 + 128].rearrange(
                        "(j p) c -> p j c", p=128)
                    d = dst[:].rearrange("p (j c) -> p j c", c=128)
                    return P.dma("g", d, src)

                wq_mk = [qkv_block(wq_bf[h], h * 128) for h in range(2)]
                wk_mk = [qkv_block(wk_bf[h], D + h * 128) for h in range(2)]

                with ExitStack() as esv:
                    wv_bf = sb("wv_bf", (128, DJ * D), BF16, esv)
                    wv_mk = [P.dma("g", wv_bf[:, j * D:(j + 1) * D],
                                   dr["w_qkv"][j * 128:(j + 1) * 128, 2 * D:3 * D])
                             for j in range(DJ)]
                    m_vz = P.run("vector", lambda e: e.memset(
                        v_tm[0:6, NT * D:NT * D + D], 0.0), "v")
                    vev = []
                    for i in range(NT + 1):
                        rows = 128 if i < NT else 3
                        for half in range(2):
                            pv = pp[2 * (i % 2) + half]
                            if len(vev) >= 4:
                                P.wait("tensor", "v", vev[-4])
                            mk = None
                            for j in range(DJ):
                                P.waitm("tensor", wv_mk[j])
                                if i < NT:
                                    st_ap = h1p[:, j * NP + i * 128:
                                                j * NP + i * 128 + 128]
                                else:
                                    st_ap = h1p[:, j * NP + N:j * NP + N + 3]
                                mk = P.run("tensor",
                                           lambda e, pv=pv, st_ap=st_ap, j=j,
                                           half=half, r=rows: e.matmul(
                                    pv[0:r, 0:512], st_ap,
                                    wv_bf[:, j * D + half * 512:j * D + half * 512 + 512],
                                    start=(j == 0), stop=(j == DJ - 1)),
                                    "t" if j == DJ - 1 else None)
                            P.wait("vector", "t", mk)
                            if i == NT:
                                P.wait("vector", "v", m_vz)
                            vev.append(P.runf("vector",
                                              lambda e, pv=pv, i=i, half=half, r=rows:
                                              e.tensor_copy(
                                v_tm[0:r, i * D + half * 512:i * D + half * 512 + 512],
                                pv[0:r, 0:512]), "v"))
                    m_vtm_done = vev[-1]
                    if debug:
                        P.wait("sync", "v", m_vtm_done)
                        P.dma("o", dbg["vtm"][:, 0:NT * D], v_tm[:, 0:NT * D])
                        P.dma("o", dbg["vtm"][0:6, NT * D:NT * D + D],
                              v_tm[0:6, NT * D:NT * D + D])

                zpad_done = [None, None]
                hst = [dict() for _ in range(H)]
                qk_use = [None] * H

                def qk_pass(h, p):
                    dstq = q_bf if p < 2 else k_bf
                    wsrc = wq_bf if p < 2 else wk_bf
                    mks = wq_mk if p < 2 else wk_mk
                    qh = p % 2
                    st = hst[h]
                    P.waitm("tensor", mks[h])
                    P.wait("tensor", "v", st.get("qk_ev"))
                    if p == 0:
                        if h >= 1:
                            P.wait("tensor", "v", hst[h - 1]["phi"])
                        else:
                            P.wait("tensor", "v", m_vtm_done)
                    mk = None
                    for j in range(DJ):
                        for b2 in range(2):
                            mk = P.run("tensor",
                                       lambda e, h=h, j=j, b2=b2, qh=qh, wsrc=wsrc:
                                       e.matmul(
                                pp[b2][:, 0:512],
                                wsrc[h % 2][:, j * 128:(j + 1) * 128],
                                h1p[:, j * NP + 3 + qh * 1024 + b2 * 512:
                                    j * NP + 3 + qh * 1024 + b2 * 512 + 512],
                                start=(j == 0), stop=(j == DJ - 1)),
                                "t" if (j == DJ - 1 and b2 == 1) else None)
                    if p == 3:
                        qk_use[h] = mk
                    P.wait("vector", "t", mk)
                    ev = None
                    for b2 in range(2):
                        ev = P.runf("vector", lambda e, b2=b2, qh=qh, dstq=dstq:
                                    e.tensor_copy(
                            dstq[:, qh * 1024 + b2 * 512:qh * 1024 + b2 * 512 + 512],
                            pp[b2][:, 0:512]), "v")
                    st["qk_ev"] = ev

                def proj_pass(h, b4):
                    st = hst[h]
                    hp = h % 2
                    P.waitm("tensor", m_proj)
                    P.wait("tensor", "v", st.get("phi_p", st["qk_ev"]))
                    P.run("tensor", lambda e, b4=b4: e.matmul(
                        pp[0][0:64, 0:512], proj_bf[:, 0:FT],
                        q_bf[:, b4 * 512:(b4 + 1) * 512], start=True, stop=True))
                    mkp = P.run("tensor", lambda e, b4=b4: e.matmul(
                        pp[1][0:64, 0:512], proj_bf[:, 0:FT],
                        k_bf[:, b4 * 512:(b4 + 1) * 512], start=True, stop=True), "t")
                    P.wait("vector", "t", mkp)
                    if h >= 2:
                        P.wait("vector", "t", hst[h - 2].get("wlast"))
                    P.runf("vector", lambda e, b4=b4: e.tensor_scalar(
                        t_min_q[:, 0:512], pp[0][0:64, 0:512], 0.0, None, ALU.min))
                    m_tmin = P.runf("vector", lambda e, b4=b4: e.tensor_scalar(
                        t_min_k[:, 0:512], pp[1][0:64, 0:512], 0.0, None,
                        ALU.min), "v")
                    P.wait("scalar", "v", m_tmin)
                    P.runf("scalar", lambda e: e.activation(
                        t_min_q[:, 0:512], t_min_q[:, 0:512], AF.Exp))
                    m_exp = P.runf("scalar", lambda e: e.activation(
                        t_min_k[:, 0:512], t_min_k[:, 0:512], AF.Exp), "s")
                    P.wait("vector", "s", m_exp)
                    P.runf("vector", lambda e, b4=b4, hp=hp: e.scalar_tensor_tensor(
                        q_phi[hp][:, b4 * 512:(b4 + 1) * 512], pp[0][0:64, 0:512],
                        0.0, t_min_q[:, 0:512], ALU.max, ALU.add))
                    phi = P.runf("vector", lambda e, b4=b4, hp=hp:
                                 e.scalar_tensor_tensor(
                        k_pad[hp][:, 3 + b4 * 512:3 + (b4 + 1) * 512],
                        pp[1][0:64, 0:512],
                        0.0, t_min_k[:, 0:512], ALU.max, ALU.add), "v")
                    st["phi_p"] = phi
                    if b4 == 3:
                        if zpad_done[hp] is None:
                            P.runf("vector", lambda e, hp=hp: e.memset(
                                k_pad[hp][:, 0:3], 0.0))
                            phi = P.runf("vector", lambda e, hp=hp: e.memset(
                                k_pad[hp][:, N + 3:N + 6], 0.0), "v")
                            zpad_done[hp] = phi
                        st["phi"] = phi

                def wA(h, grp):
                    st = hst[h]
                    hp = h % 2
                    P.wait("tensor", "v", st["phi"])
                    P.wait("tensor", "v", zpad_done[hp])
                    prev = st if grp >= 1 else (hst[h - 1] if h >= 1 else None)
                    if prev is not None and "sev" in prev:
                        P.wait("tensor", "v", prev["sev"])
                    smk = None
                    for t4 in range(4):
                        t = grp * 4 + t4
                        c0 = t4 * 128
                        P.run("tensor", lambda e, t=t, c0=c0, hp=hp: e.matmul(
                            pp[2][:, c0:c0 + 128],
                            k_pad[hp][:, t * 128:t * 128 + 128],
                            q_phi[hp][:, t * 128:(t + 1) * 128],
                            start=True, stop=True))
                        smk = P.run("tensor", lambda e, t=t, c0=c0, hp=hp: e.matmul(
                            pp[3][0:6, c0:c0 + 128],
                            k_pad[hp][:, t * 128 + 128:t * 128 + 134],
                            q_phi[hp][:, t * 128:(t + 1) * 128],
                            start=True, stop=True), "t")
                    P.wait("vector", "t", smk)
                    P.waitm("vector", mka_mk)
                    P.waitm("vector", mkb_mk)
                    P.runf("vector", lambda e, g=grp: e.tensor_tensor(
                        s_a[g % 2][:], pp[2][:, 0:512], mask_a[:], ALU.mult))
                    st["sev"] = P.runf("vector", lambda e, g=grp: e.tensor_tensor(
                        s_b[g % 2][:], pp[3][0:6, 0:512], mask_b[:],
                        ALU.mult), "v")

                def wB(h, grp):
                    st = hst[h]
                    prev = st if grp >= 1 else (hst[h - 1] if h >= 1 else None)
                    P.wait("tensor", "v", st["sev"])
                    if prev is not None and "zr" in prev:
                        P.wait("tensor", "v", prev["zr"])
                        P.wait("tensor", "s", prev["cpm"])
                    amk = None
                    for t4 in range(4):
                        t = grp * 4 + t4
                        c0 = t4 * 128
                        sa_ = s_a[grp % 2]
                        sb_ = s_b[grp % 2]
                        P.run("tensor", lambda e, t=t, c0=c0, sa_=sa_, h=h:
                              e.matmul(
                            pp[4][:, c0:c0 + 128], sa_[:, c0:c0 + 128],
                            v_tm[:, t * D + h * 128:t * D + (h + 1) * 128],
                            start=True, stop=False))
                        P.run("tensor", lambda e, t=t, c0=c0, sb_=sb_, h=h:
                              e.matmul(
                            pp[4][:, c0:c0 + 128], sb_[:, c0:c0 + 128],
                            v_tm[0:6, (t + 1) * D + h * 128:(t + 1) * D + (h + 1) * 128],
                            start=False, stop=True))
                        P.run("tensor", lambda e, t4=t4, c0=c0, sa_=sa_: e.matmul(
                            pp[5][:, t4:t4 + 1], sa_[:, c0:c0 + 128],
                            ones_bf[:, 0:1], start=True, stop=False))
                        amk = P.run("tensor", lambda e, t4=t4, c0=c0, sb_=sb_:
                                    e.matmul(
                            pp[5][:, t4:t4 + 1], sb_[:, c0:c0 + 128],
                            ones_bf[0:6, 0:1], start=False, stop=True), "t")
                    P.wait("vector", "t", amk)
                    P.runf("vector", lambda e: e.tensor_scalar(
                        colA[:, 8:12], pp[5][:, 0:4], 1e-8, None, ALU.add))
                    st["zr"] = P.run("vector", lambda e: e.reciprocal(
                        colA[:, 8:12], colA[:, 8:12]), "v")
                    P.wait("scalar", "v", st["zr"])
                    cpms = []
                    for t4 in range(4):
                        cpms.append(P.runf("scalar",
                                           lambda e, t4=t4, g=grp: e.activation(
                            attn_tm[g % 2][:, t4 * 128:(t4 + 1) * 128],
                            pp[4][:, t4 * 128:(t4 + 1) * 128], AF.Copy,
                            scale=colA[:, 8 + t4:9 + t4]), "s"))
                    st["cpm"] = cpms[-1]
                    st["cpms"] = cpms

                def wC(h, grp):
                    st = hst[h]
                    if "cat" in st:
                        P.wait("tensor", "v", st["cat"])
                    elif h >= 1:
                        P.wait("tensor", "v", hst[h - 1]["cat"])
                    trm = None
                    for t4 in range(4):
                        P.wait("tensor", "s", st["cpms"][t4])
                        trm = P.run("tensor", lambda e, t4=t4, g=grp: e.transpose(
                            pb[0][:, t4 * 128:(t4 + 1) * 128],
                            attn_tm[g % 2][:, t4 * 128:(t4 + 1) * 128],
                            ident_bf[:]), "t")
                    st["wlast"] = trm
                    P.wait("vector", "t", trm)
                    st["cat"] = P.runf("vector", lambda e, g=grp, h=h: e.tensor_copy(
                        attn_cat[:, h * N + g * 512:h * N + (g + 1) * 512],
                        pb[0][:, 0:512]), "v")

                for h in range(H):
                    hm = h - 1
                    qk_pass(h, 0)
                    if hm >= 1:
                        wB(hm - 1, 3)
                    qk_pass(h, 1)
                    if hm >= 1:
                        wC(hm - 1, 2)
                    qk_pass(h, 2)
                    if hm >= 1:
                        wC(hm - 1, 3)
                    qk_pass(h, 3)
                    if h + 2 < H:
                        P.waitm("gpsimd", ("t", qk_use[h]))
                        wq_mk.append(qkv_block(wq_bf[(h + 2) % 2], (h + 2) * 128))
                        wk_mk.append(qkv_block(wk_bf[(h + 2) % 2],
                                               D + (h + 2) * 128))
                    for p in range(4):
                        proj_pass(h, p)
                        if hm >= 0:
                            wA(hm, p)
                            if p >= 1:
                                wB(hm, p - 1)
                            if p >= 2:
                                wC(hm, p - 2)
                    for sc in range(4 * h, 4 * h + 4):
                        mse = P.dma("g", se_st[:],
                                    dr["slot_embeds"][sc * 128:(sc + 1) * 128, :])
                        P.waitm("scalar", mse)
                        m_ssq = P.runf("scalar", lambda e, sc=sc: e.activation(
                            sq2[:, (sc % 2) * 1024:(sc % 2) * 1024 + 1024],
                            se_st[:], AF.Square,
                            accum_out=ssq_se[:, sc:sc + 1]), "s")
                        P.wait("gpsimd", "s", m_ssq)
                wB(H - 2, 3)
                wC(H - 2, 2)
                wC(H - 2, 3)
                for grp in range(4):
                    wA(H - 1, grp)
                    wB(H - 1, grp)
                    wC(H - 1, grp)
                m_cpm_last = hst[H - 1]["cpm"]
                m_attn_done = P.counts["v"]
                P.emit("p1a")
                P.phase_fence()
            h1_cm.__exit__(None, None, None)

            xn_tm_cm = nc.sbuf_tensor("xn_tm", [128, NT * D], BF16, side="right")
            xn_fm_cm = nc.sbuf_tensor("xn_fm", [128, DJ * N], BF16, side="right")
            xn_tm = xn_tm_cm.__enter__()
            xn_fm = xn_fm_cm.__enter__()

            with ExitStack() as esl:
                w_out_bf = sb("w_out_bf", (128, DJ * D), BF16, esl)
                ga_rep = sb("ga_rep", (128, D), F32, esl)
                row_b = sb("row_b", (1, D), F32, esl)
                b_out_row = sb("b_out_row", (1, D), F32, esl)
                gab_bf = sb("gab_bf", (1, D), BF16, esl)
                onecb_rep = sb("onecb_rep", (128, D), F32, esl)
                sbr_rep = sb("sbr_rep", (128, D), F32, esl)
                x_rs = [sb("xrs%d" % i, (128, D), F32, esl) for i in range(2)]
                x2_b = [sb("x2b%d" % i, (128, D), F32, esl) for i in range(2)]
                h2_b = sb("h2_b", (128, D), F32, esl)
                sq2b = sb("sq2b", (128, D), F32, esl)

                m_bout = P.dma("y", b_out_row[:], dr["b_out"][:])
                wo_mk = [P.dma("g", w_out_bf[:, j * D:(j + 1) * D],
                               dr["w_out"][j * 128:(j + 1) * 128, :])
                         for j in range(DJ)]

                def col_to_row(col, pidx):
                    mk = None
                    for j in range(DJ):
                        mk = P.run("tensor", lambda e, j=j, col=col, p=pidx: e.matmul(
                            pp[p + j // 4][0:1, (j % 4) * 128:(j % 4) * 128 + 128],
                            col[:, j:j + 1], ident_f32[:],
                            start=True, stop=True), "t" if j == DJ - 1 else None)
                    return mk

                def row_evac(pidx):
                    P.run("vector", lambda e, p=pidx: e.tensor_copy(
                        row_b[:, 0:512], pp[p][0:1, 0:512]))
                    return P.run("vector", lambda e, p=pidx: e.tensor_copy(
                        row_b[:, 512:1024], pp[p + 1][0:1, 0:512]), "v")

                P.wait("tensor", "v", m_attn_done)
                P.wait("tensor", "s", m_cpm_last)
                mk = col_to_row(ga_col, 0)
                P.wait("vector", "t", mk)
                m_rowb = row_evac(0)
                P.waitm("vector", m_bout)
                m_gab = P.run("vector", lambda e: e.tensor_tensor(
                    gab_bf[:], row_b[:], b_out_row[:], ALU.mult), "v")
                P.wait("tensor", "v", m_rowb)
                for hh in range(2):
                    mk = P.run("tensor", lambda e, hh=hh: e.matmul(
                        pp[1][:, 0:512], ones_f32[0:1, 0:128],
                        row_b[0:1, hh * 512:(hh + 1) * 512],
                        start=True, stop=True), "t")
                    P.wait("vector", "t", mk)
                    mk2 = P.run("vector", lambda e, hh=hh: e.tensor_copy(
                        ga_rep[:, hh * 512:(hh + 1) * 512], pp[1][:, 0:512]), "v")
                    P.wait("tensor", "v", mk2)
                for col, dstr in ((onecb_col, onecb_rep), (sb_col, sbr_rep)):
                    mk = col_to_row(col, 2)
                    P.wait("vector", "t", mk)
                    m_rowb = row_evac(2)
                    P.wait("tensor", "v", m_rowb)
                    for hh in range(2):
                        mk = P.run("tensor", lambda e, hh=hh: e.matmul(
                            pp[1][:, 0:512], ones_f32[0:1, 0:128],
                            row_b[0:1, hh * 512:(hh + 1) * 512],
                            start=True, stop=True), "t")
                        P.wait("vector", "t", mk)
                        mk2 = P.run("vector", lambda e, d=dstr, hh=hh: e.tensor_copy(
                            d[:, hh * 512:(hh + 1) * 512], pp[1][:, 0:512]), "v")
                        P.wait("tensor", "v", mk2)
                fold_mk = []
                for j in range(DJ):
                    P.waitm("vector", wo_mk[j])
                    fold_mk.append(P.run("vector", lambda e, j=j: e.tensor_tensor(
                        w_out_bf[:, j * D:(j + 1) * D],
                        w_out_bf[:, j * D:(j + 1) * D], ga_rep[:], ALU.mult), "v"))
                m_gab_t = m_gab

                x2_mk, xn_ev, xrs_mk, x2sp_mk, msq_hist = [], [], [], [], []
                m_xn_l = []

                def emit_xnT(i):
                    P.wait("tensor", "v", m_xn_l[i])
                    if len(xn_ev) >= 2:
                        P.wait("tensor", "v", xn_ev[-2])
                    pt = pb[i % 2]
                    mk = None
                    for j in range(DJ):
                        mk = P.run("tensor", lambda e, pt=pt, i=i, j=j: e.transpose(
                            pt[:, j * 128:(j + 1) * 128],
                            xn_tm[:, i * D + j * 128:i * D + (j + 1) * 128],
                            ident_bf[:]), "t" if j == DJ - 1 else None)
                    P.wait("vector", "t", mk)
                    lastev = None
                    for j in range(DJ):
                        lastev = P.runf("vector", lambda e, pt=pt, i=i, j=j:
                                        e.tensor_copy(
                            xn_fm[:, j * N + i * 128:j * N + i * 128 + 128],
                            pt[:, j * 128:(j + 1) * 128]), "v")
                    xn_ev.append(lastev)

                for i in range(NT):
                    if i >= 2:
                        P.wait("sync", "v", x2_mk[i - 2])
                    xrs_mk.append(P.dma("y", x_rs[i % 2][:],
                                        dr["x"][i * 128:(i + 1) * 128, :]))
                    for j in range(DJ):
                        P.wait("tensor", "v", fold_mk[j])
                        for half in range(2):
                            P.run("tensor", lambda e, i=i, j=j, half=half: e.matmul(
                                pp[2 + 2 * (i % 2) + half][:, 0:512],
                                attn_cat[:, j * N + i * 128:j * N + i * 128 + 128],
                                w_out_bf[:, j * D + half * 512:j * D + half * 512 + 512],
                                start=(j == 0), stop=False))
                    wmk = None
                    P.wait("tensor", "v", m_gab_t)
                    for half in range(2):
                        wmk = P.run("tensor", lambda e, i=i, half=half: e.matmul(
                            pp[2 + 2 * (i % 2) + half][:, 0:512],
                            ones_bf[0:1, 0:128],
                            gab_bf[0:1, half * 512:(half + 1) * 512],
                            start=False, stop=True), "t")
                    if i >= 1:
                        emit_xnT(i - 1)
                    P.wait("vector", "t", wmk)
                    P.waitm("vector", xrs_mk[i])
                    if i >= 2:
                        P.waitm("vector", x2sp_mk[i - 2])
                        P.wait("vector", "s", msq_hist[i - 2])
                    for half in range(2):
                        P.runf("vector", lambda e, i=i, half=half: e.tensor_tensor(
                            x2_b[i % 2][:, half * 512:(half + 1) * 512],
                            pp[2 + 2 * (i % 2) + half][:, 0:512],
                            x_rs[i % 2][:, half * 512:(half + 1) * 512], ALU.add))
                    m_x2 = P.runf("vector", lambda e: e.memset(colD[:, 2:3], 0.0), "v")
                    x2_mk.append(m_x2)
                    P.wait("gpsimd", "v", m_x2)
                    x2sp_mk.append(P.dma("g", dr["x2_sp"][i * 128:(i + 1) * 128, :],
                                         x2_b[i % 2][:]))
                    c = i % 16
                    P.wait("scalar", "v", m_x2)
                    m_sq = P.runf("scalar", lambda e, i=i, c=c: e.activation(
                        sq2b[:], x2_b[i % 2][:], AF.Square,
                        accum_out=colA[:, c:c + 1]), "s")
                    msq_hist.append(m_sq)
                    P.run("vector", lambda e, i=i, c=c: e.reduce_sum(
                        colC[:, c:c + 1], x2_b[i % 2][:], AX.X))
                    P.run("vector", lambda e, c=c: e.tensor_scalar(
                        colC[:, c:c + 1], colC[:, c:c + 1], 1.0 / D, None, ALU.mult))
                    P.run("vector", lambda e, c=c: e.tensor_tensor(
                        colB[:, c:c + 1], colC[:, c:c + 1], colC[:, c:c + 1],
                        ALU.mult))
                    P.wait("vector", "s", m_sq)
                    m_var = P.run("vector", lambda e, c=c: e.scalar_tensor_tensor(
                        colB[:, c:c + 1], colA[:, c:c + 1], 1.0 / D, colB[:, c:c + 1],
                        ALU.mult, ALU.subtract), "v")
                    P.wait("scalar", "v", m_var)
                    m_sg = P.runf("scalar", lambda e, c=c: e.activation(
                        colB[:, c:c + 1], colB[:, c:c + 1], AF.Sqrt,
                        bias=eps_col[:, 0:1]), "s")
                    P.wait("vector", "s", m_sg)
                    P.run("vector", lambda e, c=c: e.reciprocal(
                        colB[:, c:c + 1], colB[:, c:c + 1]))
                    P.run("vector", lambda e, i=i, c=c: e.tensor_scalar(
                        h2_b[:], x2_b[i % 2][:], colC[:, c:c + 1], colB[:, c:c + 1],
                        ALU.subtract, ALU.mult))
                    P.run("vector", lambda e: e.tensor_tensor(
                        h2_b[:], h2_b[:], onecb_rep[:], ALU.mult))
                    m_h2 = P.run("vector", lambda e: e.tensor_tensor(
                        h2_b[:], h2_b[:], sbr_rep[:], ALU.add), "v")
                    P.wait("scalar", "v", m_h2)
                    m_sq2 = P.runf("scalar", lambda e, c=c: e.activation(
                        sq2b[:], h2_b[:], AF.Square, accum_out=colA[:, c:c + 1]), "s")
                    m_nrm = P.run("scalar", lambda e, c=c: e.activation(
                        colA[:, c:c + 1], colA[:, c:c + 1], AF.Sqrt), "s")
                    P.wait("vector", "s", m_nrm)
                    P.run("vector", lambda e, c=c: e.tensor_scalar(
                        colA[:, c:c + 1], colA[:, c:c + 1], 1e-12, None, ALU.max))
                    P.run("vector", lambda e, c=c: e.reciprocal(
                        colA[:, c:c + 1], colA[:, c:c + 1]))
                    m_xn = P.run("vector", lambda e, i=i, c=c: e.tensor_scalar(
                        xn_tm[:, i * D:(i + 1) * D], h2_b[:], colA[:, c:c + 1], 32.0,
                        ALU.mult, ALU.mult), "v")
                    m_xn_l.append(m_xn)

                emit_xnT(NT - 1)
                m_rsq = P.run("scalar", lambda e: e.activation(
                    rscale_all[:], ssq_se[:], AF.Sqrt), "s")
                P.wait("vector", "s", m_rsq)
                P.run("vector", lambda e: e.tensor_scalar(
                    rscale_all[:], rscale_all[:], 1e-12, None, ALU.max))
                P.run("vector", lambda e: e.reciprocal(rscale_all[:], rscale_all[:]))
                m_rsc = P.run("vector", lambda e: e.tensor_scalar(
                    rscale_all[:], rscale_all[:], 32.0, None, ALU.mult), "v")

                P.emit("p1")
                if debug:
                    P.dma("o", dbg["attn_cat"][:], attn_cat[:])
                    P.dma("o", dbg["xn_tm"][:], xn_tm[:])
                    P.dma("o", dbg["xn_fm"][:], xn_fm[:])
                    P.dma("o", dbg["rscale"][:], rscale_all[:])
                    P.qdrain("sync", "o")
                    P.emit("p1dbg")
        if stop_after == "p1":
            return nc, dr

        es_sl = ExitStack()
        slots_fm = sb("slots_fm", (128, DJ * S), BF16, es_sl)
        with ExitStack() as es2:
            P.phase_fence()
            gpre = sb("gpre", (128, D), F32, es2)
            se_tm = [sb("setm%d" % i, (128, D), BF16, es2) for i in range(2)]
            se_fs = [sb("sefs%d" % i, (128, D), BF16, es2) for i in range(2)]
            a_ch = [sb("ach%d" % i, (128, N), BF16, es2) for i in range(2)]
            l_ch = [sb("lch%d" % i, (128, N), F32, es2) for i in range(2)]
            a_tm = [sb("atm%d" % i, (128, N), BF16, es2) for i in range(2)]
            sl_sm = [sb("slsm%d" % i, (128, D), BF16, es2) for i in range(2)]
            pL = [ps("pL%d" % i, (128, 512), F32, es2) for i in range(4)]
            pSe = ps("pSe", (128, 1024), BF16, es2)
            pAt = ps("pAt", (128, 1024), BF16, es2)
            pDs = [ps("pDs%d" % i, (128, 512), F32, es2) for i in range(2)]

            for j in range(DJ):
                P.run("vector", lambda e, j=j: e.tensor_scalar(
                    gpre[:, j * 128:(j + 1) * 128], ones_f32[:, 0:128],
                    gprod_col[:, j:j + 1], None, ALU.mult))
            m_gpre = P.run("vector", lambda e: e.memset(colD[:, 4:5], 0.0), "v")
            P.wait("vector", "v", m_gpre)

            sev_m = [None] * SC
            lmk_hist = [None] * SC
            aemk_l = [None] * SC
            lev_l = [None] * SC
            atr_hist = [None] * SC
            dis_m = [None] * SC
            disv = [None] * SC
            sfm_m = [None] * SC
            sfold_m = [None] * SC
            stm_hist = [None] * SC
            lsp_l = [None] * SC
            rdcol = [None] * SC

            def p2_front(sc):
                buf = sc % 2
                if sc >= 2:
                    P.wait("gpsimd", "t", sev_m[sc - 2])
                mse = P.dma("g", se_tm[buf][:],
                            dr["slot_embeds"][sc * 128:(sc + 1) * 128, :])
                P.waitm("tensor", mse)
                if sc >= 1:
                    P.wait("tensor", "v", sfold_m[sc - 1])
                if sc >= 2:
                    P.wait("tensor", "v", sfm_m[sc - 2])
                mk = None
                for j in range(DJ):
                    mk = P.run("tensor", lambda e, buf=buf, j=j: e.transpose(
                        pSe[:, j * 128:(j + 1) * 128],
                        se_tm[buf][:, j * 128:(j + 1) * 128], ident_bf[:]),
                        "t" if j == DJ - 1 else None)
                sev_m[sc] = mk
                P.wait("vector", "t", mk)
                if sc >= 2:
                    P.wait("vector", "t", lmk_hist[sc - 2])
                sfold_m[sc] = P.runf("vector", lambda e, buf=buf: e.tensor_tensor(
                    se_fs[buf][:, 0:1024], pSe[:, 0:1024],
                    gpre[:, 0:1024], ALU.mult), "v")
                P.wait("tensor", "v", sfold_m[sc])
                if sc >= 1:
                    P.wait("tensor", "s", lev_l[sc - 1])
                lmk = None
                for j in range(DJ):
                    for nb in range(4):
                        lmk = P.run("tensor", lambda e, buf=buf, j=j, nb=nb:
                                    e.matmul(
                            pL[nb][:, 0:512],
                            se_fs[buf][:, j * 128:(j + 1) * 128],
                            xn_fm[:, j * N + nb * 512:j * N + (nb + 1) * 512],
                            start=(j == 0), stop=(j == DJ - 1)),
                            "t" if (j == DJ - 1 and nb == 3) else None)
                lmk_hist[sc] = lmk
                p = 4 * (sc % 2)
                P.wait("vector", "t", lmk)
                for nb in range(4):
                    P.runf("vector", lambda e, nb=nb, p=p: e.reduce_max(
                        colA[:, 8 + p + nb:9 + p + nb], pL[nb][:, 0:512], AX.X))
                P.run("vector", lambda e, p=p: e.reduce_max(
                    colC[:, 8 + p:9 + p], colA[:, 8 + p:12 + p], AX.X))
                m_bias = P.run("vector", lambda e, sc=sc, p=p: e.tensor_scalar(
                    colC[:, 9 + p:10 + p], colC[:, 8 + p:9 + p],
                    rscale_all[:, sc:sc + 1], -1.0, ALU.mult, ALU.mult), "v")
                P.wait("scalar", "v", m_bias)
                if sc >= 2:
                    for mkk in lsp_l[sc - 2]:
                        P.waitm("scalar", mkk)
                    P.wait("scalar", "t", atr_hist[sc - 2])
                aemk = None
                for nb in range(4):
                    aemk = P.runf("scalar", lambda e, buf=buf, nb=nb, sc=sc, p=p:
                                  e.activation(
                        a_ch[buf][:, nb * 512:(nb + 1) * 512], pL[nb][:, 0:512],
                        AF.Exp, bias=colC[:, 9 + p:10 + p],
                        scale=rscale_all[:, sc:sc + 1],
                        accum_out=colB[:, 8 + p + nb:9 + p + nb]), "s")
                aemk_l[sc] = aemk
                lev = None
                for nb in range(4):
                    lev = P.runf("scalar", lambda e, buf=buf, nb=nb, sc=sc:
                                 e.activation(
                        l_ch[buf][:, nb * 512:(nb + 1) * 512], pL[nb][:, 0:512],
                        AF.Copy, scale=rscale_all[:, sc:sc + 1]),
                        "s" if nb == 3 else None)
                lev_l[sc] = lev
                P.wait("vector", "s", aemk)
                P.runf("vector", lambda e, p=p: e.reduce_sum(
                    colC[:, 10 + p:11 + p], colB[:, 8 + p:12 + p], AX.X))
                rdcol[sc] = P.run("vector", lambda e, p=p: e.reciprocal(
                    colC[:, 10 + p:11 + p], colC[:, 10 + p:11 + p]), "v")
                P.wait("gpsimd", "s", lev)
                lsp_l[sc] = [P.dma("g", dr["L_sp"][nb8, sc * 128:(sc + 1) * 128, :],
                                   l_ch[buf][:, nb8 * 256:(nb8 + 1) * 256])
                             for nb8 in range(NBL)]

            def p2_back(sc):
                buf = sc % 2
                p = 4 * (sc % 2)
                P.wait("tensor", "s", aemk_l[sc])
                if sc >= 1:
                    P.wait("tensor", "v", disv[sc - 1])
                atev = None
                for pk in range(2):
                    mk = None
                    if pk == 1:
                        P.wait("tensor", "v", atev)
                    elif sc >= 1:
                        P.wait("tensor", "v", atev_prev[0])
                    for t8 in range(8):
                        nt = pk * 8 + t8
                        mk = P.run("tensor", lambda e, buf=buf, nt=nt, t8=t8:
                                   e.transpose(
                            pAt[:, t8 * 128:(t8 + 1) * 128],
                            a_ch[buf][:, nt * 128:(nt + 1) * 128], ident_bf[:]),
                            "t" if t8 == 7 else None)
                    if pk == 1:
                        atr_hist[sc] = mk
                    P.wait("vector", "t", mk)
                    if sc >= 2:
                        P.wait("vector", "t", dis_m[sc - 2])
                    atev = P.runf("vector", lambda e, buf=buf, pk=pk: e.tensor_copy(
                        a_tm[buf][:, pk * 1024:(pk + 1) * 1024], pAt[:, 0:1024]),
                        "v")
                atev_prev[0] = atev
                P.wait("tensor", "v", atev)
                dmk = None
                for nt in range(NT):
                    for half in range(2):
                        dmk = P.run("tensor", lambda e, buf=buf, nt=nt, half=half:
                                    e.matmul(
                            pDs[half][:, 0:512],
                            a_tm[buf][:, nt * 128:(nt + 1) * 128],
                            xn_tm[:, nt * D + half * 512:nt * D + half * 512 + 512],
                            start=(nt == 0), stop=(nt == NT - 1)),
                            "t" if (nt == NT - 1 and half == 1) else None)
                dis_m[sc] = dmk
                P.wait("vector", "t", dmk)
                if sc >= 2:
                    P.wait("vector", "t", stm_hist[sc - 2])
                P.waitm("vector", ("v", rdcol[sc]))
                for half in range(2):
                    P.runf("vector", lambda e, buf=buf, half=half, p=p:
                           e.tensor_scalar(
                        sl_sm[buf][:, half * 512:(half + 1) * 512],
                        pDs[half][:, 0:512], colC[:, 10 + p:11 + p], None,
                        ALU.mult))
                dis_l = P.runf("vector", lambda e: e.memset(colD[:, 3:4], 0.0), "v")
                disv[sc] = dis_l
                P.wait("tensor", "v", dis_l)
                if sc + 1 < SC:
                    P.wait("tensor", "v", sfold_m[sc + 1])
                elif sc >= 1:
                    P.wait("tensor", "v", sfm_m[sc - 1])
                mk = None
                for j in range(DJ):
                    mk = P.run("tensor", lambda e, buf=buf, j=j: e.transpose(
                        pSe[:, j * 128:(j + 1) * 128],
                        sl_sm[buf][:, j * 128:(j + 1) * 128], ident_bf[:]),
                        "t" if j == DJ - 1 else None)
                P.wait("vector", "t", mk)
                sfm_last = None
                for jj in range(DJ):
                    sfm_last = P.runf("vector", lambda e, jj=jj, sc=sc: e.tensor_copy(
                        slots_fm[:, jj * S + sc * 128:jj * S + sc * 128 + 128],
                        pSe[:, jj * 128:(jj + 1) * 128]),
                        "v" if jj == DJ - 1 else None)
                stm_hist[sc] = mk
                sfm_m[sc] = sfm_last

            atev_prev = [None]
            for sc in range(SC):
                p2_front(sc)
                if sc >= 1:
                    p2_back(sc - 1)
            p2_back(SC - 1)

            P.emit("p2")
            if debug:
                P.dma("o", dbg["slots_fm"][:], slots_fm[:])
                P.qdrain("sync", "o")
                P.emit("p2dbg")
        xn_fm_cm.__exit__(None, None, None)
        xn_tm_cm.__exit__(None, None, None)
        if stop_after == "p2":
            es_sl.close()
            return nc, dr

        with ExitStack() as es3:
            P.phase_fence()
            w1_bf = sb("w1_bf", (128, DJ * HID), BF16, es3)
            h_buf = [sb("hbuf%d" % i, (128, N), BF16, es3) for i in range(2)]
            pH = [ps("pH%d" % i, (128, 512), F32, es3) for i in range(8)]

            w1_mk = [P.dma("g", w1_bf[:, j * HID:(j + 1) * HID],
                           dr["w1"][j * 128:(j + 1) * 128, :]) for j in range(DJ)]
            w1f_mk = []
            for j in range(DJ):
                P.waitm("vector", w1_mk[j])
                w1f_mk.append(P.runf("vector", lambda e, j=j: e.tensor_scalar(
                    w1_bf[:, j * HID:(j + 1) * HID], w1_bf[:, j * HID:(j + 1) * HID],
                    gx_col[:, j:j + 1], None, ALU.mult), "v"))

            hsp_m = []
            for hf in range(2):
                for hc in range(HC):
                    pbase = 4 * (hc % 2)
                    if len(hsp_m) >= 2:
                        for mkk in hsp_m[-2]:
                            P.waitm("tensor", mkk)
                    gmk = None
                    for j in range(DJ):
                        P.wait("tensor", "v", w1f_mk[j])
                        for q in range(4):
                            gmk = P.run("tensor",
                                        lambda e, j=j, q=q, hc=hc, hf=hf, pbase=pbase:
                                        e.matmul(
                                pH[pbase + q][:, 0:512],
                                w1_bf[:, j * HID + hc * 128:j * HID + (hc + 1) * 128],
                                slots_fm[:, j * S + hf * 2048 + q * 512:
                                         j * S + hf * 2048 + (q + 1) * 512],
                                start=(j == 0), stop=(j == DJ - 1)),
                                "t" if (j == DJ - 1 and q == 3) else None)
                    P.wait("scalar", "t", gmk)
                    gel = None
                    for q in range(4):
                        gel = P.runf("scalar", lambda e, q=q, hc=hc, pbase=pbase:
                                     e.activation(
                            h_buf[hc % 2][:, q * 512:(q + 1) * 512],
                            pH[pbase + q][:, 0:512], AF.Gelu),
                            "s" if q == 3 else None)
                    P.wait("gpsimd", "s", gel)
                    hsp = []
                    for q8 in range(8):
                        blk8 = hf * 8 + q8
                        hsp.append(P.dma("g",
                                         dr["h_sp"][blk8, hc * 128:(hc + 1) * 128, :],
                                         h_buf[hc % 2][:, q8 * 256:(q8 + 1) * 256]))
                    hsp_m.append(hsp)
            P.emit("p3")
        es_sl.close()
        if stop_after == "p3":
            return nc, dr

        es_pr = ExitStack()
        proc_sm = sb("proc_sm", (128, SC * D), BF16, es_pr)
        with ExitStack() as es4:
            P.phase_fence()
            w2_bf = sb("w2_bf", (128, HC * D), BF16, es4)
            h_blk = [sb("hblk%d" % i, (128, HC * HBW), BF16, es4) for i in range(2)]
            gb_rep = sb("gb_rep", (128, D), F32, es4)
            row_c = sb("row_c", (1, D), F32, es4)
            pP = [ps("pP%d" % i, (128, 512), F32, es4) for i in range(4)]
            pQ = [ps("pQ%d" % i, (128, 512), F32, es4) for i in range(2)]

            mk = None
            for j in range(DJ):
                mk = P.run("tensor", lambda e, j=j: e.matmul(
                    pQ[j // 4][0:1, (j % 4) * 128:(j % 4) * 128 + 128],
                    gb_col[:, j:j + 1],
                    ident_f32[:], start=True, stop=True),
                    "t" if j == DJ - 1 else None)
            P.wait("vector", "t", mk)
            P.run("vector", lambda e: e.tensor_copy(
                row_c[:, 0:512], pQ[0][0:1, 0:512]))
            m_rowc = P.run("vector", lambda e: e.tensor_copy(
                row_c[:, 512:1024], pQ[1][0:1, 0:512]), "v")
            P.wait("tensor", "v", m_rowc)
            for hh in range(2):
                mk = P.run("tensor", lambda e, hh=hh: e.matmul(
                    pQ[1][:, 0:512], ones_f32[0:1, 0:128],
                    row_c[0:1, hh * 512:(hh + 1) * 512], start=True, stop=True), "t")
                P.wait("vector", "t", mk)
                mk2 = P.run("vector", lambda e, hh=hh: e.tensor_copy(
                    gb_rep[:, hh * 512:(hh + 1) * 512], pQ[1][:, 0:512]), "v")
                P.wait("tensor", "v", mk2)
            w2_mk = [P.dma("g", w2_bf[:, hc * D:(hc + 1) * D],
                           dr["w2"][hc * 128:(hc + 1) * 128, :]) for hc in range(HC)]
            w2f_mk = []
            for hc in range(HC):
                P.waitm("vector", w2_mk[hc])
                w2f_mk.append(P.run("vector", lambda e, hc=hc: e.tensor_tensor(
                    w2_bf[:, hc * D:(hc + 1) * D], w2_bf[:, hc * D:(hc + 1) * D],
                    gb_rep[:], ALU.mult), "v", chain=(hc == 0)))

            pev = []
            pmk_hist = []
            for blk in range(HBL):
                if blk >= 2:
                    P.wait("sync", "t", pmk_hist[blk - 2])
                hb_mk = P.dma("y", h_blk[blk % 2][:].rearrange(
                    "p (c w) -> p c w", w=HBW),
                    dr["h_sp"][blk].rearrange("(c p) w -> p c w", p=128))
                for st in range(2):
                    P.waitm("tensor", hb_mk)
                    if len(pev) >= 4:
                        P.wait("tensor", "v", pev[-3])
                    pmk = None
                    for hc in range(HC):
                        P.wait("tensor", "v", w2f_mk[hc])
                        for half in range(2):
                            pmk = P.run("tensor",
                                        lambda e, hc=hc, st=st, half=half, blk=blk:
                                        e.matmul(
                                pP[2 * (st % 2) + half][:, 0:512],
                                h_blk[blk % 2][:, hc * HBW + st * 128:
                                               hc * HBW + (st + 1) * 128],
                                w2_bf[:, hc * D + half * 512:hc * D + half * 512 + 512],
                                start=(hc == 0), stop=(hc == HC - 1)),
                                "t" if (hc == HC - 1 and half == 1) else None)
                    if st == 1:
                        pmk_hist.append(pmk)
                    P.wait("vector", "t", pmk)
                    stt = blk * 2 + st
                    for half in range(2):
                        pev.append(P.runf("vector",
                                          lambda e, stt=stt, st=st, half=half:
                                          e.tensor_copy(
                            proc_sm[:, stt * D + half * 512:stt * D + half * 512 + 512],
                            pP[2 * (st % 2) + half][:, 0:512]), "v"))
            P.emit("p4")
            if debug:
                P.dma("o", dbg["proc_sm"][:], proc_sm[:])
                P.qdrain("sync", "o")
                P.emit("p4dbg")
        if stop_after == "p4":
            es_pr.close()
            return nc, dr

        with ExitStack() as es5:
            P.phase_fence()
            l_blk = [sb("lblk%d" % i, (128, SC * NBW), F32, es5) for i in range(2)]
            l_tmb = sb("l_tmb", (128, S), F32, es5)
            w_tmb = sb("w_tmb", (128, S), BF16, es5)
            w_smb = sb("w_smb", (128, S), BF16, es5)
            x2_rs = [sb("x2rs%d" % i, (128, D), F32, es5) for i in range(2)]
            o_buf = [sb("obuf%d" % i, (128, D), F32, es5) for i in range(2)]
            pT = [ps("pT%d" % i, (128, 512), F32, es5) for i in range(4)]
            pW = [ps("pW%d" % i, (128, 1024), BF16, es5) for i in range(2)]
            pO = [ps("pO%d" % i, (128, 512), F32, es5) for i in range(2)]

            lb_mk = [None] * NBL
            x2r_mk = [None] * NT
            oev_tile = [None] * NT
            ost_mk = [None] * NT
            ltile_done = [None] * NT
            st5 = {"lt_ev": None, "mneg": [None] * NT, "mw": [None] * NT,
                   "wtrl": [None] * NT, "wtrv": [None] * NT,
                   "lt_bank": [None] * 4, "lt_tmk": [None] * 4}

            def p5_lstage(nt):
                nb, ti = nt // 2, nt % 2
                p = 4 * (nt % 2)
                if ti == 0:
                    if nb >= 2:
                        P.wait("sync", "t", ltile_done[(nb - 2) * 2 + 1])
                    lb_mk[nb] = P.dma("y", l_blk[nb % 2][:].rearrange(
                        "p (c w) -> p c w", w=NBW),
                        dr["L_sp"][nb].rearrange("(c p) w -> p c w", p=128))
                if nt >= 2:
                    P.wait("sync", "v", oev_tile[nt - 2])
                x2r_mk[nt] = P.dma("y", x2_rs[nt % 2][:],
                                   dr["x2_sp"][nt * 128:(nt + 1) * 128, :])
                P.waitm("tensor", lb_mk[nb])
                for stg in range(2):
                    mk = None
                    for t16 in range(16):
                        sc = stg * 16 + t16
                        if t16 % 4 == 0:
                            P.wait("tensor", "s", st5["lt_bank"][t16 // 4])
                        mk = P.run("tensor",
                                   lambda e, nb=nb, ti=ti, sc=sc, t16=t16:
                                   e.transpose(
                            pT[t16 // 4][:, (t16 % 4) * 128:(t16 % 4) * 128 + 128],
                            l_blk[nb % 2][:, sc * NBW + ti * 128:
                                          sc * NBW + (ti + 1) * 128],
                            ident_f32[:]), "t" if (t16 % 4) == 3 else None)
                        if t16 % 4 == 3:
                            st5["lt_tmk"][t16 // 4] = mk
                    if stg == 1:
                        ltile_done[nt] = mk
                    if stg == 0 and nt >= 1:
                        P.wait("scalar", "v", st5["mneg"][nt - 1])
                    for q in range(4):
                        P.wait("scalar", "t", st5["lt_tmk"][q])
                        st5["lt_bank"][q] = P.runf("scalar",
                                                   lambda e, stg=stg, q=q:
                                                   e.activation(
                            l_tmb[:, stg * 2048 + q * 512:
                                  stg * 2048 + (q + 1) * 512],
                            pT[q][:, 0:512], AF.Copy), "s")
                st5["lt_ev"] = st5["lt_bank"][3]
                P.wait("vector", "s", st5["lt_ev"])
                P.runf("vector", lambda e, p=p: e.reduce_max(
                    colA[:, p:p + 1], l_tmb[:], AX.X))
                st5["mneg"][nt] = P.run("vector", lambda e, p=p: e.tensor_scalar(
                    colA[:, p + 1:p + 2], colA[:, p:p + 1], -1.0, None,
                    ALU.mult), "v")
                P.wait("scalar", "v", st5["mneg"][nt])
                if nt >= 1:
                    P.wait("scalar", "t", st5["wtrl"][nt - 1])
                st5["mw"][nt] = P.run("scalar", lambda e, p=p: e.activation(
                    w_tmb[:], l_tmb[:], AF.Exp, bias=colA[:, p + 1:p + 2],
                    accum_out=colA[:, p + 2:p + 3]), "s")
                P.wait("vector", "s", st5["mw"][nt])
                P.runf("vector", lambda e, p=p: e.reciprocal(
                    colA[:, p + 3:p + 4], colA[:, p + 2:p + 3]), "v")

            def p5_wt(nt):
                P.wait("tensor", "s", st5["mw"][nt])
                wtr_ev = st5["wtrv"][nt - 1] if nt >= 1 else None
                for pk in range(4):
                    mk = None
                    if pk >= 2 or nt >= 1:
                        P.wait("tensor", "v", wtr_ev)
                    for t8 in range(8):
                        sc = pk * 8 + t8
                        mk = P.run("tensor", lambda e, sc=sc, t8=t8, pk=pk:
                                   e.transpose(
                            pW[pk % 2][:, t8 * 128:(t8 + 1) * 128],
                            w_tmb[:, sc * 128:(sc + 1) * 128], ident_bf[:]),
                            "t" if t8 == 7 else None)
                    P.wait("vector", "t", mk)
                    wtr_ev = P.runf("vector", lambda e, pk=pk: e.tensor_copy(
                        w_smb[:, pk * 1024:(pk + 1) * 1024],
                        pW[pk % 2][:, 0:1024]), "v")
                st5["wtrl"][nt] = mk
                st5["wtrv"][nt] = wtr_ev

            def p5_combine(nt):
                p = 4 * (nt % 2)
                P.wait("tensor", "v", st5["wtrv"][nt])
                if nt >= 1:
                    P.wait("tensor", "v", oev_tile[nt - 1])
                cmk = None
                for sc in range(SC):
                    for half in range(2):
                        cmk = P.run("tensor", lambda e, sc=sc, half=half:
                                    e.matmul(
                            pO[half][:, 0:512],
                            w_smb[:, sc * 128:(sc + 1) * 128],
                            proc_sm[:, sc * D + half * 512:
                                    sc * D + half * 512 + 512],
                            start=(sc == 0), stop=(sc == SC - 1)),
                            "t" if (sc == SC - 1 and half == 1) else None)
                P.wait("vector", "t", cmk)
                P.waitm("vector", x2r_mk[nt])
                if nt >= 2:
                    P.waitm("vector", ost_mk[nt - 2])
                ol = None
                for half in range(2):
                    ol = P.runf("vector",
                                lambda e, nt=nt, half=half, p=p:
                                e.scalar_tensor_tensor(
                        o_buf[nt % 2][:, half * 512:(half + 1) * 512],
                        pO[half][:, 0:512], colA[:, p + 3:p + 4],
                        x2_rs[nt % 2][:, half * 512:(half + 1) * 512],
                        ALU.mult, ALU.add), "v")
                oev_tile[nt] = ol
                P.wait("sync", "v", ol)
                ost_mk[nt] = P.dma("o", dr["out"][nt * 128:(nt + 1) * 128, :],
                                   o_buf[nt % 2][:])

            for nt in range(NT):
                p5_lstage(nt)
                if nt >= 1:
                    p5_combine(nt - 1)
                p5_wt(nt)
            p5_combine(NT - 1)
            P.qdrain("sync", "o")
            P.emit("p5")
        es_pr.close()

    return nc, dr


_CACHE = {}


def _consts():
    ident = np.eye(128)
    ones = np.ones((128, 128))
    jl = np.arange(128)[:, None]
    nl = np.arange(128)[None, :]
    ma = ((jl - nl >= 0) & (jl - nl <= 6)).astype(np.float32)
    jb = np.arange(6)[:, None]
    mb = ((125 + jb - nl >= -3) & (125 + jb - nl <= 3)).astype(np.float32)
    return {
        "ident_bf": ident.astype(BF), "ident_f32": ident.astype(np.float32),
        "ones_bf": ones.astype(BF), "ones_f32": ones.astype(np.float32),
        "mask_a": np.tile(ma, (1, 4)).astype(np.float32),
        "mask_b": np.tile(mb, (1, 4)).astype(np.float32),
    }


def make_in_maps(inputs):
    cst = _consts()
    x = np.ascontiguousarray(np.asarray(inputs["L0_attn_x"], np.float32))
    c = np.asarray(inputs["L0_c"], np.float32)
    base = {
        "w_qkv": np.ascontiguousarray(inputs["L0_attn_w_qkv"], dtype=np.float32),
        "proj": np.ascontiguousarray(inputs["L0_attn_proj"], dtype=np.float32),
        "w_out": np.ascontiguousarray(inputs["L0_attn_w_out"], dtype=np.float32),
        "b_out": np.ascontiguousarray(
            np.asarray(inputs["L0_attn_b_out"], np.float32).reshape(1, D)),
        "gx_col": np.ascontiguousarray(
            np.asarray(inputs["L0_ffn_gamma_x"], np.float32).reshape(DJ, 128).T),
        "gs_col": np.ascontiguousarray(
            np.asarray(inputs["L0_ffn_gamma_slot"], np.float32).reshape(DJ, 128).T),
        "slot_embeds": np.ascontiguousarray(
            np.asarray(inputs["L0_ffn_slot_embeds"], np.float32).reshape(S, D)),
        "w1": np.ascontiguousarray(inputs["L0_ffn_w1"], dtype=np.float32),
        "w2": np.ascontiguousarray(inputs["L0_ffn_w2"], dtype=np.float32),
        "wmod": np.ascontiguousarray(inputs["L0_wmod"], dtype=np.float32),
    }
    base.update(cst)
    maps = []
    for b in range(8):
        m = dict(base)
        m["x"] = np.ascontiguousarray(x[b])
        m["c_fm"] = np.ascontiguousarray(c[b].reshape(DJ, 128).T)
        maps.append(m)
    return maps


def kernel(**inputs):
    if "nc" not in _CACHE:
        nc, dr = build()
        _CACHE["nc"] = nc
    nc = _CACHE["nc"]
    maps = make_in_maps(inputs)
    res = run_bass_kernel_spmd(nc, maps, core_ids=list(range(8)))
    out = np.stack([np.asarray(r["out"], dtype=np.float32) for r in res.results])
    return out


# version 08
# speedup vs baseline: 1.2582x; 1.0392x over previous
import numpy as np
import ml_dtypes
from contextlib import ExitStack

import concourse.bass as bass
import concourse.mybir as mybir
from concourse.bass_utils import run_bass_kernel_spmd

F32 = mybir.dt.float32
BF16 = mybir.dt.bfloat16
AF = mybir.ActivationFunctionType
ALU = mybir.AluOpType
AX = mybir.AxisListType

N = 2048
D = 1024
NT = N // 128
DJ = D // 128
NP = N + 3
H = 8
HD = 128
FT = 64
S = 4096
SC = S // 128
HID = 4096
HC = HID // 128
NBL = 8
NBW = 256
HBL = 16
HBW = 256

BF = ml_dtypes.bfloat16


class Prog:
    ENGS = ("sync", "gpsimd", "tensor", "vector", "scalar")
    QDEPTH = {"y": 10, "g": 10, "o": 4}
    QENG = {"y": "sync", "g": "gpsimd", "o": "sync"}

    def __init__(self, nc, stack):
        self.nc = nc
        self.ops = {e: [] for e in self.ENGS}
        self.sems = {}
        self.counts = {}
        self.waited = {e: {} for e in self.ENGS}
        for name in ("t", "v", "s"):
            self.sems[name] = stack.enter_context(nc.semaphore("sem_" + name))
            self.counts[name] = 0
        self.qidx = {}
        for q, depth in self.QDEPTH.items():
            for i in range(depth):
                key = q + str(i)
                self.sems[key] = stack.enter_context(nc.semaphore("sem_" + key))
                self.counts[key] = 0
            self.qidx[q] = 0

    def wait(self, eng, sem, val):
        if val is None or val <= 0:
            return
        w = self.waited[eng]
        if w.get(sem, -1) >= val:
            return
        w[sem] = val
        h = self.sems[sem]
        self.ops[eng].append(lambda e: e.wait_ge(h, val))

    def waitm(self, eng, mark):
        if mark is None:
            return
        self.wait(eng, mark[0], mark[1])

    def run(self, eng, fn, sem=None, k=1, chain=True):
        if chain and eng in ("vector", "scalar"):
            self.ops[eng].append(lambda e: e.drain())
        if sem is None:
            self.ops[eng].append(fn)
            return None
        self.counts[sem] += k
        mark = self.counts[sem]
        h = self.sems[sem]
        self.ops[eng].append(lambda e: fn(e).then_inc(h, k))
        return mark

    def runf(self, eng, fn, sem=None, k=1):
        return self.run(eng, fn, sem, k, chain=False)

    def dma(self, q, dst, src):
        eng = self.QENG[q]
        i = self.qidx[q]
        self.qidx[q] += 1
        key = q + str(i % self.QDEPTH[q])
        prior = self.counts[key]
        if prior > 0:
            self.wait(eng, key, prior)
        self.counts[key] += 16
        val = self.counts[key]
        h = self.sems[key]
        self.ops[eng].append(lambda e: e.dma_start(dst, src).then_inc(h, 16))
        return (key, val)

    def qdrain(self, eng, q):
        for i in range(self.QDEPTH[q]):
            key = q + str(i)
            self.wait(eng, key, self.counts[key])

    def phase_fence(self):
        for eng in self.ENGS:
            for q in self.QDEPTH:
                self.qdrain(eng, q)
        for eng in ("sync", "gpsimd"):
            for s in ("t", "v", "s"):
                self.wait(eng, s, self.counts[s])

    def emit(self, name):
        with self.nc.Block(name) as blk:
            for eng in self.ENGS:
                ops = self.ops[eng]
                if not ops:
                    continue
                def body(e, ops=tuple(ops)):
                    for f in ops:
                        f(e)
                getattr(blk, eng)(body)
        self.ops = {e: [] for e in self.ENGS}


def build(stop_after=None, debug=False):
    nc = bass.Bass("TRN2", target_bir_lowering=False)

    dr = {}
    def din(name, shape, dtype=F32):
        dr[name] = nc.dram_tensor(name, list(shape), dtype, kind="ExternalInput").ap()
    def dout(name, shape, dtype=F32):
        dr[name] = nc.dram_tensor(name, list(shape), dtype, kind="ExternalOutput").ap()
    def dscr(name, shape, dtype=F32):
        kind = "ExternalOutput" if debug else "Internal"
        dr[name] = nc.dram_tensor(name, list(shape), dtype, kind=kind).ap()

    din("x", (N, D))
    din("w_qkv", (D, 3 * D))
    din("proj", (HD, FT))
    din("w_out", (D, D))
    din("b_out", (1, D))
    din("gx_col", (128, DJ))
    din("gs_col", (128, DJ))
    din("slot_embeds", (S, D))
    din("w1", (D, HID))
    din("w2", (HID, D))
    din("c_fm", (128, DJ))
    din("wmod", (D, 6 * D))
    din("ident_bf", (128, 128), BF16)
    din("ident_f32", (128, 128), F32)
    din("ones_bf", (128, 128), BF16)
    din("ones_f32", (128, 128), F32)
    din("mask_a", (128, 512))
    din("mask_b", (6, 512))
    dout("out", (N, D))
    dscr("L_sp", (NBL, S, NBW))
    dscr("h_sp", (HBL, HID, HBW), BF16)
    dscr("x2_sp", (N, D))

    dbg = {}
    if debug:
        def ddbg(name, shape, dtype=F32):
            dbg[name] = nc.dram_tensor("dbg_" + name, list(shape), dtype,
                                       kind="ExternalOutput").ap()
        ddbg("mod_row", (1, 6 * D))
        ddbg("h1p", (128, DJ * NP), BF16)
        ddbg("vtm", (128, (NT + 1) * D), BF16)
        ddbg("attn_cat", (128, H * N), BF16)
        ddbg("xn_tm", (128, NT * D), BF16)
        ddbg("xn_fm", (128, DJ * N), BF16)
        ddbg("rscale", (128, SC))
        ddbg("slots_fm", (128, DJ * S), BF16)
        ddbg("proc_sm", (128, SC * D), BF16)

    g = ExitStack()
    with g:
        P = Prog(nc, g)

        def sb(name, shape, dtype, stack=None, side=None):
            return (stack or g).enter_context(
                nc.sbuf_tensor(name, list(shape), dtype, side=side))

        def ps(name, shape, dtype, stack=None):
            return (stack or g).enter_context(
                nc.psum_tensor(name, list(shape), dtype))

        ident_bf = sb("ident_bf_s", (128, 128), BF16)
        ident_f32 = sb("ident_f32_s", (128, 128), F32)
        ones_bf = sb("ones_bf_s", (128, 128), BF16)
        ones_f32 = sb("ones_f32_s", (128, 128), F32)
        gx_col = sb("gx_col_s", (128, DJ), F32)
        gs_col = sb("gs_col_s", (128, DJ), F32)
        sa_col = sb("sa_col", (128, DJ), F32)
        oneca_col = sb("oneca_col", (128, DJ), F32)
        ga_col = sb("ga_col", (128, DJ), F32)
        sb_col = sb("sb_col", (128, DJ), F32)
        onecb_col = sb("onecb_col", (128, DJ), F32)
        gb_col = sb("gb_col", (128, DJ), F32)
        gprod_col = sb("gprod_col", (128, DJ), F32)
        rscale_all = sb("rscale_all", (128, SC), F32)
        ssq_se = sb("ssq_se", (128, SC), F32)
        colA = sb("colA", (128, 16), F32)
        colB = sb("colB", (128, 16), F32)
        colC = sb("colC", (128, 16), F32)
        colD = sb("colD", (128, 16), F32)
        eps_col = sb("eps_col", (128, 1), F32)

        h1_cm = nc.sbuf_tensor("h1p", [128, DJ * NP], BF16, side="right")
        h1p = h1_cm.__enter__()

        with ExitStack() as es:
            mod_row = sb("mod_row", (1, 6 * D), F32, es)
            rows_t = sb("rows_t", (1, D), F32, es)
            rows_u = sb("rows_u", (1, D), F32, es)
            c_fm = sb("c_fm_s", (128, DJ), F32, es)
            silu_bf = sb("silu_bf", (128, DJ), BF16, es)
            wsl = [sb("wsl%d" % i, (128, 8 * 512), BF16, es) for i in range(2)]
            x_st = [sb("xst%d" % i, (128, D), F32, es) for i in range(2)]
            sq_scr = sb("sq_scr", (128, D), F32, es)
            ln_bf = [sb("lnbf%d" % i, (128, D), BF16, es) for i in range(2)]
            p_mod = [ps("p_mod%d" % i, (1, 512), F32, es) for i in range(2)]
            p_tpb = [ps("p_tpb%d" % i, (128, 1024), BF16, es) for i in range(2)]
            p_rep = [ps("p_rep%d" % i, (128, 512), F32, es) for i in range(2)]

            for dst, src in [(ident_bf, "ident_bf"), (ident_f32, "ident_f32"),
                             (ones_bf, "ones_bf"), (ones_f32, "ones_f32"),
                             (gx_col, "gx_col"), (gs_col, "gs_col"),
                             (c_fm, "c_fm")]:
                P.dma("y", dst[:], dr[src][:])
            m_eps = P.run("vector", lambda e: e.memset(eps_col[:], 1e-5), "v")
            for j in range(DJ):
                P.run("vector", lambda e, j=j: e.memset(
                    h1p[:, j * NP:j * NP + 3], 0.0))
            P.qdrain("vector", "y")
            m_h1pad = P.run("vector", lambda e: e.tensor_tensor(
                gprod_col[:], gx_col[:], gs_col[:], ALU.mult), "v")

            P.qdrain("scalar", "y")
            m_silu = P.run("scalar", lambda e: e.activation(
                silu_bf[:], c_fm[:], AF.Silu), "s")

            mod_mm = []
            mod_ev = []
            for sl in range(12):
                pm = p_mod[sl % 2]
                wmk = []
                for j in range(DJ):
                    if sl >= 2:
                        P.wait("gpsimd", "t", mod_mm[sl - 2])
                    wmk.append(P.dma("g", wsl[sl % 2][:, j * 512:(j + 1) * 512],
                                     dr["wmod"][j * 128:(j + 1) * 128,
                                                sl * 512:(sl + 1) * 512]))
                if sl >= 2:
                    P.wait("tensor", "v", mod_ev[sl - 2])
                mk = None
                for j in range(DJ):
                    P.waitm("tensor", wmk[j])
                    if sl == 0 and j == 0:
                        P.wait("tensor", "s", m_silu)
                    mk = P.run("tensor", lambda e, pm=pm, j=j, sl=sl: e.matmul(
                        pm[:, 0:512], silu_bf[:, j:j + 1],
                        wsl[sl % 2][:, j * 512:(j + 1) * 512],
                        start=(j == 0), stop=(j == DJ - 1)),
                        "t" if j == DJ - 1 else None)
                mod_mm.append(mk)
                P.wait("vector", "t", mk)
                mod_ev.append(P.runf("vector", lambda e, pm=pm, sl=sl: e.tensor_copy(
                    mod_row[:, sl * 512:(sl + 1) * 512], pm[:, 0:512]), "v"))

            P.wait("vector", "v", mod_ev[-1])
            P.run("vector", lambda e: e.tensor_scalar(
                rows_t[:], mod_row[:, 1024:2048], 1.0, None, ALU.add))
            m_oneca = P.run("vector", lambda e: e.tensor_copy(
                mod_row[:, 1024:2048], rows_t[:]), "v")
            P.run("vector", lambda e: e.tensor_scalar(
                rows_t[:], mod_row[:, 4096:5120], 1.0, None, ALU.add))
            m_onecb = P.run("vector", lambda e: e.tensor_copy(
                mod_row[:, 4096:5120], rows_t[:]), "v")
            m_gv = m_onecb
            for off in (2048, 5120):
                P.wait("scalar", "v", m_gv)
                m_sgn = P.run("scalar", lambda e, o=off: e.activation(
                    rows_t[:], mod_row[:, o:o + D], AF.Sign), "s")
                m_abs = P.run("scalar", lambda e, o=off: e.activation(
                    rows_u[:], mod_row[:, o:o + D], AF.Abs), "s")
                P.wait("vector", "s", m_abs)
                P.run("vector", lambda e, o=off: e.tensor_tensor(
                    mod_row[:, o:o + D], rows_t[:], rows_t[:], ALU.mult))
                P.run("vector", lambda e, o=off: e.scalar_tensor_tensor(
                    rows_t[:], rows_t[:], 1.0, mod_row[:, o:o + D],
                    ALU.add, ALU.subtract))
                P.run("vector", lambda e: e.tensor_scalar(
                    rows_u[:], rows_u[:], 0.25, None, ALU.max))
                P.run("vector", lambda e, o=off: e.tensor_tensor(
                    rows_t[:], rows_t[:], rows_u[:], ALU.mult))
                m_gv = P.run("vector", lambda e, o=off: e.tensor_copy(
                    mod_row[:, o:o + D], rows_t[:]), "v")
            m_rows = m_gv
            P.wait("tensor", "v", m_rows)
            mk = None
            for k6 in range(6):
                for j in range(DJ):
                    mk = P.run("tensor", lambda e, k6=k6, j=j: e.matmul(
                        p_rep[k6 % 2][:, 8 * (k6 // 2) + j:8 * (k6 // 2) + j + 1],
                        mod_row[0:1, k6 * D + j * 128:k6 * D + (j + 1) * 128],
                        ones_f32[0:1, 0:1], start=True, stop=True),
                        "t" if j == DJ - 1 else None)
                if k6 % 2 == 1 or k6 == 5:
                    P.wait("vector", "t", mk)
            P.wait("vector", "t", mk)
            dsts = [sa_col, oneca_col, ga_col, sb_col, onecb_col, gb_col]
            m_cols = None
            for k6 in range(6):
                m_cols = P.run("vector", lambda e, k6=k6: e.tensor_copy(
                    dsts[k6][:],
                    p_rep[k6 % 2][:, 8 * (k6 // 2):8 * (k6 // 2) + 8]), "v")

            ssq_m, var_m, sqrt_m, ev_m = [], [], [], []
            lnh_m = [None] * NT
            x_mk = [None] * NT
            for i in range(NT):
                c = i % 16
                xs = x_st[i % 2]
                if i >= 2:
                    P.wait("sync", "v", lnh_m[i - 2])
                    P.wait("sync", "s", ssq_m[i - 2])
                x_mk[i] = P.dma("y", xs[:], dr["x"][i * 128:(i + 1) * 128, :])
                P.waitm("scalar", x_mk[i])
                if i >= 2:
                    P.wait("scalar", "v", var_m[i - 2])
                ssq_m.append(P.runf("scalar", lambda e, xs=xs, c=c: e.activation(
                    sq_scr[:], xs[:], AF.Square, accum_out=colA[:, c:c + 1]), "s"))
                P.waitm("vector", x_mk[i])
                P.runf("vector", lambda e, xs=xs, c=c: e.reduce_sum(
                    colC[:, c:c + 1], xs[:], AX.X))
                P.run("vector", lambda e, c=c: e.tensor_scalar(
                    colC[:, c:c + 1], colC[:, c:c + 1], 1.0 / D, None, ALU.mult))
                P.run("vector", lambda e, c=c: e.tensor_tensor(
                    colB[:, c:c + 1], colC[:, c:c + 1], colC[:, c:c + 1], ALU.mult))
                P.wait("vector", "s", ssq_m[i])
                var_m.append(P.run("vector", lambda e, c=c: e.scalar_tensor_tensor(
                    colB[:, c:c + 1], colA[:, c:c + 1], 1.0 / D, colB[:, c:c + 1],
                    ALU.mult, ALU.subtract), "v"))
                P.wait("scalar", "v", var_m[i])
                if i == 0:
                    P.wait("scalar", "v", m_eps)
                sqrt_m.append(P.runf("scalar", lambda e, c=c: e.activation(
                    colB[:, c:c + 1], colB[:, c:c + 1], AF.Sqrt,
                    bias=eps_col[:, 0:1]), "s"))
                P.wait("vector", "s", sqrt_m[i])
                P.run("vector", lambda e, c=c: e.reciprocal(
                    colB[:, c:c + 1], colB[:, c:c + 1]))
                lnh_m[i] = P.run("vector", lambda e, xs=xs, c=c, i=i: e.tensor_scalar(
                    ln_bf[i % 2][:], xs[:], colC[:, c:c + 1], colB[:, c:c + 1],
                    ALU.subtract, ALU.mult), "v")
                P.wait("tensor", "v", lnh_m[i])
                if ev_m:
                    P.wait("tensor", "v", ev_m[-1])
                pt = p_tpb[i % 2]
                mk = None
                for j in range(DJ):
                    mk = P.run("tensor", lambda e, pt=pt, j=j, i=i: e.transpose(
                        pt[:, j * 128:(j + 1) * 128],
                        ln_bf[i % 2][:, j * 128:(j + 1) * 128], ident_bf[:]),
                        "t" if j == DJ - 1 else None)
                P.wait("vector", "v", m_cols)
                P.wait("vector", "v", m_h1pad)
                P.wait("vector", "t", mk)
                lastev = None
                for j in range(DJ):
                    lastev = P.runf("vector", lambda e, pt=pt, j=j, i=i: e.tensor_scalar(
                        h1p[:, j * NP + 3 + i * 128: j * NP + 3 + i * 128 + 128],
                        pt[:, j * 128:(j + 1) * 128],
                        oneca_col[:, j:j + 1], sa_col[:, j:j + 1],
                        ALU.mult, ALU.add), "v" if j == DJ - 1 else None)
                ev_m.append(lastev)

            P.emit("p0")
            if debug:
                P.dma("o", dbg["mod_row"][:], mod_row[:])
                P.dma("o", dbg["h1p"][:], h1p[:])
                P.qdrain("sync", "o")
                P.emit("p0dbg")
        if stop_after == "p0":
            return nc, dr

        with ExitStack() as es1:
            attn_cat = sb("attn_cat", (128, H * N), BF16, es1)
            pp = [ps("pp%d" % i, (128, 512), F32, es1) for i in range(6)]
            pb = [ps("pb%d" % i, (128, 1024), BF16, es1) for i in range(2)]

            with ExitStack() as esh:
                mask_a = sb("mask_a_s", (128, 512), F32, esh)
                mask_b = sb("mask_b_s", (6, 512), F32, esh)
                wq_bf = [sb("wq%d" % i, (128, D), BF16, esh) for i in range(2)]
                wk_bf = [sb("wk%d" % i, (128, D), BF16, esh) for i in range(2)]
                proj_bf = sb("proj_bf", (128, FT), BF16, esh)
                v_tm = sb("v_tm", (128, (NT + 1) * D), BF16, esh)
                q_bf = sb("q_bf", (128, N), BF16, esh)
                k_bf = sb("k_bf", (128, N), BF16, esh)
                q_phi = [sb("q_phi%d" % i, (64, N), BF16, esh) for i in range(2)]
                k_pad = [sb("k_pad%d" % i, (64, N + 6), BF16, esh)
                         for i in range(2)]
                t_min_q = sb("t_min_q", (64, 512), F32, esh)
                t_min_k = sb("t_min_k", (64, 512), F32, esh)
                s_a = [sb("s_a%d" % i, (128, 512), BF16, esh) for i in range(2)]
                s_b = [sb("s_b%d" % i, (6, 512), BF16, esh) for i in range(2)]
                attn_tm = [sb("attn_tm%d" % i, (128, 512), BF16, esh)
                           for i in range(2)]
                se_st = sb("se_st", (128, D), F32, esh)
                sq2 = sb("sq2", (128, 2 * D), F32, esh)

                mka_mk = P.dma("y", mask_a[:], dr["mask_a"][:])
                mkb_mk = P.dma("y", mask_b[:], dr["mask_b"][:])
                m_proj = P.dma("g", proj_bf[:], dr["proj"][:])

                def qkv_block(dst, col0):
                    src = dr["w_qkv"][:, col0:col0 + 128].rearrange(
                        "(j p) c -> p j c", p=128)
                    d = dst[:].rearrange("p (j c) -> p j c", c=128)
                    return P.dma("g", d, src)

                wq_mk = [qkv_block(wq_bf[h], h * 128) for h in range(2)]
                wk_mk = [qkv_block(wk_bf[h], D + h * 128) for h in range(2)]

                with ExitStack() as esv:
                    wv_bf = sb("wv_bf", (128, DJ * D), BF16, esv)
                    wv_mk = [P.dma("g", wv_bf[:, j * D:(j + 1) * D],
                                   dr["w_qkv"][j * 128:(j + 1) * 128, 2 * D:3 * D])
                             for j in range(DJ)]
                    m_vz = P.run("vector", lambda e: e.memset(
                        v_tm[0:6, NT * D:NT * D + D], 0.0), "v")
                    vev = []
                    for i in range(NT + 1):
                        rows = 128 if i < NT else 3
                        for half in range(2):
                            pv = pp[2 * (i % 2) + half]
                            if len(vev) >= 4:
                                P.wait("tensor", "v", vev[-4])
                            mk = None
                            for j in range(DJ):
                                P.waitm("tensor", wv_mk[j])
                                if i < NT:
                                    st_ap = h1p[:, j * NP + i * 128:
                                                j * NP + i * 128 + 128]
                                else:
                                    st_ap = h1p[:, j * NP + N:j * NP + N + 3]
                                mk = P.run("tensor",
                                           lambda e, pv=pv, st_ap=st_ap, j=j,
                                           half=half, r=rows: e.matmul(
                                    pv[0:r, 0:512], st_ap,
                                    wv_bf[:, j * D + half * 512:j * D + half * 512 + 512],
                                    start=(j == 0), stop=(j == DJ - 1)),
                                    "t" if j == DJ - 1 else None)
                            P.wait("vector", "t", mk)
                            if i == NT:
                                P.wait("vector", "v", m_vz)
                            vev.append(P.runf("vector",
                                              lambda e, pv=pv, i=i, half=half, r=rows:
                                              e.tensor_copy(
                                v_tm[0:r, i * D + half * 512:i * D + half * 512 + 512],
                                pv[0:r, 0:512]), "v"))
                    m_vtm_done = vev[-1]
                    if debug:
                        P.wait("sync", "v", m_vtm_done)
                        P.dma("o", dbg["vtm"][:, 0:NT * D], v_tm[:, 0:NT * D])
                        P.dma("o", dbg["vtm"][0:6, NT * D:NT * D + D],
                              v_tm[0:6, NT * D:NT * D + D])

                zpad_done = [None, None]
                hst = [dict() for _ in range(H)]
                qk_use = [None] * H

                def qk_pass(h, p):
                    dstq = q_bf if p < 2 else k_bf
                    wsrc = wq_bf if p < 2 else wk_bf
                    mks = wq_mk if p < 2 else wk_mk
                    qh = p % 2
                    st = hst[h]
                    P.waitm("tensor", mks[h])
                    P.wait("tensor", "v", st.get("qk_ev"))
                    if p == 0:
                        if h >= 1:
                            P.wait("tensor", "v", hst[h - 1]["phi"])
                        else:
                            P.wait("tensor", "v", m_vtm_done)
                    mk = None
                    for j in range(DJ):
                        for b2 in range(2):
                            mk = P.run("tensor",
                                       lambda e, h=h, j=j, b2=b2, qh=qh, wsrc=wsrc:
                                       e.matmul(
                                pp[b2][:, 0:512],
                                wsrc[h % 2][:, j * 128:(j + 1) * 128],
                                h1p[:, j * NP + 3 + qh * 1024 + b2 * 512:
                                    j * NP + 3 + qh * 1024 + b2 * 512 + 512],
                                start=(j == 0), stop=(j == DJ - 1)),
                                "t" if (j == DJ - 1 and b2 == 1) else None)
                    if p == 3:
                        qk_use[h] = mk
                    P.wait("vector", "t", mk)
                    ev = None
                    for b2 in range(2):
                        ev = P.runf("vector", lambda e, b2=b2, qh=qh, dstq=dstq:
                                    e.tensor_copy(
                            dstq[:, qh * 1024 + b2 * 512:qh * 1024 + b2 * 512 + 512],
                            pp[b2][:, 0:512]), "v")
                    st["qk_ev"] = ev

                def proj_pass(h, b4):
                    st = hst[h]
                    hp = h % 2
                    P.waitm("tensor", m_proj)
                    P.wait("tensor", "v", st.get("phi_p", st["qk_ev"]))
                    P.run("tensor", lambda e, b4=b4: e.matmul(
                        pp[0][0:64, 0:512], proj_bf[:, 0:FT],
                        q_bf[:, b4 * 512:(b4 + 1) * 512], start=True, stop=True))
                    mkp = P.run("tensor", lambda e, b4=b4: e.matmul(
                        pp[1][0:64, 0:512], proj_bf[:, 0:FT],
                        k_bf[:, b4 * 512:(b4 + 1) * 512], start=True, stop=True), "t")
                    P.wait("vector", "t", mkp)
                    if h >= 2:
                        P.wait("vector", "t", hst[h - 2].get("wlast"))
                    P.runf("vector", lambda e, b4=b4: e.tensor_scalar(
                        t_min_q[:, 0:512], pp[0][0:64, 0:512], 0.0, None, ALU.min))
                    m_tmin = P.runf("vector", lambda e, b4=b4: e.tensor_scalar(
                        t_min_k[:, 0:512], pp[1][0:64, 0:512], 0.0, None,
                        ALU.min), "v")
                    P.wait("scalar", "v", m_tmin)
                    P.runf("scalar", lambda e: e.activation(
                        t_min_q[:, 0:512], t_min_q[:, 0:512], AF.Exp))
                    m_exp = P.runf("scalar", lambda e: e.activation(
                        t_min_k[:, 0:512], t_min_k[:, 0:512], AF.Exp), "s")
                    P.wait("vector", "s", m_exp)
                    P.runf("vector", lambda e, b4=b4, hp=hp: e.scalar_tensor_tensor(
                        q_phi[hp][:, b4 * 512:(b4 + 1) * 512], pp[0][0:64, 0:512],
                        0.0, t_min_q[:, 0:512], ALU.max, ALU.add))
                    phi = P.runf("vector", lambda e, b4=b4, hp=hp:
                                 e.scalar_tensor_tensor(
                        k_pad[hp][:, 3 + b4 * 512:3 + (b4 + 1) * 512],
                        pp[1][0:64, 0:512],
                        0.0, t_min_k[:, 0:512], ALU.max, ALU.add), "v")
                    st["phi_p"] = phi
                    if b4 == 3:
                        if zpad_done[hp] is None:
                            P.runf("vector", lambda e, hp=hp: e.memset(
                                k_pad[hp][:, 0:3], 0.0))
                            phi = P.runf("vector", lambda e, hp=hp: e.memset(
                                k_pad[hp][:, N + 3:N + 6], 0.0), "v")
                            zpad_done[hp] = phi
                        st["phi"] = phi

                def wA(h, grp):
                    st = hst[h]
                    hp = h % 2
                    P.wait("tensor", "v", st["phi"])
                    P.wait("tensor", "v", zpad_done[hp])
                    prev = st if grp >= 1 else (hst[h - 1] if h >= 1 else None)
                    if prev is not None and "sev" in prev:
                        P.wait("tensor", "v", prev["sev"])
                    smk = None
                    for t4 in range(4):
                        t = grp * 4 + t4
                        c0 = t4 * 128
                        P.run("tensor", lambda e, t=t, c0=c0, hp=hp: e.matmul(
                            pp[2][:, c0:c0 + 128],
                            k_pad[hp][:, t * 128:t * 128 + 128],
                            q_phi[hp][:, t * 128:(t + 1) * 128],
                            start=True, stop=True))
                        smk = P.run("tensor", lambda e, t=t, c0=c0, hp=hp: e.matmul(
                            pp[3][0:6, c0:c0 + 128],
                            k_pad[hp][:, t * 128 + 128:t * 128 + 134],
                            q_phi[hp][:, t * 128:(t + 1) * 128],
                            start=True, stop=True), "t")
                    P.wait("vector", "t", smk)
                    P.waitm("vector", mka_mk)
                    P.waitm("vector", mkb_mk)
                    P.runf("vector", lambda e, g=grp: e.tensor_tensor(
                        s_a[g % 2][:], pp[2][:, 0:512], mask_a[:], ALU.mult))
                    st["sev"] = P.runf("vector", lambda e, g=grp: e.tensor_tensor(
                        s_b[g % 2][:], pp[3][0:6, 0:512], mask_b[:],
                        ALU.mult), "v")

                def wB(h, grp):
                    st = hst[h]
                    prev = st if grp >= 1 else (hst[h - 1] if h >= 1 else None)
                    P.wait("tensor", "v", st["sev"])
                    if prev is not None and "zr" in prev:
                        P.wait("tensor", "v", prev["zr"])
                        P.wait("tensor", "s", prev["cpm"])
                    amk = None
                    for t4 in range(4):
                        t = grp * 4 + t4
                        c0 = t4 * 128
                        sa_ = s_a[grp % 2]
                        sb_ = s_b[grp % 2]
                        P.run("tensor", lambda e, t=t, c0=c0, sa_=sa_, h=h:
                              e.matmul(
                            pp[4][:, c0:c0 + 128], sa_[:, c0:c0 + 128],
                            v_tm[:, t * D + h * 128:t * D + (h + 1) * 128],
                            start=True, stop=False))
                        P.run("tensor", lambda e, t=t, c0=c0, sb_=sb_, h=h:
                              e.matmul(
                            pp[4][:, c0:c0 + 128], sb_[:, c0:c0 + 128],
                            v_tm[0:6, (t + 1) * D + h * 128:(t + 1) * D + (h + 1) * 128],
                            start=False, stop=True))
                        P.run("tensor", lambda e, t4=t4, c0=c0, sa_=sa_: e.matmul(
                            pp[5][:, t4:t4 + 1], sa_[:, c0:c0 + 128],
                            ones_bf[:, 0:1], start=True, stop=False))
                        amk = P.run("tensor", lambda e, t4=t4, c0=c0, sb_=sb_:
                                    e.matmul(
                            pp[5][:, t4:t4 + 1], sb_[:, c0:c0 + 128],
                            ones_bf[0:6, 0:1], start=False, stop=True), "t")
                    P.wait("vector", "t", amk)
                    P.runf("vector", lambda e: e.tensor_scalar(
                        colA[:, 8:12], pp[5][:, 0:4], 1e-8, None, ALU.add))
                    st["zr"] = P.run("vector", lambda e: e.reciprocal(
                        colA[:, 8:12], colA[:, 8:12]), "v")
                    P.wait("scalar", "v", st["zr"])
                    cpms = []
                    for t4 in range(4):
                        cpms.append(P.runf("scalar",
                                           lambda e, t4=t4, g=grp: e.activation(
                            attn_tm[g % 2][:, t4 * 128:(t4 + 1) * 128],
                            pp[4][:, t4 * 128:(t4 + 1) * 128], AF.Copy,
                            scale=colA[:, 8 + t4:9 + t4]), "s"))
                    st["cpm"] = cpms[-1]
                    st["cpms"] = cpms

                def wC(h, grp):
                    st = hst[h]
                    if "cat" in st:
                        P.wait("tensor", "v", st["cat"])
                    elif h >= 1:
                        P.wait("tensor", "v", hst[h - 1]["cat"])
                    trm = None
                    for t4 in range(4):
                        P.wait("tensor", "s", st["cpms"][t4])
                        trm = P.run("tensor", lambda e, t4=t4, g=grp: e.transpose(
                            pb[0][:, t4 * 128:(t4 + 1) * 128],
                            attn_tm[g % 2][:, t4 * 128:(t4 + 1) * 128],
                            ident_bf[:]), "t")
                    st["wlast"] = trm
                    P.wait("vector", "t", trm)
                    st["cat"] = P.runf("vector", lambda e, g=grp, h=h: e.tensor_copy(
                        attn_cat[:, h * N + g * 512:h * N + (g + 1) * 512],
                        pb[0][:, 0:512]), "v")

                for h in range(H):
                    hm = h - 1
                    qk_pass(h, 0)
                    if hm >= 1:
                        wB(hm - 1, 3)
                    qk_pass(h, 1)
                    if hm >= 1:
                        wC(hm - 1, 2)
                    qk_pass(h, 2)
                    if hm >= 1:
                        wC(hm - 1, 3)
                    qk_pass(h, 3)
                    if h + 2 < H:
                        P.waitm("gpsimd", ("t", qk_use[h]))
                        wq_mk.append(qkv_block(wq_bf[(h + 2) % 2], (h + 2) * 128))
                        wk_mk.append(qkv_block(wk_bf[(h + 2) % 2],
                                               D + (h + 2) * 128))
                    for p in range(4):
                        proj_pass(h, p)
                        if hm >= 0:
                            wA(hm, p)
                            if p >= 1:
                                wB(hm, p - 1)
                            if p >= 2:
                                wC(hm, p - 2)
                    for sc in range(4 * h, 4 * h + 4):
                        mse = P.dma("g", se_st[:],
                                    dr["slot_embeds"][sc * 128:(sc + 1) * 128, :])
                        P.waitm("scalar", mse)
                        m_ssq = P.runf("scalar", lambda e, sc=sc: e.activation(
                            sq2[:, (sc % 2) * 1024:(sc % 2) * 1024 + 1024],
                            se_st[:], AF.Square,
                            accum_out=ssq_se[:, sc:sc + 1]), "s")
                        P.wait("gpsimd", "s", m_ssq)
                wB(H - 2, 3)
                wC(H - 2, 2)
                wC(H - 2, 3)
                for grp in range(4):
                    wA(H - 1, grp)
                    wB(H - 1, grp)
                    wC(H - 1, grp)
                m_cpm_last = hst[H - 1]["cpm"]
                m_attn_done = P.counts["v"]
                P.emit("p1a")
                P.phase_fence()
            h1_cm.__exit__(None, None, None)

            xn_tm_cm = nc.sbuf_tensor("xn_tm", [128, NT * D], BF16, side="right")
            xn_fm_cm = nc.sbuf_tensor("xn_fm", [128, DJ * N], BF16, side="right")
            xn_tm = xn_tm_cm.__enter__()
            xn_fm = xn_fm_cm.__enter__()

            with ExitStack() as esl:
                w_out_bf = sb("w_out_bf", (128, DJ * D), BF16, esl)
                ga_rep = sb("ga_rep", (128, D), F32, esl)
                row_b = sb("row_b", (1, D), F32, esl)
                b_out_row = sb("b_out_row", (1, D), F32, esl)
                gab_bf = sb("gab_bf", (1, D), BF16, esl)
                onecb_rep = sb("onecb_rep", (128, D), F32, esl)
                sbr_rep = sb("sbr_rep", (128, D), F32, esl)
                x_rs = [sb("xrs%d" % i, (128, D), F32, esl) for i in range(2)]
                x2_b = [sb("x2b%d" % i, (128, D), F32, esl) for i in range(2)]
                h2_b = sb("h2_b", (128, D), F32, esl)
                sq2b = sb("sq2b", (128, D), F32, esl)

                m_bout = P.dma("y", b_out_row[:], dr["b_out"][:])
                wo_mk = [P.dma("g", w_out_bf[:, j * D:(j + 1) * D],
                               dr["w_out"][j * 128:(j + 1) * 128, :])
                         for j in range(DJ)]

                def col_to_row(col, pidx):
                    mk = None
                    for j in range(DJ):
                        mk = P.run("tensor", lambda e, j=j, col=col, p=pidx: e.matmul(
                            pp[p + j // 4][0:1, (j % 4) * 128:(j % 4) * 128 + 128],
                            col[:, j:j + 1], ident_f32[:],
                            start=True, stop=True), "t" if j == DJ - 1 else None)
                    return mk

                def row_evac(pidx):
                    P.run("vector", lambda e, p=pidx: e.tensor_copy(
                        row_b[:, 0:512], pp[p][0:1, 0:512]))
                    return P.run("vector", lambda e, p=pidx: e.tensor_copy(
                        row_b[:, 512:1024], pp[p + 1][0:1, 0:512]), "v")

                P.wait("tensor", "v", m_attn_done)
                P.wait("tensor", "s", m_cpm_last)
                mk = col_to_row(ga_col, 0)
                P.wait("vector", "t", mk)
                m_rowb = row_evac(0)
                P.waitm("vector", m_bout)
                m_gab = P.run("vector", lambda e: e.tensor_tensor(
                    gab_bf[:], row_b[:], b_out_row[:], ALU.mult), "v")
                P.wait("tensor", "v", m_rowb)
                for hh in range(2):
                    mk = P.run("tensor", lambda e, hh=hh: e.matmul(
                        pp[1][:, 0:512], ones_f32[0:1, 0:128],
                        row_b[0:1, hh * 512:(hh + 1) * 512],
                        start=True, stop=True), "t")
                    P.wait("vector", "t", mk)
                    mk2 = P.run("vector", lambda e, hh=hh: e.tensor_copy(
                        ga_rep[:, hh * 512:(hh + 1) * 512], pp[1][:, 0:512]), "v")
                    P.wait("tensor", "v", mk2)
                for col, dstr in ((onecb_col, onecb_rep), (sb_col, sbr_rep)):
                    mk = col_to_row(col, 2)
                    P.wait("vector", "t", mk)
                    m_rowb = row_evac(2)
                    P.wait("tensor", "v", m_rowb)
                    for hh in range(2):
                        mk = P.run("tensor", lambda e, hh=hh: e.matmul(
                            pp[1][:, 0:512], ones_f32[0:1, 0:128],
                            row_b[0:1, hh * 512:(hh + 1) * 512],
                            start=True, stop=True), "t")
                        P.wait("vector", "t", mk)
                        mk2 = P.run("vector", lambda e, d=dstr, hh=hh: e.tensor_copy(
                            d[:, hh * 512:(hh + 1) * 512], pp[1][:, 0:512]), "v")
                        P.wait("tensor", "v", mk2)
                fold_mk = []
                for j in range(DJ):
                    P.waitm("vector", wo_mk[j])
                    fold_mk.append(P.run("vector", lambda e, j=j: e.tensor_tensor(
                        w_out_bf[:, j * D:(j + 1) * D],
                        w_out_bf[:, j * D:(j + 1) * D], ga_rep[:], ALU.mult), "v"))
                m_gab_t = m_gab

                x2_mk, xn_ev, xrs_mk, x2sp_mk, msq_hist = [], [], [], [], []
                m_xn_l = []

                def emit_xnT(i):
                    P.wait("tensor", "v", m_xn_l[i])
                    if len(xn_ev) >= 2:
                        P.wait("tensor", "v", xn_ev[-2])
                    pt = pb[i % 2]
                    mk = None
                    for j in range(DJ):
                        mk = P.run("tensor", lambda e, pt=pt, i=i, j=j: e.transpose(
                            pt[:, j * 128:(j + 1) * 128],
                            xn_tm[:, i * D + j * 128:i * D + (j + 1) * 128],
                            ident_bf[:]), "t" if j == DJ - 1 else None)
                    P.wait("vector", "t", mk)
                    lastev = None
                    for j in range(DJ):
                        lastev = P.runf("vector", lambda e, pt=pt, i=i, j=j:
                                        e.tensor_copy(
                            xn_fm[:, j * N + i * 128:j * N + i * 128 + 128],
                            pt[:, j * 128:(j + 1) * 128]), "v")
                    xn_ev.append(lastev)

                for i in range(NT):
                    if i >= 2:
                        P.wait("sync", "v", x2_mk[i - 2])
                    xrs_mk.append(P.dma("y", x_rs[i % 2][:],
                                        dr["x"][i * 128:(i + 1) * 128, :]))
                    if i >= 2:
                        P.wait("tensor", "v", x2_mk[i - 2])
                    for j in range(DJ):
                        P.wait("tensor", "v", fold_mk[j])
                        for half in range(2):
                            P.run("tensor", lambda e, i=i, j=j, half=half: e.matmul(
                                pp[2 + 2 * (i % 2) + half][:, 0:512],
                                attn_cat[:, j * N + i * 128:j * N + i * 128 + 128],
                                w_out_bf[:, j * D + half * 512:j * D + half * 512 + 512],
                                start=(j == 0), stop=False))
                    wmk = None
                    P.wait("tensor", "v", m_gab_t)
                    for half in range(2):
                        wmk = P.run("tensor", lambda e, i=i, half=half: e.matmul(
                            pp[2 + 2 * (i % 2) + half][:, 0:512],
                            ones_bf[0:1, 0:128],
                            gab_bf[0:1, half * 512:(half + 1) * 512],
                            start=False, stop=True), "t")
                    if i >= 2:
                        emit_xnT(i - 2)
                    P.wait("vector", "t", wmk)
                    P.waitm("vector", xrs_mk[i])
                    if i >= 2:
                        P.waitm("vector", x2sp_mk[i - 2])
                        P.wait("vector", "s", msq_hist[i - 2])
                    for half in range(2):
                        P.runf("vector", lambda e, i=i, half=half: e.tensor_tensor(
                            x2_b[i % 2][:, half * 512:(half + 1) * 512],
                            pp[2 + 2 * (i % 2) + half][:, 0:512],
                            x_rs[i % 2][:, half * 512:(half + 1) * 512], ALU.add))
                    m_x2 = P.runf("vector", lambda e: e.memset(colD[:, 2:3], 0.0), "v")
                    x2_mk.append(m_x2)
                    P.wait("gpsimd", "v", m_x2)
                    x2sp_mk.append(P.dma("g", dr["x2_sp"][i * 128:(i + 1) * 128, :],
                                         x2_b[i % 2][:]))
                    c = i % 16
                    P.wait("scalar", "v", m_x2)
                    m_sq = P.runf("scalar", lambda e, i=i, c=c: e.activation(
                        sq2b[:], x2_b[i % 2][:], AF.Square,
                        accum_out=colA[:, c:c + 1]), "s")
                    msq_hist.append(m_sq)
                    P.run("vector", lambda e, i=i, c=c: e.reduce_sum(
                        colC[:, c:c + 1], x2_b[i % 2][:], AX.X))
                    P.run("vector", lambda e, c=c: e.tensor_scalar(
                        colC[:, c:c + 1], colC[:, c:c + 1], 1.0 / D, None, ALU.mult))
                    P.run("vector", lambda e, c=c: e.tensor_tensor(
                        colB[:, c:c + 1], colC[:, c:c + 1], colC[:, c:c + 1],
                        ALU.mult))
                    P.wait("vector", "s", m_sq)
                    m_var = P.run("vector", lambda e, c=c: e.scalar_tensor_tensor(
                        colB[:, c:c + 1], colA[:, c:c + 1], 1.0 / D, colB[:, c:c + 1],
                        ALU.mult, ALU.subtract), "v")
                    P.wait("scalar", "v", m_var)
                    m_sg = P.runf("scalar", lambda e, c=c: e.activation(
                        colB[:, c:c + 1], colB[:, c:c + 1], AF.Sqrt,
                        bias=eps_col[:, 0:1]), "s")
                    P.wait("vector", "s", m_sg)
                    P.run("vector", lambda e, c=c: e.reciprocal(
                        colB[:, c:c + 1], colB[:, c:c + 1]))
                    P.run("vector", lambda e, i=i, c=c: e.tensor_scalar(
                        h2_b[:], x2_b[i % 2][:], colC[:, c:c + 1], colB[:, c:c + 1],
                        ALU.subtract, ALU.mult))
                    P.run("vector", lambda e: e.tensor_tensor(
                        h2_b[:], h2_b[:], onecb_rep[:], ALU.mult))
                    m_h2 = P.run("vector", lambda e: e.tensor_tensor(
                        h2_b[:], h2_b[:], sbr_rep[:], ALU.add), "v")
                    P.wait("scalar", "v", m_h2)
                    m_sq2 = P.runf("scalar", lambda e, c=c: e.activation(
                        sq2b[:], h2_b[:], AF.Square, accum_out=colA[:, c:c + 1]), "s")
                    m_nrm = P.run("scalar", lambda e, c=c: e.activation(
                        colA[:, c:c + 1], colA[:, c:c + 1], AF.Sqrt), "s")
                    P.wait("vector", "s", m_nrm)
                    P.run("vector", lambda e, c=c: e.tensor_scalar(
                        colA[:, c:c + 1], colA[:, c:c + 1], 1e-12, None, ALU.max))
                    P.run("vector", lambda e, c=c: e.reciprocal(
                        colA[:, c:c + 1], colA[:, c:c + 1]))
                    m_xn = P.run("vector", lambda e, i=i, c=c: e.tensor_scalar(
                        xn_tm[:, i * D:(i + 1) * D], h2_b[:], colA[:, c:c + 1], 32.0,
                        ALU.mult, ALU.mult), "v")
                    m_xn_l.append(m_xn)

                emit_xnT(NT - 2)
                emit_xnT(NT - 1)
                m_rsq = P.run("scalar", lambda e: e.activation(
                    rscale_all[:], ssq_se[:], AF.Sqrt), "s")
                P.wait("vector", "s", m_rsq)
                P.run("vector", lambda e: e.tensor_scalar(
                    rscale_all[:], rscale_all[:], 1e-12, None, ALU.max))
                P.run("vector", lambda e: e.reciprocal(rscale_all[:], rscale_all[:]))
                m_rsc = P.run("vector", lambda e: e.tensor_scalar(
                    rscale_all[:], rscale_all[:], 32.0, None, ALU.mult), "v")

                P.emit("p1")
                if debug:
                    P.dma("o", dbg["attn_cat"][:], attn_cat[:])
                    P.dma("o", dbg["xn_tm"][:], xn_tm[:])
                    P.dma("o", dbg["xn_fm"][:], xn_fm[:])
                    P.dma("o", dbg["rscale"][:], rscale_all[:])
                    P.qdrain("sync", "o")
                    P.emit("p1dbg")
        if stop_after == "p1":
            return nc, dr

        es_sl = ExitStack()
        slots_fm = sb("slots_fm", (128, DJ * S), BF16, es_sl)
        with ExitStack() as es2:
            P.phase_fence()
            gpre = sb("gpre", (128, D), F32, es2)
            se_tm = [sb("setm%d" % i, (128, D), BF16, es2) for i in range(2)]
            se_fs = [sb("sefs%d" % i, (128, D), BF16, es2) for i in range(2)]
            a_ch = [sb("ach%d" % i, (128, N), BF16, es2) for i in range(2)]
            l_ch = [sb("lch%d" % i, (128, N), F32, es2) for i in range(2)]
            a_tm = [sb("atm%d" % i, (128, N), BF16, es2) for i in range(2)]
            sl_sm = [sb("slsm%d" % i, (128, D), BF16, es2) for i in range(2)]
            pL = [ps("pL%d" % i, (128, 512), F32, es2) for i in range(4)]
            pSe = ps("pSe", (128, 1024), BF16, es2)
            pAt = ps("pAt", (128, 1024), BF16, es2)
            pDs = [ps("pDs%d" % i, (128, 512), F32, es2) for i in range(2)]

            for j in range(DJ):
                P.run("vector", lambda e, j=j: e.tensor_scalar(
                    gpre[:, j * 128:(j + 1) * 128], ones_f32[:, 0:128],
                    gprod_col[:, j:j + 1], None, ALU.mult))
            m_gpre = P.run("vector", lambda e: e.memset(colD[:, 4:5], 0.0), "v")
            P.wait("vector", "v", m_gpre)

            sev_m = [None] * SC
            lmk_hist = [None] * SC
            aemk_l = [None] * SC
            lev_l = [None] * SC
            atr_hist = [None] * SC
            dis_m = [None] * SC
            disv = [None] * SC
            sfm_m = [None] * SC
            sfold_m = [None] * SC
            stm_hist = [None] * SC
            lsp_l = [None] * SC
            rdcol = [None] * SC

            se_mk = [None] * (SC + 1)

            def p2_se_load(sc):
                if sc >= SC:
                    return
                if sc >= 2:
                    P.wait("gpsimd", "t", sev_m[sc - 2])
                se_mk[sc] = P.dma("g", se_tm[sc % 2][:],
                                  dr["slot_embeds"][sc * 128:(sc + 1) * 128, :])

            def p2_front(sc):
                buf = sc % 2
                if sc == 0:
                    p2_se_load(0)
                p2_se_load(sc + 1)
                P.waitm("tensor", se_mk[sc])
                if sc >= 1:
                    P.wait("tensor", "v", sfold_m[sc - 1])
                if sc >= 2:
                    P.wait("tensor", "v", sfm_m[sc - 2])
                mk = None
                for j in range(DJ):
                    mk = P.run("tensor", lambda e, buf=buf, j=j: e.transpose(
                        pSe[:, j * 128:(j + 1) * 128],
                        se_tm[buf][:, j * 128:(j + 1) * 128], ident_bf[:]),
                        "t" if j == DJ - 1 else None)
                sev_m[sc] = mk
                P.wait("vector", "t", mk)
                if sc >= 2:
                    P.wait("vector", "t", lmk_hist[sc - 2])
                sfold_m[sc] = P.runf("vector", lambda e, buf=buf: e.tensor_tensor(
                    se_fs[buf][:, 0:1024], pSe[:, 0:1024],
                    gpre[:, 0:1024], ALU.mult), "v")
                P.wait("tensor", "v", sfold_m[sc])
                if sc >= 1:
                    P.wait("tensor", "s", lev_l[sc - 1])
                lmk = None
                for j in range(DJ):
                    for nb in range(4):
                        lmk = P.run("tensor", lambda e, buf=buf, j=j, nb=nb:
                                    e.matmul(
                            pL[nb][:, 0:512],
                            se_fs[buf][:, j * 128:(j + 1) * 128],
                            xn_fm[:, j * N + nb * 512:j * N + (nb + 1) * 512],
                            start=(j == 0), stop=(j == DJ - 1)),
                            "t" if (j == DJ - 1 and nb == 3) else None)
                lmk_hist[sc] = lmk
                p = 4 * (sc % 2)
                P.wait("vector", "t", lmk)
                for nb in range(4):
                    P.runf("vector", lambda e, nb=nb, p=p: e.reduce_max(
                        colA[:, 8 + p + nb:9 + p + nb], pL[nb][:, 0:512], AX.X))
                P.run("vector", lambda e, p=p: e.reduce_max(
                    colC[:, 8 + p:9 + p], colA[:, 8 + p:12 + p], AX.X))
                m_bias = P.run("vector", lambda e, sc=sc, p=p: e.tensor_scalar(
                    colC[:, 9 + p:10 + p], colC[:, 8 + p:9 + p],
                    rscale_all[:, sc:sc + 1], -1.0, ALU.mult, ALU.mult), "v")
                P.wait("scalar", "v", m_bias)
                if sc >= 2:
                    for mkk in lsp_l[sc - 2]:
                        P.waitm("scalar", mkk)
                    P.wait("scalar", "t", atr_hist[sc - 2])
                aemk = None
                for nb in range(4):
                    aemk = P.runf("scalar", lambda e, buf=buf, nb=nb, sc=sc, p=p:
                                  e.activation(
                        a_ch[buf][:, nb * 512:(nb + 1) * 512], pL[nb][:, 0:512],
                        AF.Exp, bias=colC[:, 9 + p:10 + p],
                        scale=rscale_all[:, sc:sc + 1],
                        accum_out=colB[:, 8 + p + nb:9 + p + nb]), "s")
                aemk_l[sc] = aemk
                lev = None
                for nb in range(4):
                    lev = P.runf("scalar", lambda e, buf=buf, nb=nb, sc=sc:
                                 e.activation(
                        l_ch[buf][:, nb * 512:(nb + 1) * 512], pL[nb][:, 0:512],
                        AF.Copy, scale=rscale_all[:, sc:sc + 1]),
                        "s" if nb == 3 else None)
                lev_l[sc] = lev
                P.wait("vector", "s", aemk)
                P.runf("vector", lambda e, p=p: e.reduce_sum(
                    colC[:, 10 + p:11 + p], colB[:, 8 + p:12 + p], AX.X))
                rdcol[sc] = P.run("vector", lambda e, p=p: e.reciprocal(
                    colC[:, 10 + p:11 + p], colC[:, 10 + p:11 + p]), "v")
                P.wait("gpsimd", "s", lev)
                lsp_l[sc] = [P.dma("g", dr["L_sp"][nb8, sc * 128:(sc + 1) * 128, :],
                                   l_ch[buf][:, nb8 * 256:(nb8 + 1) * 256])
                             for nb8 in range(NBL)]

            def p2_back(sc):
                buf = sc % 2
                p = 4 * (sc % 2)
                P.wait("tensor", "s", aemk_l[sc])
                if sc >= 1:
                    P.wait("tensor", "v", disv[sc - 1])
                atev = None
                for pk in range(2):
                    mk = None
                    if pk == 1:
                        P.wait("tensor", "v", atev)
                    elif sc >= 1:
                        P.wait("tensor", "v", atev_prev[0])
                    for t8 in range(8):
                        nt = pk * 8 + t8
                        mk = P.run("tensor", lambda e, buf=buf, nt=nt, t8=t8:
                                   e.transpose(
                            pAt[:, t8 * 128:(t8 + 1) * 128],
                            a_ch[buf][:, nt * 128:(nt + 1) * 128], ident_bf[:]),
                            "t" if t8 == 7 else None)
                    if pk == 1:
                        atr_hist[sc] = mk
                    P.wait("vector", "t", mk)
                    if sc >= 2:
                        P.wait("vector", "t", dis_m[sc - 2])
                    atev = P.runf("vector", lambda e, buf=buf, pk=pk: e.tensor_copy(
                        a_tm[buf][:, pk * 1024:(pk + 1) * 1024], pAt[:, 0:1024]),
                        "v")
                atev_prev[0] = atev
                P.wait("tensor", "v", atev)
                dmk = None
                for nt in range(NT):
                    for half in range(2):
                        dmk = P.run("tensor", lambda e, buf=buf, nt=nt, half=half:
                                    e.matmul(
                            pDs[half][:, 0:512],
                            a_tm[buf][:, nt * 128:(nt + 1) * 128],
                            xn_tm[:, nt * D + half * 512:nt * D + half * 512 + 512],
                            start=(nt == 0), stop=(nt == NT - 1)),
                            "t" if (nt == NT - 1 and half == 1) else None)
                dis_m[sc] = dmk
                P.wait("vector", "t", dmk)
                if sc >= 2:
                    P.wait("vector", "t", stm_hist[sc - 2])
                P.waitm("vector", ("v", rdcol[sc]))
                for half in range(2):
                    P.runf("vector", lambda e, buf=buf, half=half, p=p:
                           e.tensor_scalar(
                        sl_sm[buf][:, half * 512:(half + 1) * 512],
                        pDs[half][:, 0:512], colC[:, 10 + p:11 + p], None,
                        ALU.mult))
                dis_l = P.runf("vector", lambda e: e.memset(colD[:, 3:4], 0.0), "v")
                disv[sc] = dis_l
                P.wait("tensor", "v", dis_l)
                if sc + 1 < SC:
                    P.wait("tensor", "v", sfold_m[sc + 1])
                elif sc >= 1:
                    P.wait("tensor", "v", sfm_m[sc - 1])
                mk = None
                for j in range(DJ):
                    mk = P.run("tensor", lambda e, buf=buf, j=j: e.transpose(
                        pSe[:, j * 128:(j + 1) * 128],
                        sl_sm[buf][:, j * 128:(j + 1) * 128], ident_bf[:]),
                        "t" if j == DJ - 1 else None)
                P.wait("vector", "t", mk)
                sfm_last = None
                for jj in range(DJ):
                    sfm_last = P.runf("vector", lambda e, jj=jj, sc=sc: e.tensor_copy(
                        slots_fm[:, jj * S + sc * 128:jj * S + sc * 128 + 128],
                        pSe[:, jj * 128:(jj + 1) * 128]),
                        "v" if jj == DJ - 1 else None)
                stm_hist[sc] = mk
                sfm_m[sc] = sfm_last

            atev_prev = [None]
            for sc in range(SC):
                p2_front(sc)
                if sc >= 1:
                    p2_back(sc - 1)
            p2_back(SC - 1)

            P.emit("p2")
            if debug:
                P.dma("o", dbg["slots_fm"][:], slots_fm[:])
                P.qdrain("sync", "o")
                P.emit("p2dbg")
        xn_fm_cm.__exit__(None, None, None)
        xn_tm_cm.__exit__(None, None, None)
        if stop_after == "p2":
            es_sl.close()
            return nc, dr

        with ExitStack() as es3:
            P.phase_fence()
            w1_bf = sb("w1_bf", (128, DJ * HID), BF16, es3)
            h_buf = [sb("hbuf%d" % i, (128, N), BF16, es3) for i in range(2)]
            pH = [ps("pH%d" % i, (128, 512), F32, es3) for i in range(8)]

            w1_mk = [P.dma("g", w1_bf[:, j * HID:(j + 1) * HID],
                           dr["w1"][j * 128:(j + 1) * 128, :]) for j in range(DJ)]
            w1f_mk = []
            for j in range(DJ):
                P.waitm("vector", w1_mk[j])
                w1f_mk.append(P.runf("vector", lambda e, j=j: e.tensor_scalar(
                    w1_bf[:, j * HID:(j + 1) * HID], w1_bf[:, j * HID:(j + 1) * HID],
                    gx_col[:, j:j + 1], None, ALU.mult), "v"))

            hsp_m = []
            for hf in range(2):
                for hc in range(HC):
                    pbase = 4 * (hc % 2)
                    if len(hsp_m) >= 2:
                        for mkk in hsp_m[-2]:
                            P.waitm("tensor", mkk)
                    gmk = None
                    for j in range(DJ):
                        P.wait("tensor", "v", w1f_mk[j])
                        for q in range(4):
                            gmk = P.run("tensor",
                                        lambda e, j=j, q=q, hc=hc, hf=hf, pbase=pbase:
                                        e.matmul(
                                pH[pbase + q][:, 0:512],
                                w1_bf[:, j * HID + hc * 128:j * HID + (hc + 1) * 128],
                                slots_fm[:, j * S + hf * 2048 + q * 512:
                                         j * S + hf * 2048 + (q + 1) * 512],
                                start=(j == 0), stop=(j == DJ - 1)),
                                "t" if (j == DJ - 1 and q == 3) else None)
                    P.wait("scalar", "t", gmk)
                    gel = None
                    for q in range(4):
                        gel = P.runf("scalar", lambda e, q=q, hc=hc, pbase=pbase:
                                     e.activation(
                            h_buf[hc % 2][:, q * 512:(q + 1) * 512],
                            pH[pbase + q][:, 0:512], AF.Gelu),
                            "s" if q == 3 else None)
                    P.wait("gpsimd", "s", gel)
                    hsp = []
                    for q8 in range(8):
                        blk8 = hf * 8 + q8
                        hsp.append(P.dma("g",
                                         dr["h_sp"][blk8, hc * 128:(hc + 1) * 128, :],
                                         h_buf[hc % 2][:, q8 * 256:(q8 + 1) * 256]))
                    hsp_m.append(hsp)
            P.emit("p3")
        es_sl.close()
        if stop_after == "p3":
            return nc, dr

        es_pr = ExitStack()
        proc_sm = sb("proc_sm", (128, SC * D), BF16, es_pr)
        with ExitStack() as es4:
            P.phase_fence()
            w2_bf = sb("w2_bf", (128, HC * D), BF16, es4)
            h_blk = [sb("hblk%d" % i, (128, HC * HBW), BF16, es4) for i in range(2)]
            gb_rep = sb("gb_rep", (128, D), F32, es4)
            row_c = sb("row_c", (1, D), F32, es4)
            pP = [ps("pP%d" % i, (128, 512), F32, es4) for i in range(4)]
            pQ = [ps("pQ%d" % i, (128, 512), F32, es4) for i in range(2)]

            mk = None
            for j in range(DJ):
                mk = P.run("tensor", lambda e, j=j: e.matmul(
                    pQ[j // 4][0:1, (j % 4) * 128:(j % 4) * 128 + 128],
                    gb_col[:, j:j + 1],
                    ident_f32[:], start=True, stop=True),
                    "t" if j == DJ - 1 else None)
            P.wait("vector", "t", mk)
            P.run("vector", lambda e: e.tensor_copy(
                row_c[:, 0:512], pQ[0][0:1, 0:512]))
            m_rowc = P.run("vector", lambda e: e.tensor_copy(
                row_c[:, 512:1024], pQ[1][0:1, 0:512]), "v")
            P.wait("tensor", "v", m_rowc)
            for hh in range(2):
                mk = P.run("tensor", lambda e, hh=hh: e.matmul(
                    pQ[1][:, 0:512], ones_f32[0:1, 0:128],
                    row_c[0:1, hh * 512:(hh + 1) * 512], start=True, stop=True), "t")
                P.wait("vector", "t", mk)
                mk2 = P.run("vector", lambda e, hh=hh: e.tensor_copy(
                    gb_rep[:, hh * 512:(hh + 1) * 512], pQ[1][:, 0:512]), "v")
                P.wait("tensor", "v", mk2)
            w2_mk = [P.dma("g", w2_bf[:, hc * D:(hc + 1) * D],
                           dr["w2"][hc * 128:(hc + 1) * 128, :]) for hc in range(HC)]
            w2f_mk = []
            for hc in range(HC):
                P.waitm("vector", w2_mk[hc])
                w2f_mk.append(P.run("vector", lambda e, hc=hc: e.tensor_tensor(
                    w2_bf[:, hc * D:(hc + 1) * D], w2_bf[:, hc * D:(hc + 1) * D],
                    gb_rep[:], ALU.mult), "v", chain=(hc == 0)))

            pev = []
            pmk_hist = []
            for blk in range(HBL):
                if blk >= 2:
                    P.wait("sync", "t", pmk_hist[blk - 2])
                hb_mk = P.dma("y", h_blk[blk % 2][:].rearrange(
                    "p (c w) -> p c w", w=HBW),
                    dr["h_sp"][blk].rearrange("(c p) w -> p c w", p=128))
                for st in range(2):
                    P.waitm("tensor", hb_mk)
                    if len(pev) >= 4:
                        P.wait("tensor", "v", pev[-3])
                    pmk = None
                    for hc in range(HC):
                        P.wait("tensor", "v", w2f_mk[hc])
                        for half in range(2):
                            pmk = P.run("tensor",
                                        lambda e, hc=hc, st=st, half=half, blk=blk:
                                        e.matmul(
                                pP[2 * (st % 2) + half][:, 0:512],
                                h_blk[blk % 2][:, hc * HBW + st * 128:
                                               hc * HBW + (st + 1) * 128],
                                w2_bf[:, hc * D + half * 512:hc * D + half * 512 + 512],
                                start=(hc == 0), stop=(hc == HC - 1)),
                                "t" if (hc == HC - 1 and half == 1) else None)
                    if st == 1:
                        pmk_hist.append(pmk)
                    P.wait("vector", "t", pmk)
                    stt = blk * 2 + st
                    for half in range(2):
                        pev.append(P.runf("vector",
                                          lambda e, stt=stt, st=st, half=half:
                                          e.tensor_copy(
                            proc_sm[:, stt * D + half * 512:stt * D + half * 512 + 512],
                            pP[2 * (st % 2) + half][:, 0:512]), "v"))
            P.emit("p4")
            if debug:
                P.dma("o", dbg["proc_sm"][:], proc_sm[:])
                P.qdrain("sync", "o")
                P.emit("p4dbg")
        if stop_after == "p4":
            es_pr.close()
            return nc, dr

        with ExitStack() as es5:
            P.phase_fence()
            l_blk = [sb("lblk%d" % i, (128, SC * NBW), F32, es5) for i in range(2)]
            l_tmb = sb("l_tmb", (128, S), F32, es5)
            w_tmb = sb("w_tmb", (128, S), BF16, es5)
            w_smb = sb("w_smb", (128, S), BF16, es5)
            x2_rs = [sb("x2rs%d" % i, (128, D), F32, es5) for i in range(2)]
            o_buf = [sb("obuf%d" % i, (128, D), F32, es5) for i in range(2)]
            pT = [ps("pT%d" % i, (128, 512), F32, es5) for i in range(4)]
            pW = [ps("pW%d" % i, (128, 1024), BF16, es5) for i in range(2)]
            pO = [ps("pO%d" % i, (128, 512), F32, es5) for i in range(2)]

            lb_mk = [None] * NBL
            x2r_mk = [None] * NT
            oev_tile = [None] * NT
            ost_mk = [None] * NT
            ltile_done = [None] * NT
            st5 = {"lt_ev": None, "mneg": [None] * NT, "mw": [None] * NT,
                   "wtrl": [None] * NT, "wtrv": [None] * NT,
                   "lt_bank": [None] * 4, "lt_tmk": [None] * 4}

            def p5_lstage(nt):
                nb, ti = nt // 2, nt % 2
                p = 4 * (nt % 2)
                def lb_load(nbx):
                    if nbx >= NBL or lb_mk[nbx] is not None:
                        return
                    if nbx >= 2:
                        P.wait("sync", "t", ltile_done[(nbx - 2) * 2 + 1])
                    lb_mk[nbx] = P.dma("y", l_blk[nbx % 2][:].rearrange(
                        "p (c w) -> p c w", w=NBW),
                        dr["L_sp"][nbx].rearrange("(c p) w -> p c w", p=128))
                if ti == 0:
                    lb_load(nb)
                    lb_load(nb + 1)
                if nt >= 2:
                    P.wait("sync", "v", oev_tile[nt - 2])
                x2r_mk[nt] = P.dma("y", x2_rs[nt % 2][:],
                                   dr["x2_sp"][nt * 128:(nt + 1) * 128, :])
                P.waitm("tensor", lb_mk[nb])
                for stg in range(2):
                    mk = None
                    for t16 in range(16):
                        sc = stg * 16 + t16
                        if t16 % 4 == 0:
                            P.wait("tensor", "s", st5["lt_bank"][t16 // 4])
                        mk = P.run("tensor",
                                   lambda e, nb=nb, ti=ti, sc=sc, t16=t16:
                                   e.transpose(
                            pT[t16 // 4][:, (t16 % 4) * 128:(t16 % 4) * 128 + 128],
                            l_blk[nb % 2][:, sc * NBW + ti * 128:
                                          sc * NBW + (ti + 1) * 128],
                            ident_f32[:]), "t" if (t16 % 4) == 3 else None)
                        if t16 % 4 == 3:
                            st5["lt_tmk"][t16 // 4] = mk
                    if stg == 1:
                        ltile_done[nt] = mk
                    if stg == 0 and nt >= 1:
                        P.wait("scalar", "v", st5["mneg"][nt - 1])
                    for q in range(4):
                        P.wait("scalar", "t", st5["lt_tmk"][q])
                        st5["lt_bank"][q] = P.runf("scalar",
                                                   lambda e, stg=stg, q=q:
                                                   e.activation(
                            l_tmb[:, stg * 2048 + q * 512:
                                  stg * 2048 + (q + 1) * 512],
                            pT[q][:, 0:512], AF.Copy), "s")
                st5["lt_ev"] = st5["lt_bank"][3]
                P.wait("vector", "s", st5["lt_ev"])
                P.runf("vector", lambda e, p=p: e.reduce_max(
                    colA[:, p:p + 1], l_tmb[:], AX.X))
                st5["mneg"][nt] = P.run("vector", lambda e, p=p: e.tensor_scalar(
                    colA[:, p + 1:p + 2], colA[:, p:p + 1], -1.0, None,
                    ALU.mult), "v")
                P.wait("scalar", "v", st5["mneg"][nt])
                if nt >= 1:
                    P.wait("scalar", "t", st5["wtrl"][nt - 1])
                st5["mw"][nt] = P.run("scalar", lambda e, p=p: e.activation(
                    w_tmb[:], l_tmb[:], AF.Exp, bias=colA[:, p + 1:p + 2],
                    accum_out=colA[:, p + 2:p + 3]), "s")
                P.wait("vector", "s", st5["mw"][nt])
                P.runf("vector", lambda e, p=p: e.reciprocal(
                    colA[:, p + 3:p + 4], colA[:, p + 2:p + 3]), "v")

            def p5_wt(nt):
                P.wait("tensor", "s", st5["mw"][nt])
                wtr_ev = st5["wtrv"][nt - 1] if nt >= 1 else None
                for pk in range(4):
                    mk = None
                    if pk >= 2 or nt >= 1:
                        P.wait("tensor", "v", wtr_ev)
                    for t8 in range(8):
                        sc = pk * 8 + t8
                        mk = P.run("tensor", lambda e, sc=sc, t8=t8, pk=pk:
                                   e.transpose(
                            pW[pk % 2][:, t8 * 128:(t8 + 1) * 128],
                            w_tmb[:, sc * 128:(sc + 1) * 128], ident_bf[:]),
                            "t" if t8 == 7 else None)
                    P.wait("vector", "t", mk)
                    wtr_ev = P.runf("vector", lambda e, pk=pk: e.tensor_copy(
                        w_smb[:, pk * 1024:(pk + 1) * 1024],
                        pW[pk % 2][:, 0:1024]), "v")
                st5["wtrl"][nt] = mk
                st5["wtrv"][nt] = wtr_ev

            def p5_combine(nt):
                p = 4 * (nt % 2)
                P.wait("tensor", "v", st5["wtrv"][nt])
                if nt >= 1:
                    P.wait("tensor", "v", oev_tile[nt - 1])
                cmk = None
                for sc in range(SC):
                    for half in range(2):
                        cmk = P.run("tensor", lambda e, sc=sc, half=half:
                                    e.matmul(
                            pO[half][:, 0:512],
                            w_smb[:, sc * 128:(sc + 1) * 128],
                            proc_sm[:, sc * D + half * 512:
                                    sc * D + half * 512 + 512],
                            start=(sc == 0), stop=(sc == SC - 1)),
                            "t" if (sc == SC - 1 and half == 1) else None)
                P.wait("vector", "t", cmk)
                P.waitm("vector", x2r_mk[nt])
                if nt >= 2:
                    P.waitm("vector", ost_mk[nt - 2])
                ol = None
                for half in range(2):
                    ol = P.runf("vector",
                                lambda e, nt=nt, half=half, p=p:
                                e.scalar_tensor_tensor(
                        o_buf[nt % 2][:, half * 512:(half + 1) * 512],
                        pO[half][:, 0:512], colA[:, p + 3:p + 4],
                        x2_rs[nt % 2][:, half * 512:(half + 1) * 512],
                        ALU.mult, ALU.add), "v")
                oev_tile[nt] = ol
                P.wait("sync", "v", ol)
                ost_mk[nt] = P.dma("o", dr["out"][nt * 128:(nt + 1) * 128, :],
                                   o_buf[nt % 2][:])

            for nt in range(NT):
                p5_lstage(nt)
                if nt >= 1:
                    p5_combine(nt - 1)
                p5_wt(nt)
            p5_combine(NT - 1)
            P.qdrain("sync", "o")
            P.emit("p5")
        es_pr.close()

    return nc, dr


_CACHE = {}


def _consts():
    ident = np.eye(128)
    ones = np.ones((128, 128))
    jl = np.arange(128)[:, None]
    nl = np.arange(128)[None, :]
    ma = ((jl - nl >= 0) & (jl - nl <= 6)).astype(np.float32)
    jb = np.arange(6)[:, None]
    mb = ((125 + jb - nl >= -3) & (125 + jb - nl <= 3)).astype(np.float32)
    return {
        "ident_bf": ident.astype(BF), "ident_f32": ident.astype(np.float32),
        "ones_bf": ones.astype(BF), "ones_f32": ones.astype(np.float32),
        "mask_a": np.tile(ma, (1, 4)).astype(np.float32),
        "mask_b": np.tile(mb, (1, 4)).astype(np.float32),
    }


def make_in_maps(inputs):
    cst = _consts()
    x = np.ascontiguousarray(np.asarray(inputs["L0_attn_x"], np.float32))
    c = np.asarray(inputs["L0_c"], np.float32)
    base = {
        "w_qkv": np.ascontiguousarray(inputs["L0_attn_w_qkv"], dtype=np.float32),
        "proj": np.ascontiguousarray(inputs["L0_attn_proj"], dtype=np.float32),
        "w_out": np.ascontiguousarray(inputs["L0_attn_w_out"], dtype=np.float32),
        "b_out": np.ascontiguousarray(
            np.asarray(inputs["L0_attn_b_out"], np.float32).reshape(1, D)),
        "gx_col": np.ascontiguousarray(
            np.asarray(inputs["L0_ffn_gamma_x"], np.float32).reshape(DJ, 128).T),
        "gs_col": np.ascontiguousarray(
            np.asarray(inputs["L0_ffn_gamma_slot"], np.float32).reshape(DJ, 128).T),
        "slot_embeds": np.ascontiguousarray(
            np.asarray(inputs["L0_ffn_slot_embeds"], np.float32).reshape(S, D)),
        "w1": np.ascontiguousarray(inputs["L0_ffn_w1"], dtype=np.float32),
        "w2": np.ascontiguousarray(inputs["L0_ffn_w2"], dtype=np.float32),
        "wmod": np.ascontiguousarray(inputs["L0_wmod"], dtype=np.float32),
    }
    base.update(cst)
    maps = []
    for b in range(8):
        m = dict(base)
        m["x"] = np.ascontiguousarray(x[b])
        m["c_fm"] = np.ascontiguousarray(c[b].reshape(DJ, 128).T)
        maps.append(m)
    return maps


def kernel(**inputs):
    if "nc" not in _CACHE:
        nc, dr = build()
        _CACHE["nc"] = nc
    nc = _CACHE["nc"]
    maps = make_in_maps(inputs)
    res = run_bass_kernel_spmd(nc, maps, core_ids=list(range(8)))
    out = np.stack([np.asarray(r["out"], dtype=np.float32) for r in res.results])
    return out


# version 09
# speedup vs baseline: 1.3573x; 1.0787x over previous
import numpy as np
import ml_dtypes
from contextlib import ExitStack

import concourse.bass as bass
import concourse.mybir as mybir
from concourse.bass_utils import run_bass_kernel_spmd

F32 = mybir.dt.float32
BF16 = mybir.dt.bfloat16
AF = mybir.ActivationFunctionType
ALU = mybir.AluOpType
AX = mybir.AxisListType

N = 2048
D = 1024
NT = N // 128
DJ = D // 128
NP = N + 3
H = 8
HD = 128
FT = 64
S = 4096
SC = S // 128
HID = 4096
HC = HID // 128
NBL = 8
NBW = 256
HBL = 16
HBW = 256

BF = ml_dtypes.bfloat16


class Prog:
    ENGS = ("sync", "gpsimd", "tensor", "vector", "scalar")
    QDEPTH = {"y": 10, "g": 10, "o": 4}
    QENG = {"y": "sync", "g": "gpsimd", "o": "sync"}

    def __init__(self, nc, stack):
        self.nc = nc
        self.ops = {e: [] for e in self.ENGS}
        self.sems = {}
        self.counts = {}
        self.waited = {e: {} for e in self.ENGS}
        for name in ("t", "v", "s"):
            self.sems[name] = stack.enter_context(nc.semaphore("sem_" + name))
            self.counts[name] = 0
        self.qidx = {}
        for q, depth in self.QDEPTH.items():
            for i in range(depth):
                key = q + str(i)
                self.sems[key] = stack.enter_context(nc.semaphore("sem_" + key))
                self.counts[key] = 0
            self.qidx[q] = 0

    def wait(self, eng, sem, val):
        if val is None or val <= 0:
            return
        w = self.waited[eng]
        if w.get(sem, -1) >= val:
            return
        w[sem] = val
        h = self.sems[sem]
        self.ops[eng].append(lambda e: e.wait_ge(h, val))

    def waitm(self, eng, mark):
        if mark is None:
            return
        self.wait(eng, mark[0], mark[1])

    def run(self, eng, fn, sem=None, k=1, chain=True):
        if chain and eng in ("vector", "scalar"):
            self.ops[eng].append(lambda e: e.drain())
        if sem is None:
            self.ops[eng].append(fn)
            return None
        self.counts[sem] += k
        mark = self.counts[sem]
        h = self.sems[sem]
        self.ops[eng].append(lambda e: fn(e).then_inc(h, k))
        return mark

    def runf(self, eng, fn, sem=None, k=1):
        return self.run(eng, fn, sem, k, chain=False)

    def dma(self, q, dst, src):
        eng = self.QENG[q]
        i = self.qidx[q]
        self.qidx[q] += 1
        key = q + str(i % self.QDEPTH[q])
        prior = self.counts[key]
        if prior > 0:
            self.wait(eng, key, prior)
        self.counts[key] += 16
        val = self.counts[key]
        h = self.sems[key]
        self.ops[eng].append(lambda e: e.dma_start(dst, src).then_inc(h, 16))
        return (key, val)

    def qdrain(self, eng, q):
        for i in range(self.QDEPTH[q]):
            key = q + str(i)
            self.wait(eng, key, self.counts[key])

    def phase_fence(self):
        for eng in self.ENGS:
            for q in self.QDEPTH:
                self.qdrain(eng, q)
        for eng in ("sync", "gpsimd"):
            for s in ("t", "v", "s"):
                self.wait(eng, s, self.counts[s])

    def emit(self, name):
        with self.nc.Block(name) as blk:
            for eng in self.ENGS:
                ops = self.ops[eng]
                if not ops:
                    continue
                def body(e, ops=tuple(ops)):
                    for f in ops:
                        f(e)
                getattr(blk, eng)(body)
        self.ops = {e: [] for e in self.ENGS}


def build(stop_after=None, debug=False):
    nc = bass.Bass("TRN2", target_bir_lowering=False)

    dr = {}
    def din(name, shape, dtype=F32):
        dr[name] = nc.dram_tensor(name, list(shape), dtype, kind="ExternalInput").ap()
    def dout(name, shape, dtype=F32):
        dr[name] = nc.dram_tensor(name, list(shape), dtype, kind="ExternalOutput").ap()
    def dscr(name, shape, dtype=F32):
        kind = "ExternalOutput" if debug else "Internal"
        dr[name] = nc.dram_tensor(name, list(shape), dtype, kind=kind).ap()

    din("x", (N, D))
    din("w_qkv", (D, 3 * D))
    din("proj", (HD, FT))
    din("w_out", (D, D))
    din("b_out", (1, D))
    din("gx_col", (128, DJ))
    din("gs_col", (128, DJ))
    din("slot_embeds", (S, D))
    din("w1", (D, HID))
    din("w2", (HID, D))
    din("c_fm", (128, DJ))
    din("wmod", (D, 6 * D))
    din("ident_bf", (128, 128), BF16)
    din("ident_f32", (128, 128), F32)
    din("ones_bf", (128, 128), BF16)
    din("ones_f32", (128, 128), F32)
    din("mask_a", (128, 512))
    din("mask_b", (6, 512))
    dout("out", (N, D))
    dscr("L_sp", (NBL, S, NBW))
    dscr("h_sp", (HBL, HID, HBW), BF16)
    dscr("x2_sp", (N, D))

    dbg = {}
    if debug:
        def ddbg(name, shape, dtype=F32):
            dbg[name] = nc.dram_tensor("dbg_" + name, list(shape), dtype,
                                       kind="ExternalOutput").ap()
        ddbg("mod_row", (1, 6 * D))
        ddbg("h1p", (128, DJ * NP), BF16)
        ddbg("vtm", (128, (NT + 1) * D), BF16)
        ddbg("attn_cat", (128, H * N), BF16)
        ddbg("xn_tm", (128, NT * D), BF16)
        ddbg("xn_fm", (128, DJ * N), BF16)
        ddbg("rscale", (128, SC))
        ddbg("slots_fm", (128, DJ * S), BF16)
        ddbg("proc_sm", (128, SC * D), BF16)

    g = ExitStack()
    with g:
        P = Prog(nc, g)

        def sb(name, shape, dtype, stack=None, side=None):
            return (stack or g).enter_context(
                nc.sbuf_tensor(name, list(shape), dtype, side=side))

        def ps(name, shape, dtype, stack=None):
            return (stack or g).enter_context(
                nc.psum_tensor(name, list(shape), dtype))

        ident_bf = sb("ident_bf_s", (128, 128), BF16)
        ident_f32 = sb("ident_f32_s", (128, 128), F32)
        ones_bf = sb("ones_bf_s", (128, 128), BF16)
        ones_f32 = sb("ones_f32_s", (128, 128), F32)
        gx_col = sb("gx_col_s", (128, DJ), F32)
        gs_col = sb("gs_col_s", (128, DJ), F32)
        sa_col = sb("sa_col", (128, DJ), F32)
        oneca_col = sb("oneca_col", (128, DJ), F32)
        ga_col = sb("ga_col", (128, DJ), F32)
        sb_col = sb("sb_col", (128, DJ), F32)
        onecb_col = sb("onecb_col", (128, DJ), F32)
        gb_col = sb("gb_col", (128, DJ), F32)
        gprod_col = sb("gprod_col", (128, DJ), F32)
        rscale_all = sb("rscale_all", (128, SC), F32)
        ssq_se = sb("ssq_se", (128, SC), F32)
        colA = sb("colA", (128, 16), F32)
        colB = sb("colB", (128, 16), F32)
        colC = sb("colC", (128, 16), F32)
        colD = sb("colD", (128, 16), F32)
        eps_col = sb("eps_col", (128, 1), F32)

        h1_cm = nc.sbuf_tensor("h1p", [128, DJ * NP], BF16, side="right")
        h1p = h1_cm.__enter__()

        with ExitStack() as es:
            mod_row = sb("mod_row", (1, 6 * D), F32, es)
            rows_t = sb("rows_t", (1, D), F32, es)
            rows_u = sb("rows_u", (1, D), F32, es)
            c_fm = sb("c_fm_s", (128, DJ), F32, es)
            silu_bf = sb("silu_bf", (128, DJ), BF16, es)
            wsl = [sb("wsl%d" % i, (128, 8 * 512), BF16, es) for i in range(2)]
            x_st = [sb("xst%d" % i, (128, D), F32, es) for i in range(2)]
            sq_scr = sb("sq_scr", (128, D), F32, es)
            ln_bf = [sb("lnbf%d" % i, (128, D), BF16, es) for i in range(2)]
            p_mod = [ps("p_mod%d" % i, (1, 512), F32, es) for i in range(2)]
            p_tpb = [ps("p_tpb%d" % i, (128, 1024), BF16, es) for i in range(2)]
            p_rep = [ps("p_rep%d" % i, (128, 512), F32, es) for i in range(2)]

            for dst, src in [(ident_bf, "ident_bf"), (ident_f32, "ident_f32"),
                             (ones_bf, "ones_bf"), (ones_f32, "ones_f32"),
                             (gx_col, "gx_col"), (gs_col, "gs_col"),
                             (c_fm, "c_fm")]:
                P.dma("y", dst[:], dr[src][:])
            m_eps = P.run("vector", lambda e: e.memset(eps_col[:], 1e-5), "v")
            for j in range(DJ):
                P.run("vector", lambda e, j=j: e.memset(
                    h1p[:, j * NP:j * NP + 3], 0.0))
            P.qdrain("vector", "y")
            m_h1pad = P.run("vector", lambda e: e.tensor_tensor(
                gprod_col[:], gx_col[:], gs_col[:], ALU.mult), "v")

            P.qdrain("scalar", "y")
            m_silu = P.run("scalar", lambda e: e.activation(
                silu_bf[:], c_fm[:], AF.Silu), "s")

            mod_mm = []
            mod_ev = []
            for sl in range(12):
                pm = p_mod[sl % 2]
                wmk = []
                for j in range(DJ):
                    if sl >= 2:
                        P.wait("gpsimd", "t", mod_mm[sl - 2])
                    wmk.append(P.dma("g", wsl[sl % 2][:, j * 512:(j + 1) * 512],
                                     dr["wmod"][j * 128:(j + 1) * 128,
                                                sl * 512:(sl + 1) * 512]))
                if sl >= 2:
                    P.wait("tensor", "v", mod_ev[sl - 2])
                mk = None
                for j in range(DJ):
                    P.waitm("tensor", wmk[j])
                    if sl == 0 and j == 0:
                        P.wait("tensor", "s", m_silu)
                    mk = P.run("tensor", lambda e, pm=pm, j=j, sl=sl: e.matmul(
                        pm[:, 0:512], silu_bf[:, j:j + 1],
                        wsl[sl % 2][:, j * 512:(j + 1) * 512],
                        start=(j == 0), stop=(j == DJ - 1)),
                        "t" if j == DJ - 1 else None)
                mod_mm.append(mk)
                P.wait("vector", "t", mk)
                mod_ev.append(P.runf("vector", lambda e, pm=pm, sl=sl: e.tensor_copy(
                    mod_row[:, sl * 512:(sl + 1) * 512], pm[:, 0:512]), "v"))

            P.wait("vector", "v", mod_ev[-1])
            P.run("vector", lambda e: e.tensor_scalar(
                rows_t[:], mod_row[:, 1024:2048], 1.0, None, ALU.add))
            m_oneca = P.run("vector", lambda e: e.tensor_copy(
                mod_row[:, 1024:2048], rows_t[:]), "v")
            P.run("vector", lambda e: e.tensor_scalar(
                rows_t[:], mod_row[:, 4096:5120], 1.0, None, ALU.add))
            m_onecb = P.run("vector", lambda e: e.tensor_copy(
                mod_row[:, 4096:5120], rows_t[:]), "v")
            m_gv = m_onecb
            for off in (2048, 5120):
                P.wait("scalar", "v", m_gv)
                m_sgn = P.run("scalar", lambda e, o=off: e.activation(
                    rows_t[:], mod_row[:, o:o + D], AF.Sign), "s")
                m_abs = P.run("scalar", lambda e, o=off: e.activation(
                    rows_u[:], mod_row[:, o:o + D], AF.Abs), "s")
                P.wait("vector", "s", m_abs)
                P.run("vector", lambda e, o=off: e.tensor_tensor(
                    mod_row[:, o:o + D], rows_t[:], rows_t[:], ALU.mult))
                P.run("vector", lambda e, o=off: e.scalar_tensor_tensor(
                    rows_t[:], rows_t[:], 1.0, mod_row[:, o:o + D],
                    ALU.add, ALU.subtract))
                P.run("vector", lambda e: e.tensor_scalar(
                    rows_u[:], rows_u[:], 0.25, None, ALU.max))
                P.run("vector", lambda e, o=off: e.tensor_tensor(
                    rows_t[:], rows_t[:], rows_u[:], ALU.mult))
                m_gv = P.run("vector", lambda e, o=off: e.tensor_copy(
                    mod_row[:, o:o + D], rows_t[:]), "v")
            m_rows = m_gv
            P.wait("tensor", "v", m_rows)
            mk = None
            for k6 in range(6):
                for j in range(DJ):
                    mk = P.run("tensor", lambda e, k6=k6, j=j: e.matmul(
                        p_rep[k6 % 2][:, 8 * (k6 // 2) + j:8 * (k6 // 2) + j + 1],
                        mod_row[0:1, k6 * D + j * 128:k6 * D + (j + 1) * 128],
                        ones_f32[0:1, 0:1], start=True, stop=True),
                        "t" if j == DJ - 1 else None)
                if k6 % 2 == 1 or k6 == 5:
                    P.wait("vector", "t", mk)
            P.wait("vector", "t", mk)
            dsts = [sa_col, oneca_col, ga_col, sb_col, onecb_col, gb_col]
            m_cols = None
            for k6 in range(6):
                m_cols = P.run("vector", lambda e, k6=k6: e.tensor_copy(
                    dsts[k6][:],
                    p_rep[k6 % 2][:, 8 * (k6 // 2):8 * (k6 // 2) + 8]), "v")

            ssq_m, var_m, sqrt_m, ev_m = [], [], [], []
            lnh_m = [None] * NT
            x_mk = [None] * NT
            for i in range(NT):
                c = i % 16
                xs = x_st[i % 2]
                if i >= 2:
                    P.wait("sync", "v", lnh_m[i - 2])
                    P.wait("sync", "s", ssq_m[i - 2])
                x_mk[i] = P.dma("y", xs[:], dr["x"][i * 128:(i + 1) * 128, :])
                P.waitm("scalar", x_mk[i])
                if i >= 2:
                    P.wait("scalar", "v", var_m[i - 2])
                ssq_m.append(P.runf("scalar", lambda e, xs=xs, c=c: e.activation(
                    sq_scr[:], xs[:], AF.Square, accum_out=colA[:, c:c + 1]), "s"))
                P.waitm("vector", x_mk[i])
                P.runf("vector", lambda e, xs=xs, c=c: e.reduce_sum(
                    colC[:, c:c + 1], xs[:], AX.X))
                P.run("vector", lambda e, c=c: e.tensor_scalar(
                    colC[:, c:c + 1], colC[:, c:c + 1], 1.0 / D, None, ALU.mult))
                P.run("vector", lambda e, c=c: e.tensor_tensor(
                    colB[:, c:c + 1], colC[:, c:c + 1], colC[:, c:c + 1], ALU.mult))
                P.wait("vector", "s", ssq_m[i])
                var_m.append(P.run("vector", lambda e, c=c: e.scalar_tensor_tensor(
                    colB[:, c:c + 1], colA[:, c:c + 1], 1.0 / D, colB[:, c:c + 1],
                    ALU.mult, ALU.subtract), "v"))
                P.wait("scalar", "v", var_m[i])
                if i == 0:
                    P.wait("scalar", "v", m_eps)
                sqrt_m.append(P.runf("scalar", lambda e, c=c: e.activation(
                    colB[:, c:c + 1], colB[:, c:c + 1], AF.Sqrt,
                    bias=eps_col[:, 0:1]), "s"))
                P.wait("vector", "s", sqrt_m[i])
                P.run("vector", lambda e, c=c: e.reciprocal(
                    colB[:, c:c + 1], colB[:, c:c + 1]))
                lnh_m[i] = P.run("vector", lambda e, xs=xs, c=c, i=i: e.tensor_scalar(
                    ln_bf[i % 2][:], xs[:], colC[:, c:c + 1], colB[:, c:c + 1],
                    ALU.subtract, ALU.mult), "v")
                P.wait("tensor", "v", lnh_m[i])
                if ev_m:
                    P.wait("tensor", "v", ev_m[-1])
                pt = p_tpb[i % 2]
                mk = None
                for j in range(DJ):
                    mk = P.run("tensor", lambda e, pt=pt, j=j, i=i: e.transpose(
                        pt[:, j * 128:(j + 1) * 128],
                        ln_bf[i % 2][:, j * 128:(j + 1) * 128], ident_bf[:]),
                        "t" if j == DJ - 1 else None)
                P.wait("vector", "v", m_cols)
                P.wait("vector", "v", m_h1pad)
                P.wait("vector", "t", mk)
                lastev = None
                for j in range(DJ):
                    lastev = P.runf("vector", lambda e, pt=pt, j=j, i=i: e.tensor_scalar(
                        h1p[:, j * NP + 3 + i * 128: j * NP + 3 + i * 128 + 128],
                        pt[:, j * 128:(j + 1) * 128],
                        oneca_col[:, j:j + 1], sa_col[:, j:j + 1],
                        ALU.mult, ALU.add), "v" if j == DJ - 1 else None)
                ev_m.append(lastev)

            P.emit("p0")
            if debug:
                P.dma("o", dbg["mod_row"][:], mod_row[:])
                P.dma("o", dbg["h1p"][:], h1p[:])
                P.qdrain("sync", "o")
                P.emit("p0dbg")
        if stop_after == "p0":
            return nc, dr

        with ExitStack() as es1:
            attn_cat = sb("attn_cat", (128, H * N), BF16, es1)
            pp = [ps("pp%d" % i, (128, 512), F32, es1) for i in range(6)]
            pb = [ps("pb%d" % i, (128, 1024), BF16, es1) for i in range(2)]

            with ExitStack() as esh:
                mask_a = sb("mask_a_s", (128, 512), F32, esh)
                mask_b = sb("mask_b_s", (6, 512), F32, esh)
                wq_bf = [sb("wq%d" % i, (128, D), BF16, esh) for i in range(2)]
                wk_bf = [sb("wk%d" % i, (128, D), BF16, esh) for i in range(2)]
                proj_bf = sb("proj_bf", (128, FT), BF16, esh)
                v_tm = sb("v_tm", (128, (NT + 1) * D), BF16, esh)
                q_bf = sb("q_bf", (128, N), BF16, esh)
                k_bf = sb("k_bf", (128, N), BF16, esh)
                q_phi = [sb("q_phi%d" % i, (64, N), BF16, esh) for i in range(2)]
                k_pad = [sb("k_pad%d" % i, (64, N + 6), BF16, esh)
                         for i in range(2)]
                t_min_q = sb("t_min_q", (64, 512), F32, esh)
                t_min_k = sb("t_min_k", (64, 512), F32, esh)
                s_a = [sb("s_a%d" % i, (128, 512), BF16, esh) for i in range(2)]
                s_b = [sb("s_b%d" % i, (6, 512), BF16, esh) for i in range(2)]
                attn_tm = [sb("attn_tm%d" % i, (128, 512), BF16, esh)
                           for i in range(2)]
                se_st = sb("se_st", (128, D), F32, esh)
                sq2 = sb("sq2", (128, 2 * D), F32, esh)

                mka_mk = P.dma("y", mask_a[:], dr["mask_a"][:])
                mkb_mk = P.dma("y", mask_b[:], dr["mask_b"][:])
                m_proj = P.dma("g", proj_bf[:], dr["proj"][:])

                def qkv_block(dst, col0):
                    src = dr["w_qkv"][:, col0:col0 + 128].rearrange(
                        "(j p) c -> p j c", p=128)
                    d = dst[:].rearrange("p (j c) -> p j c", c=128)
                    return P.dma("g", d, src)

                wq_mk = [qkv_block(wq_bf[h], h * 128) for h in range(2)]
                wk_mk = [qkv_block(wk_bf[h], D + h * 128) for h in range(2)]

                with ExitStack() as esv:
                    wv_bf = sb("wv_bf", (128, DJ * D), BF16, esv)
                    wv_mk = [P.dma("g", wv_bf[:, j * D:(j + 1) * D],
                                   dr["w_qkv"][j * 128:(j + 1) * 128, 2 * D:3 * D])
                             for j in range(DJ)]
                    m_vz = P.run("vector", lambda e: e.memset(
                        v_tm[0:6, NT * D:NT * D + D], 0.0), "v")
                    vev = []
                    for i in range(NT + 1):
                        rows = 128 if i < NT else 3
                        for half in range(2):
                            pv = pp[2 * (i % 2) + half]
                            if len(vev) >= 4:
                                P.wait("tensor", "v", vev[-4])
                            mk = None
                            for j in range(DJ):
                                P.waitm("tensor", wv_mk[j])
                                if i < NT:
                                    st_ap = h1p[:, j * NP + i * 128:
                                                j * NP + i * 128 + 128]
                                else:
                                    st_ap = h1p[:, j * NP + N:j * NP + N + 3]
                                mk = P.run("tensor",
                                           lambda e, pv=pv, st_ap=st_ap, j=j,
                                           half=half, r=rows: e.matmul(
                                    pv[0:r, 0:512], st_ap,
                                    wv_bf[:, j * D + half * 512:j * D + half * 512 + 512],
                                    start=(j == 0), stop=(j == DJ - 1)),
                                    "t" if j == DJ - 1 else None)
                            P.wait("vector", "t", mk)
                            if i == NT:
                                P.wait("vector", "v", m_vz)
                            vev.append(P.runf("vector",
                                              lambda e, pv=pv, i=i, half=half, r=rows:
                                              e.tensor_copy(
                                v_tm[0:r, i * D + half * 512:i * D + half * 512 + 512],
                                pv[0:r, 0:512]), "v"))
                    m_vtm_done = vev[-1]
                    if debug:
                        P.wait("sync", "v", m_vtm_done)
                        P.dma("o", dbg["vtm"][:, 0:NT * D], v_tm[:, 0:NT * D])
                        P.dma("o", dbg["vtm"][0:6, NT * D:NT * D + D],
                              v_tm[0:6, NT * D:NT * D + D])

                zpad_done = [None, None]
                hst = [dict() for _ in range(H)]
                qk_use = [None] * H

                def qk_pass(h, p):
                    dstq = q_bf if p < 2 else k_bf
                    wsrc = wq_bf if p < 2 else wk_bf
                    mks = wq_mk if p < 2 else wk_mk
                    qh = p % 2
                    st = hst[h]
                    P.waitm("tensor", mks[h])
                    P.wait("tensor", "v", st.get("qk_ev"))
                    if p == 0:
                        if h >= 1:
                            P.wait("tensor", "v", hst[h - 1]["phi"])
                        else:
                            P.wait("tensor", "v", m_vtm_done)
                    mk = None
                    for j in range(DJ):
                        for b2 in range(2):
                            mk = P.run("tensor",
                                       lambda e, h=h, j=j, b2=b2, qh=qh, wsrc=wsrc:
                                       e.matmul(
                                pp[b2][:, 0:512],
                                wsrc[h % 2][:, j * 128:(j + 1) * 128],
                                h1p[:, j * NP + 3 + qh * 1024 + b2 * 512:
                                    j * NP + 3 + qh * 1024 + b2 * 512 + 512],
                                start=(j == 0), stop=(j == DJ - 1)),
                                "t" if (j == DJ - 1 and b2 == 1) else None)
                    if p == 3:
                        qk_use[h] = mk
                    P.wait("vector", "t", mk)
                    ev = None
                    for b2 in range(2):
                        ev = P.runf("vector", lambda e, b2=b2, qh=qh, dstq=dstq:
                                    e.tensor_copy(
                            dstq[:, qh * 1024 + b2 * 512:qh * 1024 + b2 * 512 + 512],
                            pp[b2][:, 0:512]), "v")
                    st["qk_ev"] = ev

                def proj_pass(h, b4):
                    st = hst[h]
                    hp = h % 2
                    P.waitm("tensor", m_proj)
                    P.wait("tensor", "v", st.get("phi_p", st["qk_ev"]))
                    P.run("tensor", lambda e, b4=b4: e.matmul(
                        pp[0][0:64, 0:512], proj_bf[:, 0:FT],
                        q_bf[:, b4 * 512:(b4 + 1) * 512], start=True, stop=True))
                    mkp = P.run("tensor", lambda e, b4=b4: e.matmul(
                        pp[1][0:64, 0:512], proj_bf[:, 0:FT],
                        k_bf[:, b4 * 512:(b4 + 1) * 512], start=True, stop=True), "t")
                    P.wait("vector", "t", mkp)
                    if h >= 2:
                        P.wait("vector", "t", hst[h - 2].get("wlast"))
                    P.runf("vector", lambda e, b4=b4: e.tensor_scalar(
                        t_min_q[:, 0:512], pp[0][0:64, 0:512], 0.0, None, ALU.min))
                    m_tmin = P.runf("vector", lambda e, b4=b4: e.tensor_scalar(
                        t_min_k[:, 0:512], pp[1][0:64, 0:512], 0.0, None,
                        ALU.min), "v")
                    P.wait("scalar", "v", m_tmin)
                    P.runf("scalar", lambda e: e.activation(
                        t_min_q[:, 0:512], t_min_q[:, 0:512], AF.Exp))
                    m_exp = P.runf("scalar", lambda e: e.activation(
                        t_min_k[:, 0:512], t_min_k[:, 0:512], AF.Exp), "s")
                    P.wait("vector", "s", m_exp)
                    P.runf("vector", lambda e, b4=b4, hp=hp: e.scalar_tensor_tensor(
                        q_phi[hp][:, b4 * 512:(b4 + 1) * 512], pp[0][0:64, 0:512],
                        0.0, t_min_q[:, 0:512], ALU.max, ALU.add))
                    phi = P.runf("vector", lambda e, b4=b4, hp=hp:
                                 e.scalar_tensor_tensor(
                        k_pad[hp][:, 3 + b4 * 512:3 + (b4 + 1) * 512],
                        pp[1][0:64, 0:512],
                        0.0, t_min_k[:, 0:512], ALU.max, ALU.add), "v")
                    st["phi_p"] = phi
                    if b4 == 3:
                        if zpad_done[hp] is None:
                            P.runf("vector", lambda e, hp=hp: e.memset(
                                k_pad[hp][:, 0:3], 0.0))
                            phi = P.runf("vector", lambda e, hp=hp: e.memset(
                                k_pad[hp][:, N + 3:N + 6], 0.0), "v")
                            zpad_done[hp] = phi
                        st["phi"] = phi

                def wA(h, grp):
                    st = hst[h]
                    hp = h % 2
                    P.wait("tensor", "v", st["phi"])
                    P.wait("tensor", "v", zpad_done[hp])
                    prev = st if grp >= 1 else (hst[h - 1] if h >= 1 else None)
                    if prev is not None and "sev" in prev:
                        P.wait("tensor", "v", prev["sev"])
                    smk = None
                    for t4 in range(4):
                        t = grp * 4 + t4
                        c0 = t4 * 128
                        P.run("tensor", lambda e, t=t, c0=c0, hp=hp: e.matmul(
                            pp[2][:, c0:c0 + 128],
                            k_pad[hp][:, t * 128:t * 128 + 128],
                            q_phi[hp][:, t * 128:(t + 1) * 128],
                            start=True, stop=True))
                        smk = P.run("tensor", lambda e, t=t, c0=c0, hp=hp: e.matmul(
                            pp[3][0:6, c0:c0 + 128],
                            k_pad[hp][:, t * 128 + 128:t * 128 + 134],
                            q_phi[hp][:, t * 128:(t + 1) * 128],
                            start=True, stop=True), "t")
                    P.wait("vector", "t", smk)
                    P.waitm("vector", mka_mk)
                    P.waitm("vector", mkb_mk)
                    P.runf("vector", lambda e, g=grp: e.tensor_tensor(
                        s_a[g % 2][:], pp[2][:, 0:512], mask_a[:], ALU.mult))
                    st["sev"] = P.runf("vector", lambda e, g=grp: e.tensor_tensor(
                        s_b[g % 2][:], pp[3][0:6, 0:512], mask_b[:],
                        ALU.mult), "v")

                def wB(h, grp):
                    st = hst[h]
                    prev = st if grp >= 1 else (hst[h - 1] if h >= 1 else None)
                    P.wait("tensor", "v", st["sev"])
                    if prev is not None and "zr" in prev:
                        P.wait("tensor", "v", prev["zr"])
                        P.wait("tensor", "s", prev["cpm"])
                    amk = None
                    for t4 in range(4):
                        t = grp * 4 + t4
                        c0 = t4 * 128
                        sa_ = s_a[grp % 2]
                        sb_ = s_b[grp % 2]
                        P.run("tensor", lambda e, t=t, c0=c0, sa_=sa_, h=h:
                              e.matmul(
                            pp[4][:, c0:c0 + 128], sa_[:, c0:c0 + 128],
                            v_tm[:, t * D + h * 128:t * D + (h + 1) * 128],
                            start=True, stop=False))
                        P.run("tensor", lambda e, t=t, c0=c0, sb_=sb_, h=h:
                              e.matmul(
                            pp[4][:, c0:c0 + 128], sb_[:, c0:c0 + 128],
                            v_tm[0:6, (t + 1) * D + h * 128:(t + 1) * D + (h + 1) * 128],
                            start=False, stop=True))
                        P.run("tensor", lambda e, t4=t4, c0=c0, sa_=sa_: e.matmul(
                            pp[5][:, t4:t4 + 1], sa_[:, c0:c0 + 128],
                            ones_bf[:, 0:1], start=True, stop=False))
                        amk = P.run("tensor", lambda e, t4=t4, c0=c0, sb_=sb_:
                                    e.matmul(
                            pp[5][:, t4:t4 + 1], sb_[:, c0:c0 + 128],
                            ones_bf[0:6, 0:1], start=False, stop=True), "t")
                    P.wait("vector", "t", amk)
                    P.runf("vector", lambda e: e.tensor_scalar(
                        colA[:, 8:12], pp[5][:, 0:4], 1e-8, None, ALU.add))
                    st["zr"] = P.run("vector", lambda e: e.reciprocal(
                        colA[:, 8:12], colA[:, 8:12]), "v")
                    P.wait("scalar", "v", st["zr"])
                    cpms = []
                    for t4 in range(4):
                        cpms.append(P.runf("scalar",
                                           lambda e, t4=t4, g=grp: e.activation(
                            attn_tm[g % 2][:, t4 * 128:(t4 + 1) * 128],
                            pp[4][:, t4 * 128:(t4 + 1) * 128], AF.Copy,
                            scale=colA[:, 8 + t4:9 + t4]), "s"))
                    st["cpm"] = cpms[-1]
                    st["cpms"] = cpms

                def wC(h, grp):
                    st = hst[h]
                    if "cat" in st:
                        P.wait("tensor", "v", st["cat"])
                    elif h >= 1:
                        P.wait("tensor", "v", hst[h - 1]["cat"])
                    trm = None
                    for t4 in range(4):
                        P.wait("tensor", "s", st["cpms"][t4])
                        trm = P.run("tensor", lambda e, t4=t4, g=grp: e.transpose(
                            pb[0][:, t4 * 128:(t4 + 1) * 128],
                            attn_tm[g % 2][:, t4 * 128:(t4 + 1) * 128],
                            ident_bf[:]), "t")
                    st["wlast"] = trm
                    P.wait("vector", "t", trm)
                    st["cat"] = P.runf("vector", lambda e, g=grp, h=h: e.tensor_copy(
                        attn_cat[:, h * N + g * 512:h * N + (g + 1) * 512],
                        pb[0][:, 0:512]), "v")

                for h in range(H):
                    hm = h - 1
                    qk_pass(h, 0)
                    if hm >= 1:
                        wB(hm - 1, 3)
                    qk_pass(h, 1)
                    if hm >= 1:
                        wC(hm - 1, 2)
                    qk_pass(h, 2)
                    if hm >= 1:
                        wC(hm - 1, 3)
                    qk_pass(h, 3)
                    if h + 2 < H:
                        P.waitm("gpsimd", ("t", qk_use[h]))
                        wq_mk.append(qkv_block(wq_bf[(h + 2) % 2], (h + 2) * 128))
                        wk_mk.append(qkv_block(wk_bf[(h + 2) % 2],
                                               D + (h + 2) * 128))
                    for p in range(4):
                        proj_pass(h, p)
                        if hm >= 0:
                            wA(hm, p)
                            if p >= 1:
                                wB(hm, p - 1)
                            if p >= 2:
                                wC(hm, p - 2)
                    for sc in range(4 * h, 4 * h + 4):
                        mse = P.dma("g", se_st[:],
                                    dr["slot_embeds"][sc * 128:(sc + 1) * 128, :])
                        P.waitm("scalar", mse)
                        m_ssq = P.runf("scalar", lambda e, sc=sc: e.activation(
                            sq2[:, (sc % 2) * 1024:(sc % 2) * 1024 + 1024],
                            se_st[:], AF.Square,
                            accum_out=ssq_se[:, sc:sc + 1]), "s")
                        P.wait("gpsimd", "s", m_ssq)
                wB(H - 2, 3)
                wC(H - 2, 2)
                wC(H - 2, 3)
                for grp in range(4):
                    wA(H - 1, grp)
                    wB(H - 1, grp)
                    wC(H - 1, grp)
                m_cpm_last = hst[H - 1]["cpm"]
                m_attn_done = P.counts["v"]
                P.emit("p1a")
                P.phase_fence()
            h1_cm.__exit__(None, None, None)

            xn_tm_cm = nc.sbuf_tensor("xn_tm", [128, NT * D], BF16, side="right")
            xn_fm_cm = nc.sbuf_tensor("xn_fm", [128, DJ * N], BF16, side="right")
            xn_tm = xn_tm_cm.__enter__()
            xn_fm = xn_fm_cm.__enter__()

            with ExitStack() as esl:
                w_out_bf = sb("w_out_bf", (128, DJ * D), BF16, esl)
                ga_rep = sb("ga_rep", (128, D), F32, esl)
                row_b = sb("row_b", (1, D), F32, esl)
                b_out_row = sb("b_out_row", (1, D), F32, esl)
                gab_bf = sb("gab_bf", (1, D), BF16, esl)
                onecb_rep = sb("onecb_rep", (128, D), F32, esl)
                sbr_rep = sb("sbr_rep", (128, D), F32, esl)
                x_rs = [sb("xrs%d" % i, (128, D), F32, esl) for i in range(2)]
                x2_b = [sb("x2b%d" % i, (128, D), F32, esl) for i in range(2)]
                h2_b = sb("h2_b", (128, D), F32, esl)
                sq2b = sb("sq2b", (128, D), F32, esl)

                m_bout = P.dma("y", b_out_row[:], dr["b_out"][:])
                wo_mk = [P.dma("g", w_out_bf[:, j * D:(j + 1) * D],
                               dr["w_out"][j * 128:(j + 1) * 128, :])
                         for j in range(DJ)]

                def col_to_row(col, pidx):
                    mk = None
                    for j in range(DJ):
                        mk = P.run("tensor", lambda e, j=j, col=col, p=pidx: e.matmul(
                            pp[p + j // 4][0:1, (j % 4) * 128:(j % 4) * 128 + 128],
                            col[:, j:j + 1], ident_f32[:],
                            start=True, stop=True), "t" if j == DJ - 1 else None)
                    return mk

                def row_evac(pidx):
                    P.run("vector", lambda e, p=pidx: e.tensor_copy(
                        row_b[:, 0:512], pp[p][0:1, 0:512]))
                    return P.run("vector", lambda e, p=pidx: e.tensor_copy(
                        row_b[:, 512:1024], pp[p + 1][0:1, 0:512]), "v")

                P.wait("tensor", "v", m_attn_done)
                P.wait("tensor", "s", m_cpm_last)
                mk = col_to_row(ga_col, 0)
                P.wait("vector", "t", mk)
                m_rowb = row_evac(0)
                P.waitm("vector", m_bout)
                m_gab = P.run("vector", lambda e: e.tensor_tensor(
                    gab_bf[:], row_b[:], b_out_row[:], ALU.mult), "v")
                P.wait("tensor", "v", m_rowb)
                for hh in range(2):
                    mk = P.run("tensor", lambda e, hh=hh: e.matmul(
                        pp[1][:, 0:512], ones_f32[0:1, 0:128],
                        row_b[0:1, hh * 512:(hh + 1) * 512],
                        start=True, stop=True), "t")
                    P.wait("vector", "t", mk)
                    mk2 = P.run("vector", lambda e, hh=hh: e.tensor_copy(
                        ga_rep[:, hh * 512:(hh + 1) * 512], pp[1][:, 0:512]), "v")
                    P.wait("tensor", "v", mk2)
                for col, dstr in ((onecb_col, onecb_rep), (sb_col, sbr_rep)):
                    mk = col_to_row(col, 2)
                    P.wait("vector", "t", mk)
                    m_rowb = row_evac(2)
                    P.wait("tensor", "v", m_rowb)
                    for hh in range(2):
                        mk = P.run("tensor", lambda e, hh=hh: e.matmul(
                            pp[1][:, 0:512], ones_f32[0:1, 0:128],
                            row_b[0:1, hh * 512:(hh + 1) * 512],
                            start=True, stop=True), "t")
                        P.wait("vector", "t", mk)
                        mk2 = P.run("vector", lambda e, d=dstr, hh=hh: e.tensor_copy(
                            d[:, hh * 512:(hh + 1) * 512], pp[1][:, 0:512]), "v")
                        P.wait("tensor", "v", mk2)
                fold_mk = []
                for j in range(DJ):
                    P.waitm("vector", wo_mk[j])
                    fold_mk.append(P.run("vector", lambda e, j=j: e.tensor_tensor(
                        w_out_bf[:, j * D:(j + 1) * D],
                        w_out_bf[:, j * D:(j + 1) * D], ga_rep[:], ALU.mult), "v"))
                m_gab_t = m_gab

                x2_mk, xn_ev, xrs_mk, x2sp_mk, msq_hist = [], [], [], [], []
                m_xn_l = []

                def emit_xnT(i):
                    P.wait("tensor", "v", m_xn_l[i])
                    if len(xn_ev) >= 2:
                        P.wait("tensor", "v", xn_ev[-2])
                    pt = pb[i % 2]
                    mk = None
                    for j in range(DJ):
                        mk = P.run("tensor", lambda e, pt=pt, i=i, j=j: e.transpose(
                            pt[:, j * 128:(j + 1) * 128],
                            xn_tm[:, i * D + j * 128:i * D + (j + 1) * 128],
                            ident_bf[:]), "t" if j == DJ - 1 else None)
                    P.wait("vector", "t", mk)
                    lastev = None
                    for j in range(DJ):
                        lastev = P.runf("vector", lambda e, pt=pt, i=i, j=j:
                                        e.tensor_copy(
                            xn_fm[:, j * N + i * 128:j * N + i * 128 + 128],
                            pt[:, j * 128:(j + 1) * 128]), "v")
                    xn_ev.append(lastev)

                for i in range(NT):
                    if i >= 2:
                        P.wait("sync", "v", x2_mk[i - 2])
                    xrs_mk.append(P.dma("y", x_rs[i % 2][:],
                                        dr["x"][i * 128:(i + 1) * 128, :]))
                    if i >= 2:
                        P.wait("tensor", "v", x2_mk[i - 2])
                    for j in range(DJ):
                        P.wait("tensor", "v", fold_mk[j])
                        for half in range(2):
                            P.run("tensor", lambda e, i=i, j=j, half=half: e.matmul(
                                pp[2 + 2 * (i % 2) + half][:, 0:512],
                                attn_cat[:, j * N + i * 128:j * N + i * 128 + 128],
                                w_out_bf[:, j * D + half * 512:j * D + half * 512 + 512],
                                start=(j == 0), stop=False))
                    wmk = None
                    P.wait("tensor", "v", m_gab_t)
                    for half in range(2):
                        wmk = P.run("tensor", lambda e, i=i, half=half: e.matmul(
                            pp[2 + 2 * (i % 2) + half][:, 0:512],
                            ones_bf[0:1, 0:128],
                            gab_bf[0:1, half * 512:(half + 1) * 512],
                            start=False, stop=True), "t")
                    if i >= 2:
                        emit_xnT(i - 2)
                    P.wait("vector", "t", wmk)
                    P.waitm("vector", xrs_mk[i])
                    if i >= 2:
                        P.waitm("vector", x2sp_mk[i - 2])
                        P.wait("vector", "s", msq_hist[i - 2])
                    for half in range(2):
                        P.runf("vector", lambda e, i=i, half=half: e.tensor_tensor(
                            x2_b[i % 2][:, half * 512:(half + 1) * 512],
                            pp[2 + 2 * (i % 2) + half][:, 0:512],
                            x_rs[i % 2][:, half * 512:(half + 1) * 512], ALU.add))
                    m_x2 = P.runf("vector", lambda e: e.memset(colD[:, 2:3], 0.0), "v")
                    x2_mk.append(m_x2)
                    P.wait("gpsimd", "v", m_x2)
                    x2sp_mk.append(P.dma("g", dr["x2_sp"][i * 128:(i + 1) * 128, :],
                                         x2_b[i % 2][:]))
                    c = i % 16
                    P.wait("scalar", "v", m_x2)
                    m_sq = P.runf("scalar", lambda e, i=i, c=c: e.activation(
                        sq2b[:], x2_b[i % 2][:], AF.Square,
                        accum_out=colA[:, c:c + 1]), "s")
                    msq_hist.append(m_sq)
                    P.run("vector", lambda e, i=i, c=c: e.reduce_sum(
                        colC[:, c:c + 1], x2_b[i % 2][:], AX.X))
                    P.run("vector", lambda e, c=c: e.tensor_scalar(
                        colC[:, c:c + 1], colC[:, c:c + 1], 1.0 / D, None, ALU.mult))
                    P.run("vector", lambda e, c=c: e.tensor_tensor(
                        colB[:, c:c + 1], colC[:, c:c + 1], colC[:, c:c + 1],
                        ALU.mult))
                    P.wait("vector", "s", m_sq)
                    m_var = P.run("vector", lambda e, c=c: e.scalar_tensor_tensor(
                        colB[:, c:c + 1], colA[:, c:c + 1], 1.0 / D, colB[:, c:c + 1],
                        ALU.mult, ALU.subtract), "v")
                    P.wait("scalar", "v", m_var)
                    m_sg = P.runf("scalar", lambda e, c=c: e.activation(
                        colB[:, c:c + 1], colB[:, c:c + 1], AF.Sqrt,
                        bias=eps_col[:, 0:1]), "s")
                    P.wait("vector", "s", m_sg)
                    P.run("vector", lambda e, c=c: e.reciprocal(
                        colB[:, c:c + 1], colB[:, c:c + 1]))
                    P.run("vector", lambda e, i=i, c=c: e.tensor_scalar(
                        h2_b[:], x2_b[i % 2][:], colC[:, c:c + 1], colB[:, c:c + 1],
                        ALU.subtract, ALU.mult))
                    P.run("vector", lambda e: e.tensor_tensor(
                        h2_b[:], h2_b[:], onecb_rep[:], ALU.mult))
                    m_h2 = P.run("vector", lambda e: e.tensor_tensor(
                        h2_b[:], h2_b[:], sbr_rep[:], ALU.add), "v")
                    P.wait("scalar", "v", m_h2)
                    m_sq2 = P.runf("scalar", lambda e, c=c: e.activation(
                        sq2b[:], h2_b[:], AF.Square, accum_out=colA[:, c:c + 1]), "s")
                    m_nrm = P.run("scalar", lambda e, c=c: e.activation(
                        colA[:, c:c + 1], colA[:, c:c + 1], AF.Sqrt), "s")
                    P.wait("vector", "s", m_nrm)
                    P.run("vector", lambda e, c=c: e.tensor_scalar(
                        colA[:, c:c + 1], colA[:, c:c + 1], 1e-12, None, ALU.max))
                    P.run("vector", lambda e, c=c: e.reciprocal(
                        colA[:, c:c + 1], colA[:, c:c + 1]))
                    m_xn = P.run("vector", lambda e, i=i, c=c: e.tensor_scalar(
                        xn_tm[:, i * D:(i + 1) * D], h2_b[:], colA[:, c:c + 1], 32.0,
                        ALU.mult, ALU.mult), "v")
                    m_xn_l.append(m_xn)

                emit_xnT(NT - 2)
                emit_xnT(NT - 1)
                m_rsq = P.run("scalar", lambda e: e.activation(
                    rscale_all[:], ssq_se[:], AF.Sqrt), "s")
                P.wait("vector", "s", m_rsq)
                P.run("vector", lambda e: e.tensor_scalar(
                    rscale_all[:], rscale_all[:], 1e-12, None, ALU.max))
                P.run("vector", lambda e: e.reciprocal(rscale_all[:], rscale_all[:]))
                m_rsc = P.run("vector", lambda e: e.tensor_scalar(
                    rscale_all[:], rscale_all[:], 32.0, None, ALU.mult), "v")

                P.emit("p1")
                if debug:
                    P.dma("o", dbg["attn_cat"][:], attn_cat[:])
                    P.dma("o", dbg["xn_tm"][:], xn_tm[:])
                    P.dma("o", dbg["xn_fm"][:], xn_fm[:])
                    P.dma("o", dbg["rscale"][:], rscale_all[:])
                    P.qdrain("sync", "o")
                    P.emit("p1dbg")
        if stop_after == "p1":
            return nc, dr

        es_sl = ExitStack()
        slots_fm = sb("slots_fm", (128, DJ * S), BF16, es_sl)
        with ExitStack() as es2:
            P.phase_fence()
            gpre = sb("gpre", (128, D), F32, es2)
            se_tm = [sb("setm%d" % i, (128, D), BF16, es2) for i in range(2)]
            se_fs = [sb("sefs%d" % i, (128, D), BF16, es2) for i in range(2)]
            a_ch = [sb("ach%d" % i, (128, N), BF16, es2) for i in range(2)]
            l_ch = [sb("lch%d" % i, (128, N), F32, es2) for i in range(2)]
            a_tm = [sb("atm%d" % i, (128, N), BF16, es2) for i in range(2)]
            sl_sm = [sb("slsm%d" % i, (128, D), BF16, es2) for i in range(2)]
            pL = [ps("pL%d" % i, (128, 512), F32, es2) for i in range(4)]
            pSe = ps("pSe", (128, 1024), BF16, es2)
            pAt = ps("pAt", (128, 1024), BF16, es2)
            pDs = [ps("pDs%d" % i, (128, 512), F32, es2) for i in range(2)]

            for j in range(DJ):
                P.run("vector", lambda e, j=j: e.tensor_scalar(
                    gpre[:, j * 128:(j + 1) * 128], ones_f32[:, 0:128],
                    gprod_col[:, j:j + 1], None, ALU.mult))
            m_gpre = P.run("vector", lambda e: e.memset(colD[:, 4:5], 0.0), "v")
            P.wait("vector", "v", m_gpre)

            sev_m = [None] * SC
            lmk_hist = [None] * SC
            aemk_l = [None] * SC
            lev_l = [None] * SC
            atr_hist = [None] * SC
            dis_m = [None] * SC
            disv = [None] * SC
            sfm_m = [None] * SC
            sfold_m = [None] * SC
            stm_hist = [None] * SC
            lsp_l = [None] * SC
            rdcol = [None] * SC

            se_mk = [None] * (SC + 1)

            def p2_se_load(sc):
                if sc >= SC:
                    return
                if sc >= 2:
                    P.wait("gpsimd", "t", sev_m[sc - 2])
                se_mk[sc] = P.dma("g", se_tm[sc % 2][:],
                                  dr["slot_embeds"][sc * 128:(sc + 1) * 128, :])

            def p2_front(sc):
                buf = sc % 2
                if sc == 0:
                    p2_se_load(0)
                p2_se_load(sc + 1)
                P.waitm("tensor", se_mk[sc])
                if sc >= 1:
                    P.wait("tensor", "v", sfold_m[sc - 1])
                if sc >= 2:
                    P.wait("tensor", "v", sfm_m[sc - 2])
                mk = None
                for j in range(DJ):
                    mk = P.run("tensor", lambda e, buf=buf, j=j: e.transpose(
                        pSe[:, j * 128:(j + 1) * 128],
                        se_tm[buf][:, j * 128:(j + 1) * 128], ident_bf[:]),
                        "t" if j == DJ - 1 else None)
                sev_m[sc] = mk
                P.wait("vector", "t", mk)
                if sc >= 2:
                    P.wait("vector", "t", lmk_hist[sc - 2])
                sfold_m[sc] = P.runf("vector", lambda e, buf=buf: e.tensor_tensor(
                    se_fs[buf][:, 0:1024], pSe[:, 0:1024],
                    gpre[:, 0:1024], ALU.mult), "v")
                P.wait("tensor", "v", sfold_m[sc])
                if sc >= 1:
                    P.wait("tensor", "s", lev_l[sc - 1])
                lmk = None
                for j in range(DJ):
                    for nb in range(4):
                        lmk = P.run("tensor", lambda e, buf=buf, j=j, nb=nb:
                                    e.matmul(
                            pL[nb][:, 0:512],
                            se_fs[buf][:, j * 128:(j + 1) * 128],
                            xn_fm[:, j * N + nb * 512:j * N + (nb + 1) * 512],
                            start=(j == 0), stop=(j == DJ - 1)),
                            "t" if (j == DJ - 1 and nb == 3) else None)
                lmk_hist[sc] = lmk

            def p2_fB(sc):
                buf = sc % 2
                lmk = lmk_hist[sc]
                p = 4 * (sc % 2)
                P.wait("vector", "t", lmk)
                for nb in range(4):
                    P.runf("vector", lambda e, nb=nb, p=p: e.reduce_max(
                        colA[:, 8 + p + nb:9 + p + nb], pL[nb][:, 0:512], AX.X))
                P.run("vector", lambda e, p=p: e.reduce_max(
                    colC[:, 8 + p:9 + p], colA[:, 8 + p:12 + p], AX.X))
                m_bias = P.run("vector", lambda e, sc=sc, p=p: e.tensor_scalar(
                    colC[:, 9 + p:10 + p], colC[:, 8 + p:9 + p],
                    rscale_all[:, sc:sc + 1], -1.0, ALU.mult, ALU.mult), "v")
                P.wait("scalar", "v", m_bias)
                if sc >= 2:
                    for mkk in lsp_l[sc - 2]:
                        P.waitm("scalar", mkk)
                    P.wait("scalar", "t", atr_hist[sc - 2])
                aemk = None
                for nb in range(4):
                    aemk = P.runf("scalar", lambda e, buf=buf, nb=nb, sc=sc, p=p:
                                  e.activation(
                        a_ch[buf][:, nb * 512:(nb + 1) * 512], pL[nb][:, 0:512],
                        AF.Exp, bias=colC[:, 9 + p:10 + p],
                        scale=rscale_all[:, sc:sc + 1],
                        accum_out=colB[:, 8 + p + nb:9 + p + nb]), "s")
                aemk_l[sc] = aemk
                lev = None
                for nb in range(4):
                    lev = P.runf("scalar", lambda e, buf=buf, nb=nb, sc=sc:
                                 e.activation(
                        l_ch[buf][:, nb * 512:(nb + 1) * 512], pL[nb][:, 0:512],
                        AF.Copy, scale=rscale_all[:, sc:sc + 1]),
                        "s" if nb == 3 else None)
                lev_l[sc] = lev
                P.wait("vector", "s", aemk)
                P.runf("vector", lambda e, p=p: e.reduce_sum(
                    colC[:, 10 + p:11 + p], colB[:, 8 + p:12 + p], AX.X))
                rdcol[sc] = P.run("vector", lambda e, p=p: e.reciprocal(
                    colC[:, 10 + p:11 + p], colC[:, 10 + p:11 + p]), "v")
                P.wait("gpsimd", "s", lev)
                lsp_l[sc] = [P.dma("g", dr["L_sp"][nb8, sc * 128:(sc + 1) * 128, :],
                                   l_ch[buf][:, nb8 * 256:(nb8 + 1) * 256])
                             for nb8 in range(NBL)]

            def p2_backT1(sc):
                buf = sc % 2
                P.wait("tensor", "s", aemk_l[sc])
                if sc >= 1:
                    P.wait("tensor", "v", disv[sc - 1])
                atev = None
                for pk in range(2):
                    mk = None
                    if pk == 1:
                        P.wait("tensor", "v", atev)
                    elif sc >= 1:
                        P.wait("tensor", "v", atev_prev[0])
                    for t8 in range(8):
                        nt = pk * 8 + t8
                        mk = P.run("tensor", lambda e, buf=buf, nt=nt, t8=t8:
                                   e.transpose(
                            pAt[:, t8 * 128:(t8 + 1) * 128],
                            a_ch[buf][:, nt * 128:(nt + 1) * 128], ident_bf[:]),
                            "t" if t8 == 7 else None)
                    if pk == 1:
                        atr_hist[sc] = mk
                    P.wait("vector", "t", mk)
                    if sc >= 2:
                        P.wait("vector", "t", dis_m[sc - 2])
                    atev = P.runf("vector", lambda e, buf=buf, pk=pk: e.tensor_copy(
                        a_tm[buf][:, pk * 1024:(pk + 1) * 1024], pAt[:, 0:1024]),
                        "v")
                atev_prev[0] = atev

            def p2_backT2(sc):
                buf = sc % 2
                p = 4 * (sc % 2)
                atev = atev_prev[0]
                P.wait("tensor", "v", atev)
                dmk = None
                for nt in range(NT):
                    for half in range(2):
                        dmk = P.run("tensor", lambda e, buf=buf, nt=nt, half=half:
                                    e.matmul(
                            pDs[half][:, 0:512],
                            a_tm[buf][:, nt * 128:(nt + 1) * 128],
                            xn_tm[:, nt * D + half * 512:nt * D + half * 512 + 512],
                            start=(nt == 0), stop=(nt == NT - 1)),
                            "t" if (nt == NT - 1 and half == 1) else None)
                dis_m[sc] = dmk
                P.wait("vector", "t", dmk)
                if sc >= 2:
                    P.wait("vector", "t", stm_hist[sc - 2])
                P.waitm("vector", ("v", rdcol[sc]))
                for half in range(2):
                    P.runf("vector", lambda e, buf=buf, half=half, p=p:
                           e.tensor_scalar(
                        sl_sm[buf][:, half * 512:(half + 1) * 512],
                        pDs[half][:, 0:512], colC[:, 10 + p:11 + p], None,
                        ALU.mult))
                dis_l = P.runf("vector", lambda e: e.memset(colD[:, 3:4], 0.0), "v")
                disv[sc] = dis_l
                P.wait("tensor", "v", dis_l)
                if sc + 1 < SC:
                    P.wait("tensor", "v", sfold_m[sc + 1])
                elif sc >= 1:
                    P.wait("tensor", "v", sfm_m[sc - 1])
                mk = None
                for j in range(DJ):
                    mk = P.run("tensor", lambda e, buf=buf, j=j: e.transpose(
                        pSe[:, j * 128:(j + 1) * 128],
                        sl_sm[buf][:, j * 128:(j + 1) * 128], ident_bf[:]),
                        "t" if j == DJ - 1 else None)
                P.wait("vector", "t", mk)
                sfm_last = None
                for jj in range(DJ):
                    sfm_last = P.runf("vector", lambda e, jj=jj, sc=sc: e.tensor_copy(
                        slots_fm[:, jj * S + sc * 128:jj * S + sc * 128 + 128],
                        pSe[:, jj * 128:(jj + 1) * 128]),
                        "v" if jj == DJ - 1 else None)
                stm_hist[sc] = mk
                sfm_m[sc] = sfm_last

            atev_prev = [None]
            for sc in range(SC):
                p2_front(sc)
                if sc >= 1:
                    p2_backT1(sc - 1)
                p2_fB(sc)
                if sc >= 1:
                    p2_backT2(sc - 1)
            p2_backT1(SC - 1)
            p2_backT2(SC - 1)

            P.emit("p2")
            if debug:
                P.dma("o", dbg["slots_fm"][:], slots_fm[:])
                P.qdrain("sync", "o")
                P.emit("p2dbg")
        xn_fm_cm.__exit__(None, None, None)
        xn_tm_cm.__exit__(None, None, None)
        if stop_after == "p2":
            es_sl.close()
            return nc, dr

        with ExitStack() as es3:
            P.phase_fence()
            w1_bf = sb("w1_bf", (128, DJ * HID), BF16, es3)
            h_buf = [sb("hbuf%d" % i, (128, N), BF16, es3) for i in range(2)]
            pH = [ps("pH%d" % i, (128, 512), F32, es3) for i in range(8)]

            w1_mk = [P.dma("g", w1_bf[:, j * HID:(j + 1) * HID],
                           dr["w1"][j * 128:(j + 1) * 128, :]) for j in range(DJ)]
            w1f_mk = []
            for j in range(DJ):
                P.waitm("vector", w1_mk[j])
                w1f_mk.append(P.runf("vector", lambda e, j=j: e.tensor_scalar(
                    w1_bf[:, j * HID:(j + 1) * HID], w1_bf[:, j * HID:(j + 1) * HID],
                    gx_col[:, j:j + 1], None, ALU.mult), "v"))

            hsp_m = []
            for hf in range(2):
                for hc in range(HC):
                    pbase = 4 * (hc % 2)
                    if len(hsp_m) >= 2:
                        for mkk in hsp_m[-2]:
                            P.waitm("tensor", mkk)
                    gmk = None
                    for j in range(DJ):
                        P.wait("tensor", "v", w1f_mk[j])
                        for q in range(4):
                            gmk = P.run("tensor",
                                        lambda e, j=j, q=q, hc=hc, hf=hf, pbase=pbase:
                                        e.matmul(
                                pH[pbase + q][:, 0:512],
                                w1_bf[:, j * HID + hc * 128:j * HID + (hc + 1) * 128],
                                slots_fm[:, j * S + hf * 2048 + q * 512:
                                         j * S + hf * 2048 + (q + 1) * 512],
                                start=(j == 0), stop=(j == DJ - 1)),
                                "t" if (j == DJ - 1 and q == 3) else None)
                    P.wait("scalar", "t", gmk)
                    gel = None
                    for q in range(4):
                        gel = P.runf("scalar", lambda e, q=q, hc=hc, pbase=pbase:
                                     e.activation(
                            h_buf[hc % 2][:, q * 512:(q + 1) * 512],
                            pH[pbase + q][:, 0:512], AF.Gelu),
                            "s" if q == 3 else None)
                    P.wait("gpsimd", "s", gel)
                    hsp = []
                    for q8 in range(8):
                        blk8 = hf * 8 + q8
                        hsp.append(P.dma("g",
                                         dr["h_sp"][blk8, hc * 128:(hc + 1) * 128, :],
                                         h_buf[hc % 2][:, q8 * 256:(q8 + 1) * 256]))
                    hsp_m.append(hsp)
            P.emit("p3")
        es_sl.close()
        if stop_after == "p3":
            return nc, dr

        es_pr = ExitStack()
        proc_sm = sb("proc_sm", (128, SC * D), BF16, es_pr)
        with ExitStack() as es4:
            P.phase_fence()
            w2_bf = sb("w2_bf", (128, HC * D), BF16, es4)
            h_blk = [sb("hblk%d" % i, (128, HC * HBW), BF16, es4) for i in range(2)]
            gb_rep = sb("gb_rep", (128, D), F32, es4)
            row_c = sb("row_c", (1, D), F32, es4)
            pP = [ps("pP%d" % i, (128, 512), F32, es4) for i in range(4)]
            pQ = [ps("pQ%d" % i, (128, 512), F32, es4) for i in range(2)]

            mk = None
            for j in range(DJ):
                mk = P.run("tensor", lambda e, j=j: e.matmul(
                    pQ[j // 4][0:1, (j % 4) * 128:(j % 4) * 128 + 128],
                    gb_col[:, j:j + 1],
                    ident_f32[:], start=True, stop=True),
                    "t" if j == DJ - 1 else None)
            P.wait("vector", "t", mk)
            P.run("vector", lambda e: e.tensor_copy(
                row_c[:, 0:512], pQ[0][0:1, 0:512]))
            m_rowc = P.run("vector", lambda e: e.tensor_copy(
                row_c[:, 512:1024], pQ[1][0:1, 0:512]), "v")
            P.wait("tensor", "v", m_rowc)
            for hh in range(2):
                mk = P.run("tensor", lambda e, hh=hh: e.matmul(
                    pQ[1][:, 0:512], ones_f32[0:1, 0:128],
                    row_c[0:1, hh * 512:(hh + 1) * 512], start=True, stop=True), "t")
                P.wait("vector", "t", mk)
                mk2 = P.run("vector", lambda e, hh=hh: e.tensor_copy(
                    gb_rep[:, hh * 512:(hh + 1) * 512], pQ[1][:, 0:512]), "v")
                P.wait("tensor", "v", mk2)
            w2_mk = [P.dma("g", w2_bf[:, hc * D:(hc + 1) * D],
                           dr["w2"][hc * 128:(hc + 1) * 128, :]) for hc in range(HC)]
            w2f_mk = []
            for hc in range(HC):
                P.waitm("vector", w2_mk[hc])
                w2f_mk.append(P.run("vector", lambda e, hc=hc: e.tensor_tensor(
                    w2_bf[:, hc * D:(hc + 1) * D], w2_bf[:, hc * D:(hc + 1) * D],
                    gb_rep[:], ALU.mult), "v", chain=(hc == 0)))

            pev = []
            pmk_hist = []
            for blk in range(HBL):
                if blk >= 2:
                    P.wait("sync", "t", pmk_hist[blk - 2])
                hb_mk = P.dma("y", h_blk[blk % 2][:].rearrange(
                    "p (c w) -> p c w", w=HBW),
                    dr["h_sp"][blk].rearrange("(c p) w -> p c w", p=128))
                for st in range(2):
                    P.waitm("tensor", hb_mk)
                    if len(pev) >= 4:
                        P.wait("tensor", "v", pev[-3])
                    pmk = None
                    for hc in range(HC):
                        P.wait("tensor", "v", w2f_mk[hc])
                        for half in range(2):
                            pmk = P.run("tensor",
                                        lambda e, hc=hc, st=st, half=half, blk=blk:
                                        e.matmul(
                                pP[2 * (st % 2) + half][:, 0:512],
                                h_blk[blk % 2][:, hc * HBW + st * 128:
                                               hc * HBW + (st + 1) * 128],
                                w2_bf[:, hc * D + half * 512:hc * D + half * 512 + 512],
                                start=(hc == 0), stop=(hc == HC - 1)),
                                "t" if (hc == HC - 1 and half == 1) else None)
                    if st == 1:
                        pmk_hist.append(pmk)
                    P.wait("vector", "t", pmk)
                    stt = blk * 2 + st
                    for half in range(2):
                        pev.append(P.runf("vector",
                                          lambda e, stt=stt, st=st, half=half:
                                          e.tensor_copy(
                            proc_sm[:, stt * D + half * 512:stt * D + half * 512 + 512],
                            pP[2 * (st % 2) + half][:, 0:512]), "v"))
            P.emit("p4")
            if debug:
                P.dma("o", dbg["proc_sm"][:], proc_sm[:])
                P.qdrain("sync", "o")
                P.emit("p4dbg")
        if stop_after == "p4":
            es_pr.close()
            return nc, dr

        with ExitStack() as es5:
            P.phase_fence()
            l_blk = [sb("lblk%d" % i, (128, SC * NBW), F32, es5) for i in range(2)]
            l_tmb = sb("l_tmb", (128, S), F32, es5)
            w_tmb = sb("w_tmb", (128, S), BF16, es5)
            w_smb = sb("w_smb", (128, S), BF16, es5)
            x2_rs = [sb("x2rs%d" % i, (128, D), F32, es5) for i in range(2)]
            o_buf = [sb("obuf%d" % i, (128, D), F32, es5) for i in range(2)]
            pT = [ps("pT%d" % i, (128, 512), F32, es5) for i in range(4)]
            pW = [ps("pW%d" % i, (128, 1024), BF16, es5) for i in range(2)]
            pO = [ps("pO%d" % i, (128, 512), F32, es5) for i in range(2)]

            lb_mk = [None] * NBL
            x2r_mk = [None] * NT
            oev_tile = [None] * NT
            ost_mk = [None] * NT
            ltile_done = [None] * NT
            st5 = {"lt_ev": None, "mneg": [None] * NT, "mw": [None] * NT,
                   "wtrl": [None] * NT, "wtrv": [None] * NT,
                   "lt_bank": [None] * 4, "lt_tmk": [None] * 4}

            def p5_lstage(nt):
                nb, ti = nt // 2, nt % 2
                p = 4 * (nt % 2)
                def lb_load(nbx):
                    if nbx >= NBL or lb_mk[nbx] is not None:
                        return
                    if nbx >= 2:
                        P.wait("sync", "t", ltile_done[(nbx - 2) * 2 + 1])
                    lb_mk[nbx] = P.dma("y", l_blk[nbx % 2][:].rearrange(
                        "p (c w) -> p c w", w=NBW),
                        dr["L_sp"][nbx].rearrange("(c p) w -> p c w", p=128))
                if ti == 0:
                    lb_load(nb)
                    lb_load(nb + 1)
                if nt >= 2:
                    P.wait("sync", "v", oev_tile[nt - 2])
                x2r_mk[nt] = P.dma("y", x2_rs[nt % 2][:],
                                   dr["x2_sp"][nt * 128:(nt + 1) * 128, :])
                P.waitm("tensor", lb_mk[nb])
                for stg in range(2):
                    mk = None
                    for t16 in range(16):
                        sc = stg * 16 + t16
                        if t16 % 4 == 0:
                            P.wait("tensor", "s", st5["lt_bank"][t16 // 4])
                        mk = P.run("tensor",
                                   lambda e, nb=nb, ti=ti, sc=sc, t16=t16:
                                   e.transpose(
                            pT[t16 // 4][:, (t16 % 4) * 128:(t16 % 4) * 128 + 128],
                            l_blk[nb % 2][:, sc * NBW + ti * 128:
                                          sc * NBW + (ti + 1) * 128],
                            ident_f32[:]), "t" if (t16 % 4) == 3 else None)
                        if t16 % 4 == 3:
                            st5["lt_tmk"][t16 // 4] = mk
                    if stg == 1:
                        ltile_done[nt] = mk
                    if stg == 0 and nt >= 1:
                        P.wait("scalar", "v", st5["mneg"][nt - 1])
                    for q in range(4):
                        P.wait("scalar", "t", st5["lt_tmk"][q])
                        st5["lt_bank"][q] = P.runf("scalar",
                                                   lambda e, stg=stg, q=q:
                                                   e.activation(
                            l_tmb[:, stg * 2048 + q * 512:
                                  stg * 2048 + (q + 1) * 512],
                            pT[q][:, 0:512], AF.Copy), "s")
                st5["lt_ev"] = st5["lt_bank"][3]
                P.wait("vector", "s", st5["lt_ev"])
                P.runf("vector", lambda e, p=p: e.reduce_max(
                    colA[:, p:p + 1], l_tmb[:], AX.X))
                st5["mneg"][nt] = P.run("vector", lambda e, p=p: e.tensor_scalar(
                    colA[:, p + 1:p + 2], colA[:, p:p + 1], -1.0, None,
                    ALU.mult), "v")
                P.wait("scalar", "v", st5["mneg"][nt])
                if nt >= 1:
                    P.wait("scalar", "t", st5["wtrl"][nt - 1])
                st5["mw"][nt] = P.run("scalar", lambda e, p=p: e.activation(
                    w_tmb[:], l_tmb[:], AF.Exp, bias=colA[:, p + 1:p + 2],
                    accum_out=colA[:, p + 2:p + 3]), "s")
                P.wait("vector", "s", st5["mw"][nt])
                P.runf("vector", lambda e, p=p: e.reciprocal(
                    colA[:, p + 3:p + 4], colA[:, p + 2:p + 3]), "v")

            def p5_wt(nt):
                P.wait("tensor", "s", st5["mw"][nt])
                wtr_ev = st5["wtrv"][nt - 1] if nt >= 1 else None
                for pk in range(4):
                    mk = None
                    if pk >= 2 or nt >= 1:
                        P.wait("tensor", "v", wtr_ev)
                    for t8 in range(8):
                        sc = pk * 8 + t8
                        mk = P.run("tensor", lambda e, sc=sc, t8=t8, pk=pk:
                                   e.transpose(
                            pW[pk % 2][:, t8 * 128:(t8 + 1) * 128],
                            w_tmb[:, sc * 128:(sc + 1) * 128], ident_bf[:]),
                            "t" if t8 == 7 else None)
                    P.wait("vector", "t", mk)
                    wtr_ev = P.runf("vector", lambda e, pk=pk: e.tensor_copy(
                        w_smb[:, pk * 1024:(pk + 1) * 1024],
                        pW[pk % 2][:, 0:1024]), "v")
                st5["wtrl"][nt] = mk
                st5["wtrv"][nt] = wtr_ev

            def p5_combine(nt):
                p = 4 * (nt % 2)
                P.wait("tensor", "v", st5["wtrv"][nt])
                if nt >= 1:
                    P.wait("tensor", "v", oev_tile[nt - 1])
                cmk = None
                for sc in range(SC):
                    for half in range(2):
                        cmk = P.run("tensor", lambda e, sc=sc, half=half:
                                    e.matmul(
                            pO[half][:, 0:512],
                            w_smb[:, sc * 128:(sc + 1) * 128],
                            proc_sm[:, sc * D + half * 512:
                                    sc * D + half * 512 + 512],
                            start=(sc == 0), stop=(sc == SC - 1)),
                            "t" if (sc == SC - 1 and half == 1) else None)
                P.wait("vector", "t", cmk)
                P.waitm("vector", x2r_mk[nt])
                if nt >= 2:
                    P.waitm("vector", ost_mk[nt - 2])
                ol = None
                for half in range(2):
                    ol = P.runf("vector",
                                lambda e, nt=nt, half=half, p=p:
                                e.scalar_tensor_tensor(
                        o_buf[nt % 2][:, half * 512:(half + 1) * 512],
                        pO[half][:, 0:512], colA[:, p + 3:p + 4],
                        x2_rs[nt % 2][:, half * 512:(half + 1) * 512],
                        ALU.mult, ALU.add), "v")
                oev_tile[nt] = ol
                P.wait("sync", "v", ol)
                ost_mk[nt] = P.dma("o", dr["out"][nt * 128:(nt + 1) * 128, :],
                                   o_buf[nt % 2][:])

            for nt in range(NT):
                p5_lstage(nt)
                if nt >= 1:
                    p5_combine(nt - 1)
                p5_wt(nt)
            p5_combine(NT - 1)
            P.qdrain("sync", "o")
            P.emit("p5")
        es_pr.close()

    return nc, dr


_CACHE = {}


def _consts():
    ident = np.eye(128)
    ones = np.ones((128, 128))
    jl = np.arange(128)[:, None]
    nl = np.arange(128)[None, :]
    ma = ((jl - nl >= 0) & (jl - nl <= 6)).astype(np.float32)
    jb = np.arange(6)[:, None]
    mb = ((125 + jb - nl >= -3) & (125 + jb - nl <= 3)).astype(np.float32)
    return {
        "ident_bf": ident.astype(BF), "ident_f32": ident.astype(np.float32),
        "ones_bf": ones.astype(BF), "ones_f32": ones.astype(np.float32),
        "mask_a": np.tile(ma, (1, 4)).astype(np.float32),
        "mask_b": np.tile(mb, (1, 4)).astype(np.float32),
    }


def make_in_maps(inputs):
    cst = _consts()
    x = np.ascontiguousarray(np.asarray(inputs["L0_attn_x"], np.float32))
    c = np.asarray(inputs["L0_c"], np.float32)
    base = {
        "w_qkv": np.ascontiguousarray(inputs["L0_attn_w_qkv"], dtype=np.float32),
        "proj": np.ascontiguousarray(inputs["L0_attn_proj"], dtype=np.float32),
        "w_out": np.ascontiguousarray(inputs["L0_attn_w_out"], dtype=np.float32),
        "b_out": np.ascontiguousarray(
            np.asarray(inputs["L0_attn_b_out"], np.float32).reshape(1, D)),
        "gx_col": np.ascontiguousarray(
            np.asarray(inputs["L0_ffn_gamma_x"], np.float32).reshape(DJ, 128).T),
        "gs_col": np.ascontiguousarray(
            np.asarray(inputs["L0_ffn_gamma_slot"], np.float32).reshape(DJ, 128).T),
        "slot_embeds": np.ascontiguousarray(
            np.asarray(inputs["L0_ffn_slot_embeds"], np.float32).reshape(S, D)),
        "w1": np.ascontiguousarray(inputs["L0_ffn_w1"], dtype=np.float32),
        "w2": np.ascontiguousarray(inputs["L0_ffn_w2"], dtype=np.float32),
        "wmod": np.ascontiguousarray(inputs["L0_wmod"], dtype=np.float32),
    }
    base.update(cst)
    maps = []
    for b in range(8):
        m = dict(base)
        m["x"] = np.ascontiguousarray(x[b])
        m["c_fm"] = np.ascontiguousarray(c[b].reshape(DJ, 128).T)
        maps.append(m)
    return maps


def kernel(**inputs):
    if "nc" not in _CACHE:
        nc, dr = build()
        _CACHE["nc"] = nc
    nc = _CACHE["nc"]
    maps = make_in_maps(inputs)
    res = run_bass_kernel_spmd(nc, maps, core_ids=list(range(8)))
    out = np.stack([np.asarray(r["out"], dtype=np.float32) for r in res.results])
    return out
